# version 11
# baseline (speedup 1.0000x reference)
import contextlib
import os
import numpy as np
import ml_dtypes
import concourse.bass as bass
import concourse.mybir as mybir
from concourse.bass_utils import run_bass_kernel_spmd

F32 = mybir.dt.float32
BF16 = mybir.dt.bfloat16
FP16 = mybir.dt.float16
AF = mybir.ActivationFunctionType
ALU = mybir.AluOpType
AX = mybir.AxisListType

B = 8
T = 2048
D = 1024
NH = 16
HD = 64
DH = 4096
NT = T // 128
NDC = D // 128
SCALE = HD ** -0.5
MAGIC = float(np.float32(1.5 * 2.0 ** 23))
EPS = 1e-5
QG = 256
NQG = T // QG
NHIDC = DH // 128
_NLN = int(os.environ.get("KLN_NT", str(NT)))
_NQG = int(os.environ.get("KLN_NQG", str(NQG)))
_NPAIR = int(os.environ.get("KLN_NPAIR", "4"))


def _consts():
    c = {}
    kk = np.arange(128)[:, None]
    qq = np.arange(128)[None, :]
    tri = (kk <= qq).astype(np.float32)
    ev = np.concatenate([tri, np.ones((128, 128), np.float32)], axis=1)
    od = np.concatenate([np.zeros((128, 128), np.float32), tri], axis=1)
    mask = np.concatenate([np.tile(ev, (1, 4)), np.tile(od, (1, 4))], axis=1)
    c["maskEO"] = mask.astype(ml_dtypes.bfloat16)
    o3 = np.zeros((128, 16, 16), dtype=np.float32)
    for p in range(16):
        o3[:, p, p] = 1
    c["onehot3"] = o3.reshape(128, 256).astype(ml_dtypes.bfloat16)
    c["ones_c"] = np.ones((128, 128), dtype=ml_dtypes.bfloat16)
    c["onesf"] = np.ones((128, 256), dtype=np.float32)
    s2 = np.zeros((2, 128), dtype=np.float32)
    s2[0, 0:64] = 1
    s2[1, 64:128] = 1
    c["sel2"] = s2.astype(ml_dtypes.bfloat16)
    return c


class Sem:
    def __init__(self, nc, st, name):
        self.h = st.enter_context(nc.semaphore(name))
        self.n = 0

    def bump(self, k=1):
        self.n += k
        return self.n


class Prog:
    ENGS = ("T", "V", "S", "G", "Y")

    def __init__(self, nc, st):
        self.nc = nc
        self.st = st
        self.steps = []
        self.sem = {e: Sem(nc, st, f"s{e}") for e in self.ENGS}
        self.dsems = []
        self.bar = Sem(nc, st, "bar")
        self.nbar = 0

    def dsem(self, name):
        s = Sem(self.nc, self.st, name)
        self.dsems.append(s)
        return s

    def on(self, eng):
        def reg(fn):
            self.steps.append((eng, fn))
            return fn
        return reg

    def wait(self, eng, sem, val):
        if val <= 0:
            return
        @self.on(eng)
        def _(e, sem=sem, val=val):
            e.wait_ge(sem.h, val)

    def barrier(self):
        self.nbar += 1
        n = self.nbar
        tgt = [(s.h, 16 * s.n) for s in self.dsems if s.n]
        for eng in self.ENGS:
            @self.on(eng)
            def _(e, tgt=tuple(tgt), n=n):
                for h, v in tgt:
                    e.wait_ge(h, v)
                e.drain()
                e.nop().then_inc(self.bar.h, 1)
                e.wait_ge(self.bar.h, 5 * n)

    def finalize(self, block):
        for eng, attr in (("T", "tensor"), ("V", "vector"), ("S", "scalar"),
                          ("G", "gpsimd"), ("Y", "sync")):
            def mk(eng):
                def body(e):
                    for name, fn in self.steps:
                        if name == eng:
                            fn(e)
                return body
            getattr(block, attr)(mk(eng))


def build(stage=99):
    nc = bass.Bass("TRN2", target_bir_lowering=False)
    x_hbm = nc.dram_tensor("x", [T, D], F32, kind="ExternalInput").ap()
    W_hbm = {w: nc.dram_tensor(w + "T", [D, D], F32, kind="ExternalInput").ap()
             for w in ("Wq", "Wk1", "Wk2", "Wv", "Wo")}
    lam_hbm = nc.dram_tensor("lam", [NH, 1], F32, kind="ExternalInput").ap()
    w1_hbm = nc.dram_tensor("w1T", [D, DH], F32, kind="ExternalInput").ap()
    convw_hbm = nc.dram_tensor("convw", [128, 3, 32], F32,
                               kind="ExternalInput").ap()
    convb_hbm = nc.dram_tensor("convb", [128, 32], F32,
                               kind="ExternalInput").ap()
    w2_hbm = nc.dram_tensor("w2T", [DH, D], F32, kind="ExternalInput").ap()
    cvals = _consts()
    C_hbm = {k: nc.dram_tensor("h_" + k, list(v.shape),
                               BF16 if v.dtype == ml_dtypes.bfloat16 else F32,
                               kind="ExternalInput").ap()
             for k, v in cvals.items()}
    out_hbm = nc.dram_tensor("out", [T, D], F32, kind="ExternalOutput").ap()
    ik = "Internal" if stage >= 99 else "ExternalOutput"
    xnew_hbm = nc.dram_tensor("s_xnew", [T, D], F32, kind=ik).ap()
    lamrt_hbm = nc.dram_tensor("s_lamrt", [16, 2], F32, kind="Internal").ap()
    dqrt_hbm = nc.dram_tensor("s_dqrt", [NT, 128], F32, kind="Internal").ap()
    s2rt_hbm = nc.dram_tensor("s_s2rt", [NT, 128], F32, kind="Internal").ap()
    w1q_hbm = nc.dram_tensor("s_w1q", [D, DH], BF16, kind=ik).ap()
    w2q_hbm = nc.dram_tensor("s_w2q", [DH, D], BF16, kind=ik).ap()
    y_hbm = nc.dram_tensor("s_y", [DH, T], BF16, kind=ik).ap()
    dbg_hbm = {}
    if stage < 99:
        for k, shape, dt in (
            ("lnxT", [128, NDC * T], BF16), ("k1f", [128, 4 * T], BF16),
            ("k2f", [128, 4 * T], BF16), ("vtk", [128, NT * 512], BF16),
            ("qf", [128, 4 * QG], BF16), ("ao", [128, 8 * T], BF16),
            ("e", [128, 16 * 1024], BF16), ("row", [128, 2048], F32),
            ("xq", [128, NDC * T], BF16), ("h3", [128, NHIDC * T], FP16),
        ):
            dbg_hbm[k] = nc.dram_tensor("d_" + k, shape, dt,
                                        kind="ExternalOutput").ap()
    st = contextlib.ExitStack()
    with st:
        P = Prog(nc, st)
        _body(nc, st, P, stage, locals())
        with nc.Block() as block:
            P.finalize(block)
    return nc


def _body(nc, st, P, stage, tn):
    x_hbm = tn["x_hbm"]; W_hbm = tn["W_hbm"]
    lam_hbm = tn["lam_hbm"]; C_hbm = tn["C_hbm"]; out_hbm = tn["out_hbm"]
    w1_hbm = tn["w1_hbm"]; w2_hbm = tn["w2_hbm"]
    convw_hbm = tn["convw_hbm"]; convb_hbm = tn["convb_hbm"]
    xnew_hbm = tn["xnew_hbm"]; lamrt_hbm = tn["lamrt_hbm"]
    w1q_hbm = tn["w1q_hbm"]; w2q_hbm = tn["w2q_hbm"]; y_hbm = tn["y_hbm"]
    dqrt_hbm = tn["dqrt_hbm"]; s2rt_hbm = tn["s2rt_hbm"]
    dbg = tn["dbg_hbm"]

    sb = lambda name, shape, dt, **kw: st.enter_context(
        nc.sbuf_tensor(name, shape, dt, **kw))
    T_, V_, S_, G_, Y_ = "T", "V", "S", "G", "Y"
    sT, sV, sS, sG, sY = (P.sem[k] for k in ("T", "V", "S", "G", "Y"))

    lnxT = sb("c_lnxT", [128, NDC, T], BF16, is_transpose_dest=True)
    dq_all = sb("c_dq", [128, NT], F32)
    dq2_all = sb("c_dq2", [128, NT], F32)
    ones_c = sb("c_ones", [128, 128], BF16)
    lamcol = sb("c_lamcol", [128, 1], F32)
    lamsc = sb("c_lamsc", [1, 32], F32)
    convw_sb = sb("c_convw", [128, 3, NHIDC], F32)
    convb_sb = sb("c_convb", [128, NHIDC], F32)
    colscr = sb("c_colscr", [128, 16], F32)
    wcols = sb("c_wcols", [128, 8], F32)
    colhist = None
    phBC = contextlib.ExitStack()
    sbc = lambda name, shape, dt, **kw: phBC.enter_context(
        nc.sbuf_tensor(name, shape, dt, **kw))
    maskEO = sbc("c_maskEO", [128, 2048], BF16)
    onehot3 = sbc("c_onehot3", [128, 256], BF16)
    onesf = sbc("c_onesf", [128, 256], F32)
    sel2 = sbc("c_sel2", [2, 128], BF16)
    aoT = sbc("c_aoT", [128, 2 * 4, T], BF16)

    phA = contextlib.ExitStack()
    xfull = phA.enter_context(nc.sbuf_tensor("a_xfull", [128, NT, D], F32))
    sq_scr = phA.enter_context(nc.sbuf_tensor("a_sqscr", [128, D], F32))
    ltile = [phA.enter_context(nc.sbuf_tensor(f"a_ltile{i}", [128, D], BF16))
             for i in range(2)]
    dCon = P.dsem("dCon")
    dXf = P.dsem("dXf")
    dLamW = P.dsem("dLamW")
    dLamR = P.dsem("dLamR")
    dTr = P.dsem("dTr")
    dDbg = P.dsem("dDbg")

    @P.on(V_)
    def _(e):
        e.memset(colscr[:], 0.0)
        e.memset(lamcol[:], 1.0)
        e.memset(wcols[:], 0.0)
        e.drain()
        e.nop().then_inc(sV.h, 1)
    ms_v = sV.bump()
    P.wait(Y_, sV, ms_v)

    @P.on(Y_)
    def _(e):
        e.dma_start(out=maskEO[:], in_=C_hbm["maskEO"]).then_inc(dCon.h, 16)
        e.dma_start(out=onehot3[:], in_=C_hbm["onehot3"]).then_inc(dCon.h, 16)
        e.dma_start(out=ones_c[:], in_=C_hbm["ones_c"]).then_inc(dCon.h, 16)
        e.dma_start(out=onesf[:], in_=C_hbm["onesf"]).then_inc(dCon.h, 16)
        e.dma_start(out=sel2[:], in_=C_hbm["sel2"]).then_inc(dCon.h, 16)
        e.dma_start(out=convw_sb[:], in_=convw_hbm).then_inc(dCon.h, 16)
        e.dma_start(out=convb_sb[:], in_=convb_hbm).then_inc(dCon.h, 16)
        e.dma_start(out=colscr[0:16, 0:1], in_=lam_hbm).then_inc(dCon.h, 16)
        e.dma_start(out=xfull[:],
                    in_=x_hbm.rearrange("(n p) m -> p n m", p=128)
                    ).then_inc(dXf.h, 16)
    consts_done = 16 * dCon.bump(8)
    xfull_done = 16 * dXf.bump()

    P.wait(S_, dCon, consts_done)
    @P.on(S_)
    def _(e):
        e.activation(out=colscr[:, 1:2], in_=colscr[:, 0:1], func=AF.Exp,
                     scale=-1.0).then_inc(sS.h, 1)
    v = sS.bump()
    P.wait(V_, sS, v)
    @P.on(V_)
    def _(e):
        e.tensor_scalar(out=colscr[:, 2:3], in0=colscr[:, 1:2],
                        scalar1=1.0, scalar2=1e-30, op0=ALU.add, op1=ALU.max)
        e.drain()
        e.reciprocal(out=lamcol[:], in_=colscr[:, 2:3]).then_inc(sV.h, 1)
    v = sV.bump()
    P.wait(S_, sV, v)
    @P.on(S_)
    def _(e):
        e.drain()
        e.activation(out=colscr[:, 3:4], in_=lamcol[:],
                     func=AF.Ln).then_inc(sS.h, 1)
    v = sS.bump()
    P.wait(V_, sS, v)
    @P.on(V_)
    def _(e):
        e.drain()
        e.tensor_scalar(out=colscr[:, 4:5], in0=lamcol[:], scalar1=-1.0,
                        scalar2=1.0, op0=ALU.mult, op1=ALU.add
                        ).then_inc(sV.h, 1)
    v = sV.bump()
    P.wait(Y_, sV, v)
    @P.on(Y_)
    def _(e):
        e.dma_start(out=lamrt_hbm, in_=colscr[0:16, 3:5]).then_inc(dLamW.h, 16)
    P.wait(Y_, dLamW, 16 * dLamW.bump())
    @P.on(Y_)
    def _(e):
        e.dma_start(out=lamsc[:], in_=lamrt_hbm.rearrange(
            "a b -> (a b)")[None, :]).then_inc(dLamR.h, 16)
    lam_rd2 = 16 * dLamR.bump()
    lam_done_v = sV.n

    ln_s = []
    ln_v = []
    P.wait(V_, dXf, xfull_done)
    P.wait(V_, dLamW, 16)
    P.wait(S_, dXf, xfull_done)
    for t in range(_NLN):
        xt = xfull[:, t, :]
        @P.on(V_)
        def _(e, xt=xt):
            e.drain()
            e.tensor_reduce(out=colscr[:, 0:1], in_=xt, axis=AX.X,
                            op=ALU.add).then_inc(sV.h, 1)
        v_red = sV.bump()
        if t > 0:
            P.wait(S_, sV, ln_v[t - 1])
        @P.on(S_)
        def _(e, xt=xt):
            e.drain()
            e.activation(out=sq_scr[:], in_=xt, func=AF.Square,
                         accum_out=colscr[:, 1:2]).then_inc(sS.h, 1)
        s_sq = sS.bump()
        P.wait(V_, sS, s_sq)
        @P.on(V_)
        def _(e):
            e.drain()
            e.tensor_scalar(out=colscr[:, 2:3], in0=colscr[:, 0:1],
                            scalar1=1.0 / D, scalar2=None, op0=ALU.mult)
            e.tensor_scalar(out=colscr[:, 3:4], in0=colscr[:, 1:2],
                            scalar1=1.0 / D, scalar2=EPS, op0=ALU.mult,
                            op1=ALU.add)
            e.drain()
            e.tensor_tensor(out=colscr[:, 4:5], in0=colscr[:, 2:3],
                            in1=colscr[:, 2:3], op=ALU.mult)
            e.drain()
            e.tensor_tensor(out=colscr[:, 5:6], in0=colscr[:, 3:4],
                            in1=colscr[:, 4:5], op=ALU.subtract
                            ).then_inc(sV.h, 1)
        v_var = sV.bump()
        P.wait(S_, sV, v_var)
        @P.on(S_)
        def _(e):
            e.drain()
            e.activation(out=colscr[:, 8:9], in_=colscr[:, 5:6], func=AF.Ln)
            e.drain()
            e.activation(out=colscr[:, 6:7], in_=colscr[:, 8:9], func=AF.Exp,
                         scale=-0.5).then_inc(sS.h, 1)
        s_rstd = sS.bump()
        P.wait(V_, sS, s_rstd)
        @P.on(V_)
        def _(e):
            e.drain()
            e.scalar_tensor_tensor(out=colscr[:, 7:8], in0=colscr[:, 2:3],
                                   scalar=-1.0, in1=colscr[:, 6:7],
                                   op0=ALU.mult, op1=ALU.mult).then_inc(sV.h, 1)
        v_nmr = sV.bump()
        ln_v.append(sV.n)
        P.wait(S_, sV, v_nmr)
        if t >= 2:
            P.wait(S_, dTr, 16 * (t - 1))
        @P.on(S_)
        def _(e, t=t, xt=xt):
            e.drain()
            e.activation(out=ltile[t % 2][:], in_=xt,
                         func=AF.Identity, scale=colscr[:, 6:7],
                         bias=colscr[:, 7:8]).then_inc(sS.h, 1)
        s_lnx = sS.bump()
        ln_s.append(s_lnx)
        if t == 0:
            P.wait(S_, dCon, consts_done)
            P.wait(S_, dLamR, lam_rd2)
        @P.on(S_)
        def _(e, t=t, w=s_lnx):
            e.wait_ge(sS.h, w)
            e.wait_ge(dTr.h, 16 * t)
            e.dma_start(out=lnxT[:, :, 128 * t:128 * (t + 1)],
                        in_=ltile[t % 2][:], transpose=True).then_inc(dTr.h, 16)
        dTr.bump()
    lnxT_done = 16 * dTr.n

    if "lnxT" in dbg:
        P.wait(Y_, dTr, lnxT_done)
        @P.on(Y_)
        def _(e, w=sV.n):
            e.wait_ge(sV.h, w)
            e.dma_start(out=dbg["lnxT"].rearrange("p (c m) -> p c m", c=NDC
                                                  )[:, :, 0:128 * _NLN],
                        in_=lnxT[:, :, 0:128 * _NLN]).then_inc(dDbg.h, 16)
        dDbg.bump(1)

    P.barrier()
    phA.close()
    if stage <= 1:
        _dev_out(nc, P, out_hbm)
        return

    phB = contextlib.ExitStack()
    ps_sc = phB.enter_context(nc.psum_tensor("b_sc", [128, 4, 512], F32))
    ps_ev = phB.enter_context(nc.psum_tensor("b_ev", [128, 2, 512], F32))
    ps_r = phB.enter_context(nc.psum_tensor("b_r", [128, 2, 512], F32))
    k1f = phB.enter_context(nc.sbuf_tensor("b_k1f", [128, 4, T], BF16))
    k2f = phB.enter_context(nc.sbuf_tensor("b_k2f", [128, 4, T], BF16))
    vtk = phB.enter_context(nc.sbuf_tensor("b_vtk", [128, NT, 512], BF16))
    Wk_sb = phB.enter_context(nc.sbuf_tensor("b_wk", [128, NDC, 512], BF16))
    Wq_sb = phB.enter_context(nc.sbuf_tensor("b_wq", [128, NDC, 512], BF16))
    qf = [phB.enter_context(nc.sbuf_tensor("b_qf0", [128, 4, QG], BF16))] * 2
    e_sb = [phB.enter_context(nc.sbuf_tensor("b_e0", [128, 16, 4, 256],
                                             BF16))] * 2
    tbuf = phB.enter_context(nc.sbuf_tensor("b_tbuf", [128, 1024], BF16))
    cbuf = phB.enter_context(nc.sbuf_tensor("b_cbuf", [128, 2, 1024], BF16))
    rcbuf = phB.enter_context(nc.sbuf_tensor("b_rcbuf", [128, 2, 1024], BF16))
    Gb = [phB.enter_context(nc.sbuf_tensor("b_Gb0", [128, 1024], BF16))] * 2
    Db = [phB.enter_context(nc.sbuf_tensor("b_Db0", [128, 512], BF16))] * 2
    srow = phB.enter_context(nc.sbuf_tensor("b_srow", [1, 4 * 512], F32))
    grow = phB.enter_context(nc.sbuf_tensor("b_grow", [1, 512], BF16))
    dinvrow = phB.enter_context(nc.sbuf_tensor("b_dinvr", [1, 512], BF16))
    dWh = P.dsem("dWh")

    if stage == 2 or _NQG < NQG or _NPAIR < 4:
        @P.on("V")
        def _(e):
            e.memset(aoT[:], 0.0)
            e.memset(e_sb[0][:], 0.0)
            e.drain()
            e.nop().then_inc(P.sem["V"].h, 1)
        P.sem["V"].bump()

    ctx = dict(
        P=P, nc=nc, lnxT=lnxT, k1f=k1f, k2f=k2f, vtk=vtk, Wk_sb=Wk_sb,
        Wq_sb=Wq_sb, qf=qf, e_sb=e_sb, aoT=aoT, tbuf=tbuf, cbuf=cbuf,
        rcbuf=rcbuf, Gb=Gb, Db=Db, srow=srow, grow=grow, dinvrow=dinvrow, lamsc=lamsc,
        ps_sc=ps_sc, ps_ev=ps_ev, ps_r=ps_r, W_hbm=W_hbm, dWh=dWh,
        maskEO=maskEO, onehot3=onehot3, ones_c=ones_c, dbg=dbg, stage=stage,
    )
    for H in range(2 if stage >= 3 else 1):
        _attn_half(ctx, H)
        if stage == 2:
            break

    if stage == 3 and "ao" in dbg:
        dD3 = P.dsem("dDbg3")
        @P.on("Y")
        def _(e, w=P.sem["V"].n):
            e.wait_ge(P.sem["V"].h, w)
            e.dma_start(out=dbg["ao"].rearrange("p (c m) -> p c m", c=8),
                        in_=aoT[:]).then_inc(dD3.h, 16)
        dD3.bump(1)

    P.barrier()
    phB.close()
    if stage <= 3:
        _dev_out(nc, P, out_hbm)
        return

    phC = contextlib.ExitStack()
    psC = phC.enter_context(nc.psum_tensor("c3_ps", [128, 4, 512], F32))
    WoT_sb = phC.enter_context(nc.sbuf_tensor("c3_wo", [128, NDC, D], BF16))
    xtl = [phC.enter_context(nc.sbuf_tensor(f"c3_xt{i}", [128, D], F32))
           for i in range(2)]
    xmidb = [phC.enter_context(nc.sbuf_tensor(f"c3_xm{i}", [128, D], F32))
             for i in range(2)]
    lx2b = [phC.enter_context(nc.sbuf_tensor(f"c3_lx{i}", [128, D], F32))
            for i in range(2)]
    sqscrC = phC.enter_context(nc.sbuf_tensor("c3_sq", [128, D], F32))
    qscrC = phC.enter_context(nc.sbuf_tensor("c3_qs", [128, D], F32))
    xqfull = phC.enter_context(nc.sbuf_tensor("c3_xqf", [128, NT, D], BF16))
    dWo = P.dsem("dWo")
    dXc = P.dsem("dXc")
    dXw = P.dsem("dXw")
    dTr2 = P.dsem("dTr2")

    @P.on(G_)
    def _(e):
        e.dma_start(out=WoT_sb[:], in_=W_hbm["Wo"].rearrange(
            "(c p) m -> p c m", p=128)).then_inc(dWo.h, 16)
    wo_ld = 16 * dWo.bump()
    P.wait(T_, dWo, wo_ld)

    v_xmid = []
    v_q = []
    s_lx = []
    for t in range(NT):
        if t >= 2:
            P.wait(Y_, sV, v_xmid[t - 2])
        P.wait(Y_, dXc, 16 * dXc.n)
        @P.on(Y_)
        def _(e, t=t):
            e.dma_start(out=xtl[t % 2][:],
                        in_=x_hbm[128 * t:128 * (t + 1), :]).then_inc(dXc.h, 16)
        xld = 16 * dXc.bump()
        if t >= 2:
            P.wait(T_, sV, v_xmid[t - 2])
        @P.on(T_)
        def _(e, t=t):
            for half in range(2):
                for c2 in range(NDC):
                    mm = e.matmul(psC[:, 2 * (t % 2) + half, :],
                                  aoT[:, c2, 128 * t:128 * (t + 1)],
                                  WoT_sb[:, c2, 512 * half:512 * (half + 1)],
                                  start=(c2 == 0), stop=(c2 == NDC - 1))
            mm.then_inc(sT.h, 1)
        t_mm = sT.bump()
        P.wait(V_, sT, t_mm)
        P.wait(V_, dXc, xld)
        if t >= 2:
            P.wait(V_, sS, s_lx[t - 2])
            P.wait(V_, dXw, 16 * (t - 1))
        @P.on(V_)
        def _(e, t=t):
            e.drain()
            e.tensor_tensor(
                out=xmidb[t % 2][:].rearrange("p (a m) -> p a m", a=2),
                in0=psC[:, 2 * (t % 2):2 * (t % 2) + 2, :],
                in1=xtl[t % 2][:].rearrange("p (a m) -> p a m", a=2),
                op=ALU.add).then_inc(sV.h, 1)
        v_xmid.append(sV.bump())
        P.wait(G_, sV, v_xmid[t])
        P.wait(G_, dXw, 16 * dXw.n)
        @P.on(G_)
        def _(e, t=t):
            e.dma_start(out=xnew_hbm[128 * t:128 * (t + 1), :],
                        in_=xmidb[t % 2][:]).then_inc(dXw.h, 16)
        dXw.bump()
        @P.on(V_)
        def _(e, t=t):
            e.drain()
            e.tensor_reduce(out=colscr[:, 0:1], in_=xmidb[t % 2][:], axis=AX.X,
                            op=ALU.add).then_inc(sV.h, 1)
        v_red = sV.bump()
        P.wait(S_, sV, v_red)
        @P.on(S_)
        def _(e, t=t):
            e.drain()
            e.activation(out=sqscrC[:], in_=xmidb[t % 2][:], func=AF.Square,
                         accum_out=colscr[:, 1:2]).then_inc(sS.h, 1)
        s_sq = sS.bump()
        P.wait(V_, sS, s_sq)
        @P.on(V_)
        def _(e):
            e.drain()
            e.tensor_scalar(out=colscr[:, 2:3], in0=colscr[:, 0:1],
                            scalar1=1.0 / D, scalar2=None, op0=ALU.mult)
            e.tensor_scalar(out=colscr[:, 3:4], in0=colscr[:, 1:2],
                            scalar1=1.0 / D, scalar2=EPS, op0=ALU.mult,
                            op1=ALU.add)
            e.drain()
            e.tensor_tensor(out=colscr[:, 4:5], in0=colscr[:, 2:3],
                            in1=colscr[:, 2:3], op=ALU.mult)
            e.drain()
            e.tensor_tensor(out=colscr[:, 5:6], in0=colscr[:, 3:4],
                            in1=colscr[:, 4:5], op=ALU.subtract
                            ).then_inc(sV.h, 1)
        v_var = sV.bump()
        P.wait(S_, sV, v_var)
        @P.on(S_)
        def _(e):
            e.drain()
            e.activation(out=colscr[:, 8:9], in_=colscr[:, 5:6], func=AF.Ln)
            e.drain()
            e.activation(out=colscr[:, 6:7], in_=colscr[:, 8:9], func=AF.Exp,
                         scale=-0.5).then_inc(sS.h, 1)
        s_rstd = sS.bump()
        P.wait(V_, sS, s_rstd)
        @P.on(V_)
        def _(e):
            e.drain()
            e.scalar_tensor_tensor(out=colscr[:, 7:8], in0=colscr[:, 2:3],
                                   scalar=-1.0, in1=colscr[:, 6:7],
                                   op0=ALU.mult, op1=ALU.mult).then_inc(sV.h, 1)
        v_nmr = sV.bump()
        P.wait(S_, sV, v_nmr)
        if t >= 2:
            P.wait(S_, sV, v_q[t - 2])
        @P.on(S_)
        def _(e, t=t):
            e.drain()
            e.activation(out=lx2b[t % 2][:], in_=xmidb[t % 2][:],
                         func=AF.Identity, scale=colscr[:, 6:7],
                         bias=colscr[:, 7:8]).then_inc(sS.h, 1)
        s_lx.append(sS.bump())
        P.wait(V_, sS, s_lx[t])
        @P.on(V_)
        def _(e, t=t):
            e.drain()
            e.tensor_reduce(out=colscr[:, 9:10], in_=lx2b[t % 2][:], axis=AX.X,
                            op=ALU.max, apply_absolute_value=True)
            e.drain()
            e.tensor_scalar(out=colscr[:, 10:11], in0=colscr[:, 9:10],
                            scalar1=1e-5, scalar2=None, op0=ALU.max)
            e.drain()
            e.reciprocal(out=colscr[:, 11:12], in_=colscr[:, 10:11])
            e.tensor_scalar(out=colscr[:, 13:14], in0=colscr[:, 10:11],
                            scalar1=1.0 / 127.0, scalar2=None, op0=ALU.mult)
            e.drain()
            e.tensor_scalar(out=colscr[:, 12:13], in0=colscr[:, 11:12],
                            scalar1=127.0, scalar2=None, op0=ALU.mult)
            e.tensor_copy(out=dq_all[:, t:t + 1], in_=colscr[:, 13:14])
            e.drain()
            e.tensor_scalar(out=qscrC[:], in0=lx2b[t % 2][:],
                            scalar1=colscr[:, 12:13], scalar2=None,
                            op0=ALU.mult)
            e.drain()
            e.tensor_scalar(out=xqfull[:, t, :], in0=qscrC[:],
                            scalar1=MAGIC, scalar2=MAGIC, op0=ALU.add,
                            op1=ALU.subtract).then_inc(sV.h, 1)
        v_q.append(sV.bump())

    P.wait(S_, sV, v_q[NT - 1])
    P.wait(S_, dXc, 16 * dXc.n)
    P.wait(S_, dXw, 16 * dXw.n)
    P.wait(S_, dWo, wo_ld)
    for t in range(NT):
        @P.on(S_)
        def _(e, t=t):
            e.wait_ge(dTr2.h, 16 * t)
            e.dma_start(out=lnxT[:, :, 128 * t:128 * (t + 1)],
                        in_=xqfull[:, t, :], transpose=True
                        ).then_inc(dTr2.h, 16)
        dTr2.bump()

    if stage == 4:
        dD4 = P.dsem("dDbg4")
        P.wait(Y_, dTr2, 16 * dTr2.n)
        @P.on(Y_)
        def _(e, w=sV.n):
            e.wait_ge(sV.h, w)
            e.dma_start(out=dbg["xq"].rearrange("p (c m) -> p c m", c=NDC),
                        in_=lnxT[:]).then_inc(dD4.h, 16)
            e.dma_start(out=dbg["row"][:, 0:NT], in_=dq_all[:]
                        ).then_inc(dD4.h, 16)
        dD4.bump(2)

    P.barrier()
    phC.close()
    phBC.close()
    if stage <= 4:
        _dev_out(nc, P, out_hbm)
        return

    phW = contextlib.ExitStack()
    psW = phW.enter_context(nc.psum_tensor("w_ps", [128, 1, 512], F32))
    wbuf = [phW.enter_context(nc.sbuf_tensor(f"w_wb{i}", [128, DH], F32))
            for i in range(2)]
    wqb = [phW.enter_context(nc.sbuf_tensor(f"w_wq{i}", [128, DH], BF16))
           for i in range(2)]
    qscrW = phW.enter_context(nc.sbuf_tensor("w_qs", [128, DH], F32))
    gscrW = phW.enter_context(nc.sbuf_tensor("w_gs", [128, DH], F32))
    wacc = phW.enter_context(nc.sbuf_tensor("w_acc", [128, 8], F32))
    wtb = phW.enter_context(nc.sbuf_tensor("w_tb", [128, 1], BF16))
    wtl = phW.enter_context(nc.sbuf_tensor("w_tl", [128, 1], BF16))
    dWL = P.dsem("dWL")
    dWS = P.dsem("dWS")

    w1_src = w1_hbm.rearrange("(c p) m -> p c m", p=128)
    w1_dst = w1q_hbm.rearrange("(c p) m -> p c m", p=128)
    w2_src = w2_hbm.rearrange("(c p) m -> p c m", p=128)
    w2_dst = w2q_hbm.rearrange("(c p) m -> p c m", p=128)

    for wi, (src, dst, swc, invc) in enumerate((
            (w1_src, w1_dst, 0, 1), (w2_src, w2_dst, 2, 3))):
        def chunk(ap, i, wi=wi):
            if wi == 0:
                return ap[:, i, :]
            return ap[:, 4 * i:4 * (i + 1), :]
        vshape = (lambda b: b[:]) if wi == 0 else (
            lambda b: b[:].rearrange("p (a m) -> p a m", a=4))
        P.wait(G_, sV, sV.n)
        v_r = []
        for i in range(8):
            if i >= 2:
                P.wait(G_, sV, v_r[i - 2])
            P.wait(G_, dWL, 16 * dWL.n)
            @P.on(G_)
            def _(e, i=i, src=src, chunk=chunk, vshape=vshape):
                e.dma_start(out=vshape(wbuf[i % 2]), in_=chunk(src, i)
                            ).then_inc(dWL.h, 16)
            ld = 16 * dWL.bump()
            P.wait(V_, dWL, ld)
            @P.on(V_)
            def _(e, i=i):
                e.drain()
                e.tensor_reduce(out=wacc[:, i:i + 1], in_=wbuf[i % 2][:],
                                axis=AX.X, op=ALU.add,
                                apply_absolute_value=True).then_inc(sV.h, 1)
            v_r.append(sV.bump())
        @P.on(V_)
        def _(e):
            e.drain()
            e.tensor_reduce(out=colscr[:, 0:1], in_=wacc[:], axis=AX.X,
                            op=ALU.add)
            e.drain()
            e.tensor_copy(out=wtb[:], in_=colscr[:, 0:1])
            e.drain()
            e.tensor_tensor(out=wtl[:], in0=colscr[:, 0:1], in1=wtb[:],
                            op=ALU.subtract).then_inc(sV.h, 1)
        v_tot = sV.bump()
        P.wait(T_, sV, v_tot)
        @P.on(T_)
        def _(e):
            e.matmul(psW[:, 0, 0:1], ones_c[:], wtb[:], start=True,
                     stop=False)
            e.matmul(psW[:, 0, 0:1], ones_c[:], wtl[:], start=False,
                     stop=True).then_inc(sT.h, 1)
        t_tot = sT.bump()
        P.wait(V_, sT, t_tot)
        @P.on(V_)
        def _(e, swc=swc, invc=invc):
            e.drain()
            e.tensor_scalar(out=wcols[:, invc:invc + 1], in0=psW[:, 0, 0:1],
                            scalar1=1.0 / (D * DH), scalar2=1e-5,
                            op0=ALU.mult, op1=ALU.max)
            e.drain()
            e.reciprocal(out=wcols[:, swc:swc + 1],
                         in_=wcols[:, invc:invc + 1]).then_inc(sV.h, 1)
        v_sw = sV.bump()
        g_q = []
        for i in range(8):
            P.wait(G_, sV, v_sw if i < 2 else 0)
            if i >= 2:
                P.wait(G_, sV, v_r2_list[i - 2])
            P.wait(G_, dWL, 16 * dWL.n)
            @P.on(G_)
            def _(e, i=i, src=src, chunk=chunk, vshape=vshape):
                e.dma_start(out=vshape(wbuf[i % 2]), in_=chunk(src, i)
                            ).then_inc(dWL.h, 16)
            ld = 16 * dWL.bump()
            P.wait(V_, dWL, ld)
            if i >= 2:
                P.wait(V_, dWS, 16 * (dWS.n - 1))
            @P.on(V_)
            def _(e, i=i, swc=swc):
                e.drain()
                e.tensor_scalar(out=qscrW[:], in0=wbuf[i % 2][:],
                                scalar1=wcols[:, swc:swc + 1], scalar2=MAGIC,
                                op0=ALU.mult, op1=ALU.add).then_inc(sV.h, 1)
            vq2 = sV.bump()
            if i == 0:
                v_r2_list = []
            v_r2_list.append(vq2)
            @P.on(V_)
            def _(e, i=i):
                e.drain()
                e.tensor_scalar(out=gscrW[:], in0=qscrW[:], scalar1=MAGIC,
                                scalar2=1.0, op0=ALU.subtract, op1=ALU.min)
                e.drain()
                e.tensor_scalar(out=wqb[i % 2][:], in0=gscrW[:], scalar1=-1.0,
                                scalar2=None, op0=ALU.max
                                ).then_inc(sV.h, 1)
            g_q.append(sV.bump())
            P.wait(Y_, sV, g_q[i])
            P.wait(Y_, dWS, 16 * dWS.n)
            @P.on(Y_)
            def _(e, i=i, dst=dst, chunk=chunk, vshape=vshape):
                e.dma_start(out=chunk(dst, i), in_=vshape(wqb[i % 2])
                            ).then_inc(dWS.h, 16)
            dWS.bump()

    if stage == 5:
        dD5 = P.dsem("dDbg5")
        P.wait(Y_, dWS, 16 * dWS.n)
        @P.on(Y_)
        def _(e, w=sV.n):
            e.wait_ge(sV.h, w)
            e.dma_start(out=dbg["row"][:, 16:24], in_=wcols[:]
                        ).then_inc(dD5.h, 16)
        dD5.bump(1)

    P.barrier()
    phW.close()
    if stage <= 5:
        _dev_out(nc, P, out_hbm)
        return

    phD = contextlib.ExitStack()
    psD = phD.enter_context(nc.psum_tensor("d_ps", [128, 8, 512], F32))
    h3 = phD.enter_context(nc.sbuf_tensor("ds_h3", [128, NHIDC, T], FP16))
    phDq = contextlib.ExitStack()
    dqbc = phDq.enter_context(nc.sbuf_tensor("d_dqbc", [128, T], F32))
    phDp = contextlib.ExitStack()
    dqrow = phDp.enter_context(nc.sbuf_tensor("d_dqr", [1, T], F32))
    dqhi = phDp.enter_context(nc.sbuf_tensor("d_dqh", [1, T], BF16))
    dqlo = phDp.enter_context(nc.sbuf_tensor("d_dql", [1, T], BF16))
    dDq = P.dsem("dDq")
    dW1q = P.dsem("dW1q")

    @P.on(Y_)
    def _(e):
        for c2 in range(NT):
            e.dma_start(out=dqrt_hbm[c2:c2 + 1, :].rearrange("a b -> b a"),
                        in_=dq_all[:, c2:c2 + 1]).then_inc(dDq.h, 16)
        e.wait_ge(dDq.h, 16 * NT)
        e.dma_start(out=dqrow[:], in_=dqrt_hbm.rearrange("a b -> (a b)"
                                                         )[None, :]
                    ).then_inc(dDq.h, 16)
    dDq.bump(NT + 1)
    P.wait(V_, dDq, 16 * (NT + 1))
    @P.on(V_)
    def _(e):
        e.drain()
        e.tensor_copy(out=dqhi[:], in_=dqrow[:])
        e.drain()
        e.tensor_tensor(out=dqlo[:], in0=dqrow[:], in1=dqhi[:],
                        op=ALU.subtract).then_inc(sV.h, 1)
    v_split = sV.bump()
    P.wait(T_, sV, v_split)
    @P.on(T_)
    def _(e):
        for s4 in range(4):
            e.matmul(psD[:, s4, :], ones_c[0:1, :],
                     dqhi[0:1, 512 * s4:512 * (s4 + 1)], start=True,
                     stop=False)
            mm = e.matmul(psD[:, s4, :], ones_c[0:1, :],
                          dqlo[0:1, 512 * s4:512 * (s4 + 1)], start=False,
                          stop=True)
        mm.then_inc(sT.h, 1)
    t_bc = sT.bump()
    P.wait(V_, sT, t_bc)
    @P.on(V_)
    def _(e):
        e.drain()
        e.tensor_copy(out=dqbc[:].rearrange("p (a m) -> p a m", a=4),
                      in_=psD[:, 0:4, :]).then_inc(sV.h, 1)
    v_dqbc = sV.bump()
    P.barrier()
    phDp.close()

    phD2 = contextlib.ExitStack()
    hbuf = phD2.enter_context(nc.sbuf_tensor("d_hb", [128, T + 2], F32))
    g0 = phD2.enter_context(nc.sbuf_tensor("d_g0", [128, T], F32))
    g1 = phD2.enter_context(nc.sbuf_tensor("d_g1", [128, T], F32))
    w1qsb = [phD2.enter_context(nc.sbuf_tensor(f"d_w1q{i}", [128, NDC, 128],
                                               BF16)) for i in range(2)]
    @P.on(V_)
    def _(e):
        e.memset(hbuf[:, 0:1], 0.0)
        e.memset(hbuf[:, T + 1:T + 2], 0.0)
        e.drain()
        e.nop().then_inc(sV.h, 1)
    v_init = sV.bump()
    v_evac = []
    s_silu = []
    for hc in range(NHIDC):
        if hc >= 2:
            P.wait(G_, sT, t_mm1[hc - 2])
        P.wait(G_, dW1q, 16 * dW1q.n)
        @P.on(G_)
        def _(e, hc=hc):
            e.dma_start(out=w1qsb[hc % 2][:],
                        in_=w1q_hbm.rearrange("(c p) m -> p c m", p=128
                                              )[:, :, 128 * hc:128 * (hc + 1)]
                        ).then_inc(dW1q.h, 16)
        w1ld = 16 * dW1q.bump()
        P.wait(T_, dW1q, w1ld)
        P.wait(T_, sV, v_dqbc if hc == 0 else
               (v_evac[hc - 2] if hc >= 2 else 0))
        @P.on(T_)
        def _(e, hc=hc):
            for ts4 in range(4):
                for dc in range(NDC):
                    mm = e.matmul(psD[:, 4 * (hc % 2) + ts4, :],
                                  w1qsb[hc % 2][:, dc, :],
                                  lnxT[:, dc, 512 * ts4:512 * (ts4 + 1)],
                                  start=(dc == 0), stop=(dc == NDC - 1))
            mm.then_inc(sT.h, 1)
        if hc == 0:
            t_mm1 = []
        t_mm1.append(sT.bump())
        P.wait(V_, sT, t_mm1[hc])
        if hc >= 1:
            P.wait(V_, sS, s_silu[hc - 1])
        @P.on(V_)
        def _(e, hc=hc):
            e.drain()
            e.tensor_tensor(
                out=hbuf[:, 1:T + 1].rearrange("p (a m) -> p a m", a=4),
                in0=psD[:, 4 * (hc % 2):4 * (hc % 2) + 4, :],
                in1=dqbc[:].rearrange("p (a m) -> p a m", a=4), op=ALU.mult)
            e.drain()
            e.tensor_scalar(out=g1[:], in0=hbuf[:, 1:T + 1],
                            scalar1=convw_sb[:, 1, hc:hc + 1], scalar2=None,
                            op0=ALU.mult)
            e.drain()
            e.scalar_tensor_tensor(out=g0[:], in0=hbuf[:, 0:T],
                                   scalar=convw_sb[:, 0, hc:hc + 1],
                                   in1=g1[:], op0=ALU.mult, op1=ALU.add)
            e.drain()
            e.scalar_tensor_tensor(out=g1[:], in0=hbuf[:, 2:T + 2],
                                   scalar=convw_sb[:, 2, hc:hc + 1],
                                   in1=g0[:], op0=ALU.mult, op1=ALU.add
                                   ).then_inc(sV.h, 1)
        v_evac.append(sV.bump())
        P.wait(S_, sV, v_evac[hc])
        @P.on(S_)
        def _(e, hc=hc):
            e.drain()
            e.activation(out=h3[:, hc, :], in_=g1[:], func=AF.Silu,
                         scale=wcols[:, 1:2], bias=convb_sb[:, hc:hc + 1]
                         ).then_inc(sS.h, 1)
        s_silu.append(sS.bump())

    P.barrier()
    phD2.close()
    phDq.close()

    phD3a = contextlib.ExitStack()
    s2bc = phD3a.enter_context(nc.sbuf_tensor("d_s2bc", [128, T], F32))
    s2all = phD3a.enter_context(nc.sbuf_tensor("d_s2a", [128, NT], F32))
    phD3b = contextlib.ExitStack()
    mxp = [phD3b.enter_context(nc.sbuf_tensor(f"d_mx{i}", [128, T], FP16))
           for i in range(2)]
    mnp = [phD3b.enter_context(nc.sbuf_tensor(f"d_mn{i}", [128, T], FP16))
           for i in range(2)]
    mxT = phD3b.enter_context(nc.sbuf_tensor("d_mxT", [128, NT, 128], FP16,
                                             is_transpose_dest=True))
    s2row = phD3b.enter_context(nc.sbuf_tensor("d_s2r", [1, T], F32))
    s2hi = phD3b.enter_context(nc.sbuf_tensor("d_s2h", [1, T], BF16))
    s2lo = phD3b.enter_context(nc.sbuf_tensor("d_s2l", [1, T], BF16))
    dS2 = P.dsem("dS2")
    dYw = P.dsem("dYw")

    @P.on(V_)
    def _(e):
        e.memset(mxp[0][:], 0.0)
        e.memset(mnp[0][:], 0.0)
        e.drain()
        for hc in range(NHIDC):
            e.tensor_tensor(out=mxp[(hc + 1) % 2][:], in0=mxp[hc % 2][:],
                            in1=h3[:, hc, :], op=ALU.max)
            e.tensor_tensor(out=mnp[(hc + 1) % 2][:], in0=mnp[hc % 2][:],
                            in1=h3[:, hc, :], op=ALU.min)
            e.drain()
        e.nop().then_inc(sV.h, 1)
    v_mn = sV.bump()
    @P.on(V_)
    def _(e):
        e.tensor_scalar(out=mnp[1][:], in0=mnp[0][:], scalar1=-1.0,
                        scalar2=None, op0=ALU.mult)
        e.drain()
        e.tensor_tensor(out=mxp[1][:], in0=mxp[0][:], in1=mnp[1][:],
                        op=ALU.max).then_inc(sV.h, 1)
    v_comb = sV.bump()
    P.wait(S_, sV, v_comb)
    @P.on(S_)
    def _(e):
        e.dma_start(out=mxT[:], in_=mxp[1][:], transpose=True
                    ).then_inc(dS2.h, 16)
    P.wait(V_, dS2, 16 * dS2.bump())
    @P.on(V_)
    def _(e):
        e.drain()
        e.tensor_reduce(out=colscr[:].rearrange("p (c u) -> p c u", u=1),
                        in_=mxT[:], axis=AX.X, op=ALU.max)
        e.drain()
        e.tensor_scalar(out=dq2_all[:], in0=colscr[:], scalar1=1e-5,
                        scalar2=1.0 / 127.0, op0=ALU.max, op1=ALU.mult)
        e.tensor_scalar(out=s2all[:], in0=colscr[:], scalar1=1e-5,
                        scalar2=None, op0=ALU.max)
        e.drain()
        e.reciprocal(out=s2all[:], in_=s2all[:])
        e.drain()
        e.tensor_scalar(out=s2all[:], in0=s2all[:], scalar1=127.0,
                        scalar2=None, op0=ALU.mult).then_inc(sV.h, 1)
    v_s2 = sV.bump()
    P.wait(Y_, sV, v_s2)
    @P.on(Y_)
    def _(e):
        for c2 in range(NT):
            e.dma_start(out=s2rt_hbm[c2:c2 + 1, :].rearrange("a b -> b a"),
                        in_=s2all[:, c2:c2 + 1]).then_inc(dS2.h, 16)
        e.wait_ge(dS2.h, 16 * (NT + 1))
        e.dma_start(out=s2row[:], in_=s2rt_hbm.rearrange("a b -> (a b)"
                                                         )[None, :]
                    ).then_inc(dS2.h, 16)
    dS2.bump(NT + 1)
    P.wait(V_, dS2, 16 * (NT + 2))
    @P.on(V_)
    def _(e):
        e.drain()
        e.tensor_copy(out=s2hi[:], in_=s2row[:])
        e.drain()
        e.tensor_tensor(out=s2lo[:], in0=s2row[:], in1=s2hi[:],
                        op=ALU.subtract).then_inc(sV.h, 1)
    v_s2s = sV.bump()
    P.wait(T_, sV, v_s2s)
    @P.on(T_)
    def _(e):
        for s4 in range(4):
            e.matmul(psD[:, s4, :], ones_c[0:1, :],
                     s2hi[0:1, 512 * s4:512 * (s4 + 1)], start=True,
                     stop=False)
            mm = e.matmul(psD[:, s4, :], ones_c[0:1, :],
                          s2lo[0:1, 512 * s4:512 * (s4 + 1)], start=False,
                          stop=True)
        mm.then_inc(sT.h, 1)
    t_s2bc = sT.bump()
    P.wait(V_, sT, t_s2bc)
    @P.on(V_)
    def _(e):
        e.drain()
        e.tensor_copy(out=s2bc[:].rearrange("p (a m) -> p a m", a=4),
                      in_=psD[:, 0:4, :]).then_inc(sV.h, 1)
    v_s2bc = sV.bump()
    P.barrier()
    phD3b.close()

    phD3c = contextlib.ExitStack()
    qscrD = phD3c.enter_context(nc.sbuf_tensor("d_qsD", [128, T], F32))
    hqrow = [phD3c.enter_context(nc.sbuf_tensor(f"d_hq{i}", [128, T], BF16))
             for i in range(2)]
    g_hq = []
    for hc in range(NHIDC):
        if hc >= 2:
            P.wait(V_, dYw, 16 * (hc - 1))
        @P.on(V_)
        def _(e, hc=hc):
            e.drain()
            e.tensor_tensor(out=qscrD[:], in0=h3[:, hc, :], in1=s2bc[:],
                            op=ALU.mult)
            e.drain()
            e.tensor_scalar(out=hqrow[hc % 2][:], in0=qscrD[:], scalar1=MAGIC,
                            scalar2=MAGIC, op0=ALU.add, op1=ALU.subtract
                            ).then_inc(sV.h, 1)
        g_hq.append(sV.bump())
        P.wait(Y_, sV, g_hq[hc])
        P.wait(Y_, dYw, 16 * dYw.n)
        @P.on(Y_)
        def _(e, hc=hc):
            e.dma_start(out=y_hbm.rearrange("(c p) m -> p c m", p=128
                                            )[:, hc, :],
                        in_=hqrow[hc % 2][:]).then_inc(dYw.h, 16)
        dYw.bump()

    if stage == 6:
        dD6 = P.dsem("dDbg6")
        P.wait(Y_, dYw, 16 * dYw.n)
        @P.on(Y_)
        def _(e, w=sV.n):
            e.wait_ge(sV.h, w)
            e.dma_start(out=dbg["h3"].rearrange("p (c m) -> p c m", c=NHIDC),
                        in_=h3[:]).then_inc(dD6.h, 16)
            e.dma_start(out=dbg["row"][:, 32:48], in_=dq2_all[:]
                        ).then_inc(dD6.h, 16)
        dD6.bump(2)

    P.barrier()
    phD3c.close()
    phD3a.close()
    phD.close()
    if stage <= 6:
        _dev_out(nc, P, out_hbm)
        return

    phE = contextlib.ExitStack()
    psE = phE.enter_context(nc.psum_tensor("e_ps", [128, 4, 512], F32))
    w2qsb = phE.enter_context(nc.sbuf_tensor("e_w2q", [128, NHIDC, D], BF16))
    hqt = [phE.enter_context(nc.sbuf_tensor(f"e_hq{i}", [128, NHIDC, 128],
                                            BF16)) for i in range(2)]
    xmt = [phE.enter_context(nc.sbuf_tensor(f"e_xm{i}", [128, D], F32))
           for i in range(2)]
    outb = [phE.enter_context(nc.sbuf_tensor(f"e_ob{i}", [128, D], F32))
            for i in range(2)]
    dq2w = phE.enter_context(nc.sbuf_tensor("e_dq2w", [128, NT], F32))
    dW2q = P.dsem("dW2q")
    dHq = P.dsem("dHq")
    dXm = P.dsem("dXm")
    dOut = P.dsem("dOut")

    @P.on(G_)
    def _(e):
        e.dma_start(out=w2qsb[:], in_=w2q_hbm.rearrange("(c p) m -> p c m",
                                                        p=128)
                    ).then_inc(dW2q.h, 16)
    w2ld = 16 * dW2q.bump()
    @P.on(V_)
    def _(e):
        e.drain()
        e.tensor_scalar(out=dq2w[:], in0=dq2_all[:], scalar1=wcols[:, 3:4],
                        scalar2=None, op0=ALU.mult).then_inc(sV.h, 1)
    v_dq2w = sV.bump()
    P.wait(T_, dW2q, w2ld)

    y_src = y_hbm.rearrange("(c p) (n m) -> p c n m", p=128, m=128)
    v_out = []
    for t in range(NT):
        if t >= 2:
            P.wait(Y_, sT, t_mm2[t - 2])
            P.wait(Y_, sV, v_out[t - 2])
        P.wait(Y_, dHq, 16 * dHq.n)
        P.wait(Y_, dXm, 16 * dXm.n)
        @P.on(Y_)
        def _(e, t=t):
            e.dma_start(out=hqt[t % 2][:], in_=y_src[:, :, t, :]
                        ).then_inc(dHq.h, 16)
            e.dma_start(out=xmt[t % 2][:],
                        in_=xnew_hbm[128 * t:128 * (t + 1), :]
                        ).then_inc(dXm.h, 16)
        ld_hq = 16 * dHq.bump()
        ld_xm = 16 * dXm.bump()
        P.wait(T_, dHq, ld_hq)
        if t >= 2:
            P.wait(T_, sV, v_out[t - 2])
        @P.on(T_)
        def _(e, t=t):
            for half in range(2):
                for hc in range(NHIDC):
                    mm = e.matmul(psE[:, 2 * (t % 2) + half, :],
                                  hqt[t % 2][:, hc, :],
                                  w2qsb[:, hc, 512 * half:512 * (half + 1)],
                                  start=(hc == 0), stop=(hc == NHIDC - 1))
            mm.then_inc(sT.h, 1)
        if t == 0:
            t_mm2 = []
        t_mm2.append(sT.bump())
        P.wait(V_, sT, t_mm2[t])
        P.wait(V_, dXm, ld_xm)
        if t >= 2:
            P.wait(V_, dOut, 16 * (t - 1))
        @P.on(V_)
        def _(e, t=t):
            e.drain()
            e.scalar_tensor_tensor(
                out=outb[t % 2][:].rearrange("p (a m) -> p a m", a=2),
                in0=psE[:, 2 * (t % 2):2 * (t % 2) + 2, :],
                scalar=dq2w[:, t:t + 1],
                in1=xmt[t % 2][:].rearrange("p (a m) -> p a m", a=2),
                op0=ALU.mult, op1=ALU.add).then_inc(sV.h, 1)
        v_out.append(sV.bump())
        P.wait(G_, sV, v_out[t])
        P.wait(G_, dOut, 16 * dOut.n)
        @P.on(G_)
        def _(e, t=t):
            e.dma_start(out=out_hbm[128 * t:128 * (t + 1), :],
                        in_=outb[t % 2][:]).then_inc(dOut.h, 16)
        dOut.bump()
    P.wait(G_, dOut, 16 * NT)
    P.wait(Y_, dOut, 16 * NT)
    phE.close()


def _attn_half(c, H):
    P = c["P"]
    sT, sV, sS, sG = (P.sem[k] for k in ("T", "V", "S", "G"))
    T_, V_, S_, G_, Y_ = "T", "V", "S", "G", "Y"
    lnxT = c["lnxT"]; k1f = c["k1f"]; k2f = c["k2f"]; vtk = c["vtk"]
    Wk_sb = c["Wk_sb"]; Wq_sb = c["Wq_sb"]; qf = c["qf"]; e_sb = c["e_sb"]
    aoT = c["aoT"]; tbuf = c["tbuf"]; cbuf = c["cbuf"]; rcbuf = c["rcbuf"]
    Gb = c["Gb"]; Db = c["Db"]; srow = c["srow"]; grow = c["grow"]
    dinvrow = c["dinvrow"]; lamsc = c["lamsc"]
    ps_sc = c["ps_sc"]; ps_ev = c["ps_ev"]
    ps_r = c["ps_r"]; W_hbm = c["W_hbm"]; dWh = c["dWh"]
    maskEO = c["maskEO"]; onehot3 = c["onehot3"]; ones_c = c["ones_c"]
    dbg = c["dbg"]

    cols = slice(512 * H, 512 * H + 512)
    dWq = P.dsem(f"dWq{H}")
    entry_mm = P.sem["T"].n

    def warm(n, ev=False):
        @P.on(T_)
        def _(e, n=n, ev=ev):
            for _ in range(n):
                if ev:
                    e.matmul(ps_ev[0:64, 1, 0:448], ones_c[0:64, 0:64],
                             maskEO[0:64, 0:448], start=True, stop=True,
                             tile_position=(0, 0))
                else:
                    e.matmul(ps_sc[0:64, 0, 0:448], ones_c[0:64, 0:64],
                             maskEO[0:64, 0:448], start=True, stop=True,
                             tile_position=(0, 0))

    P.wait(G_, sT, entry_mm)
    @P.on(G_)
    def _(e):
        e.dma_start(out=Wq_sb[:],
                    in_=W_hbm["Wq"].rearrange("(c p) m -> p c m", p=128
                                              )[:, :, cols]).then_inc(dWq.h, 16)
    wq_ld = 16 * dWq.bump()

    def proj(name, which, prev_mm):
        if prev_mm:
            P.wait(G_, sT, prev_mm)
        @P.on(G_)
        def _(e, name=name):
            e.dma_start(out=Wk_sb[:],
                        in_=W_hbm[name].rearrange("(c p) m -> p c m", p=128
                                                  )[:, :, cols]
                        ).then_inc(dWh.h, 16)
        ld = 16 * dWh.bump()
        P.wait(T_, dWh, ld)
        evs = []
        if which in ("k1", "k2"):
            dst = k1f if which == "k1" else k2f
            for tg in range(4):
                for dc in range(4):
                    if len(evs) >= 4:
                        P.wait(T_, sV, evs[-4])
                    @P.on(T_)
                    def _(e, dc=dc, tg=tg):
                        for dinc in range(NDC):
                            mm = e.matmul(
                                ps_sc[:, dc, 0:512],
                                Wk_sb[:, dinc, 128 * dc:128 * (dc + 1)],
                                lnxT[:, dinc, 512 * tg:512 * (tg + 1)],
                                start=(dinc == 0), stop=(dinc == NDC - 1))
                        mm.then_inc(sT.h, 1)
                    mm = sT.bump()
                    @P.on(V_)
                    def _(e, dc=dc, tg=tg, mm=mm, dst=dst):
                        e.wait_ge(sT.h, mm)
                        e.drain()
                        e.tensor_copy(out=dst[:, dc, 512 * tg:512 * (tg + 1)],
                                      in_=ps_sc[:, dc, 0:512]).then_inc(sV.h, 1)
                    evs.append(sV.bump())
            P.wait(T_, sV, evs[-1])
        else:
            for tokt in range(NT):
                if len(evs) >= 4:
                    P.wait(T_, sV, evs[-4])
                @P.on(T_)
                def _(e, tokt=tokt):
                    for dinc in range(NDC):
                        mm = e.matmul(ps_sc[:, tokt % 4, 0:512],
                                      lnxT[:, dinc, 128 * tokt:128 * (tokt + 1)],
                                      Wk_sb[:, dinc, 0:512],
                                      start=(dinc == 0), stop=(dinc == NDC - 1))
                    mm.then_inc(sT.h, 1)
                mm = sT.bump()
                @P.on(V_)
                def _(e, tokt=tokt, mm=mm):
                    e.wait_ge(sT.h, mm)
                    e.drain()
                    e.tensor_copy(out=vtk[:, tokt, :],
                                  in_=ps_sc[:, tokt % 4, 0:512]).then_inc(sV.h, 1)
                evs.append(sV.bump())
            P.wait(T_, sV, evs[-1])
        return evs[-1], sT.n

    pk1, m1 = proj("Wk1", "k1", entry_mm)
    pk2, m2 = proj("Wk2", "k2", m1)
    pv, m3 = proj("Wv", "v", m2)
    P.wait(T_, dWq, wq_ld)

    if c["stage"] == 2 and H == 0 and "k1f" in dbg:
        dD = P.dsem(f"dDbgB{H}")
        @P.on(Y_)
        def _(e, w=pv):
            e.wait_ge(sV.h, w)
            e.dma_start(out=dbg["k1f"].rearrange("p (c m) -> p c m", c=4),
                        in_=k1f[:]).then_inc(dD.h, 16)
            e.dma_start(out=dbg["k2f"].rearrange("p (c m) -> p c m", c=4),
                        in_=k2f[:]).then_inc(dD.h, 16)
            e.dma_start(out=dbg["vtk"].rearrange("p (c m) -> p c m", c=NT),
                        in_=vtk[:]).then_inc(dD.h, 16)
        dD.bump(3)

    for qg in range(_NQG):
        qs = slice(QG * qg, QG * (qg + 1))
        qbuf = qf[qg % 2]
        @P.on(T_)
        def _(e, qs=qs):
            for dc in range(4):
                for dinc in range(NDC):
                    mm = e.matmul(ps_sc[:, dc, 0:256],
                                  Wq_sb[:, dinc, 128 * dc:128 * (dc + 1)],
                                  lnxT[:, dinc, qs],
                                  start=(dinc == 0), stop=(dinc == NDC - 1))
            mm.then_inc(sT.h, 1)
        qmm = sT.bump()
        P.wait(S_, sT, qmm)
        @P.on(S_)
        def _(e, qbuf=qbuf):
            e.drain()
            for dc in range(4):
                act = e.activation(out=qbuf[:, dc, :], in_=ps_sc[:, dc, 0:256],
                                   func=AF.Identity, scale=SCALE)
            act.then_inc(sS.h, 1)
        qev = sS.bump()
        P.wait(T_, sS, qev)

        nkc = 2 * qg + 2
        for p in range(_NPAIR):
            par = p % 2
            ebuf = e_sb[p % 2]
            gbuf = Gb[par]
            dbuf = Db[par]
            exp_s = []
            for kc2 in range(0, nkc, 2):
                for kc in (kc2, kc2 + 1):
                    b = 2 * (kc % 2)
                    ks = slice(128 * kc, 128 * (kc + 1))
                    @P.on(T_)
                    def _(e, b=b, ks=ks, p=p, qs=qs, qbuf=qbuf):
                        e.matmul(ps_sc[:, b, 0:256], k1f[0:64, p, ks],
                                 qbuf[0:64, p, :], start=True, stop=True,
                                 tile_position=(0, 0))
                        e.matmul(ps_sc[:, b + 1, 0:256], k1f[64:128, p, ks],
                                 qbuf[64:128, p, :], start=True, stop=True,
                                 tile_position=(64, 0))
                        e.matmul(ps_sc[:, b, 256:512], k2f[0:64, p, ks],
                                 qbuf[0:64, p, :], start=True, stop=True,
                                 tile_position=(0, 0))
                        e.matmul(ps_sc[:, b + 1, 256:512], k2f[64:128, p, ks],
                                 qbuf[64:128, p, :], start=True, stop=True,
                                 tile_position=(64, 0)).then_inc(sT.h, 1)
                    sT.bump()
                P.wait(S_, sT, sT.n)
                @P.on(S_)
                def _(e, kc2=kc2, ebuf=ebuf):
                    e.drain()
                    e.activation(out=ebuf[:, kc2:kc2 + 2, :, :],
                                 in_=ps_sc[:], func=AF.Exp).then_inc(sS.h, 1)
                ex = sS.bump()
                exp_s.append(ex)
                if kc2 + 2 >= nkc:
                    P.wait(V_, sS, ex)
                    @P.on(V_)
                    def _(e, kc2=kc2, ebuf=ebuf):
                        e.drain()
                        e.tensor_tensor(
                            out=ebuf[:, kc2:kc2 + 2, :, :].rearrange(
                                "p a b m -> p (a b m)"),
                            in0=ebuf[:, kc2:kc2 + 2, :, :].rearrange(
                                "p a b m -> p (a b m)"),
                            in1=maskEO[:], op=ALU.mult).then_inc(sV.h, 1)
                    sV.bump()
                    P.wait(T_, sV, sV.n)
                else:
                    P.wait(T_, sS, ex)
                @P.on(T_)
                def _(e, kc2=kc2, p=p, ebuf=ebuf, first=(kc2 == 0),
                      last=(kc2 + 2 >= nkc)):
                    for i, kc in enumerate((kc2, kc2 + 1)):
                        st = first and i == 0
                        sp = last and i == 1
                        e.matmul(ps_r[0:16, 0, 0:512],
                                 onehot3[:, 0:16],
                                 ebuf[:, kc, 0:2, :], start=st, stop=sp,
                                 tile_position=(0, 0))
                        mm = e.matmul(ps_r[0:16, 1, 0:512],
                                      onehot3[:, 0:16],
                                      ebuf[:, kc, 2:4, :], start=st, stop=sp,
                                      tile_position=(0, 0))
                    mm.then_inc(sT.h, 1)
                sT.bump()
            r_done = sT.n

            P.wait(S_, sT, r_done)
            @P.on(S_)
            def _(e, p=p):
                e.drain()
                e.activation(out=srow[0:1, 0:512], in_=ps_r[0:1, 0, 0:512],
                             func=AF.Ln)
                e.activation(out=srow[0:1, 512:1024], in_=ps_r[0:1, 1, 0:512],
                             func=AF.Ln).then_inc(sS.h, 1)
            ln_r = sS.bump()
            P.wait(V_, sS, ln_r)
            @P.on(V_)
            def _(e, p=p, H=H):
                e.drain()
                e.tensor_tensor(out=srow[0:1, 1024:1280], in0=srow[0:1, 0:256],
                                in1=srow[0:1, 256:512], op=ALU.subtract)
                e.tensor_tensor(out=srow[0:1, 1280:1536], in0=srow[0:1, 512:768],
                                in1=srow[0:1, 768:1024], op=ALU.subtract)
                hA = 8 * H + 2 * p
                e.drain()
                e.tensor_scalar(out=srow[0:1, 0:256],
                                in0=ps_r[0:1, 0, 0:256],
                                scalar1=lamsc[0:1, 2 * hA + 1:2 * hA + 2],
                                scalar2=None, op0=ALU.mult)
                e.tensor_scalar(out=srow[0:1, 256:512],
                                in0=ps_r[0:1, 1, 0:256],
                                scalar1=lamsc[0:1, 2 * hA + 3:2 * hA + 4],
                                scalar2=None, op0=ALU.mult)
                e.tensor_scalar(
                    out=srow[0:1, 512:1024].rearrange("p (a m) -> p a m", a=2),
                    in0=ps_r[0:1, 0:2, 0:256],
                    scalar1=1e-6, scalar2=None, op0=ALU.mult
                    ).then_inc(sV.h, 1)
            v_t2 = sV.bump()
            P.wait(S_, sV, v_t2)
            @P.on(S_)
            def _(e, p=p, H=H):
                hA = 8 * H + 2 * p
                e.drain()
                e.activation(out=grow[0:1, 0:256], in_=srow[0:1, 1024:1280],
                             func=AF.Exp, bias=lamsc[0:1, 2 * hA:2 * hA + 1])
                e.activation(out=grow[0:1, 256:512], in_=srow[0:1, 1280:1536],
                             func=AF.Exp,
                             bias=lamsc[0:1, 2 * hA + 2:2 * hA + 3]
                             ).then_inc(sS.h, 1)
            s_g = sS.bump()
            P.wait(T_, sS, s_g)
            @P.on(T_)
            def _(e):
                e.matmul(ps_r[:, 1, :], ones_c[0:1, :], grow[0:1, :],
                         start=True, stop=True).then_inc(sT.h, 1)
            t_gb = sT.bump()
            P.wait(V_, sT, t_gb)
            @P.on(V_)
            def _(e, gbuf=gbuf):
                e.drain()
                e.tensor_copy(out=gbuf[:, 0:512], in_=ps_r[:, 1, :])
                e.tensor_copy(out=gbuf[:, 512:1024], in_=ps_r[:, 1, :]
                              ).then_inc(sV.h, 1)
            v_gb = sV.bump()

            P.wait(V_, sS, exp_s[-1])
            ev_mms = []
            rc_s = []
            for g in range(nkc // 2):
                par2 = g % 2
                if g >= 2:
                    P.wait(V_, sT, ev_mms[g - 2])
                    P.wait(V_, sS, rc_s[g - 2])
                @P.on(V_)
                def _(e, g=g, par2=par2, ebuf=ebuf, gbuf=gbuf, w=v_gb):
                    e.wait_ge(sV.h, w)
                    e.drain()
                    e.tensor_tensor(
                        out=tbuf[:].rearrange("p (a b m) -> p a b m",
                                              a=2, b=2),
                        in0=ebuf[:, 2 * g:2 * g + 2, 1:4:2, :],
                        in1=gbuf[:].rearrange("p (a b m) -> p a b m",
                                              a=2, b=2),
                        op=ALU.mult)
                    e.drain()
                    e.tensor_tensor(
                        out=cbuf[:, par2, :].rearrange(
                            "p (a b m) -> p a b m", a=2, b=2),
                        in0=ebuf[:, 2 * g:2 * g + 2, 0:4:2, :],
                        in1=tbuf[:].rearrange("p (a b m) -> p a b m",
                                              a=2, b=2),
                        op=ALU.subtract).then_inc(sV.h, 1)
                v_c = sV.bump()
                P.wait(S_, sV, v_c)
                if g >= 2:
                    P.wait(S_, sT, ev_mms[g - 2])
                @P.on(S_)
                def _(e, par2=par2):
                    e.drain()
                    e.activation(out=rcbuf[:, par2, :],
                                 in_=cbuf[:, par2, :],
                                 func=AF.Relu).then_inc(sS.h, 1)
                rc_s.append(sS.bump())
                P.wait(T_, sV, v_c)
                P.wait(T_, sS, rc_s[g])
                @P.on(T_)
                def _(e, g=g, par2=par2, p=p, w=v_t2, stg=(g == 0),
                      spg=(g == nkc // 2 - 1)):
                    e.wait_ge(sV.h, w)
                    for j in range(2):
                        kc = 2 * g + j
                        st = stg and j == 0
                        sp = spg and j == 1
                        e.matmul(ps_ev[0:64, 0, 0:256],
                                 vtk[:, kc, 128 * p:128 * p + 64],
                                 cbuf[:, par2, 512 * j:512 * j + 256],
                                 start=st, stop=sp, tile_position=(0, 0))
                        e.matmul(ps_ev[64:128, 1, 0:256],
                                 vtk[:, kc, 128 * p + 64:128 * (p + 1)],
                                 cbuf[:, par2, 512 * j + 256:512 * (j + 1)],
                                 start=st, stop=sp, tile_position=(0, 64))
                        mm = e.matmul(ps_r[0:16, 0, 0:512],
                                      onehot3[:, 0:16],
                                      rcbuf[:, par2, 512 * j:512 * (j + 1)],
                                      start=st, stop=sp, tile_position=(0, 0))
                    mm.then_inc(sT.h, 1)
                ev_mms.append(sT.bump())
            ev_done = sT.n
            P.wait(V_, sT, ev_done)

            @P.on(V_)
            def _(e, p=p):
                e.drain()
                e.scalar_tensor_tensor(out=srow[0:1, 1024:1536],
                                       in0=ps_r[0:1, 0, :],
                                       scalar=2.0, in1=srow[0:1, 0:512],
                                       op0=ALU.mult, op1=ALU.subtract)
                e.drain()
                e.tensor_tensor(out=srow[0:1, 1536:2048], in0=srow[0:1, 512:1024],
                                in1=srow[0:1, 1024:1536], op=ALU.max
                                ).then_inc(sV.h, 1)
            v_d = sV.bump()
            P.wait(S_, sV, v_d)
            @P.on(S_)
            def _(e):
                e.drain()
                e.activation(out=srow[0:1, 1024:1536], in_=srow[0:1, 1536:2048], func=AF.Ln)
                e.drain()
                e.activation(out=dinvrow[0:1, :], in_=srow[0:1, 1024:1536],
                             func=AF.Exp, scale=-1.0).then_inc(sS.h, 1)
            s_d = sS.bump()
            P.wait(T_, sS, s_d)
            @P.on(T_)
            def _(e):
                e.matmul(ps_r[:, 1, 0:256], ones_c[0:1, :], dinvrow[0:1, 0:256],
                         start=True, stop=True)
                e.matmul(ps_r[:, 1, 256:512], ones_c[0:1, :],
                         dinvrow[0:1, 256:512],
                         start=True, stop=True).then_inc(sT.h, 1)
            t_db = sT.bump()
            P.wait(V_, sT, t_db)
            @P.on(V_)
            def _(e, dbuf=dbuf):
                e.drain()
                e.tensor_copy(out=dbuf[:], in_=ps_r[:, 1, :]
                              ).then_inc(sV.h, 1)
            v_db = sV.bump()
            @P.on(V_)
            def _(e, p=p, H=H, qs=qs, dbuf=dbuf):
                e.drain()
                e.tensor_tensor(out=aoT[0:64, 4 * H + p, qs],
                                in0=ps_ev[0:64, 0, 0:256],
                                in1=dbuf[0:64, 0:256], op=ALU.mult)
                e.tensor_tensor(out=aoT[64:128, 4 * H + p, qs],
                                in0=ps_ev[64:128, 1, 0:256],
                                in1=dbuf[64:128, 256:512],
                                op=ALU.mult).then_inc(sV.h, 1)
            v_ao = sV.bump()
            P.wait(T_, sV, v_ao)
            P.wait(S_, sV, v_ao)

    if c["stage"] == 2 and "ao" in dbg:
        dD2 = P.dsem(f"dDbgB2{H}")
        @P.on(Y_)
        def _(e, w=sV.n):
            e.wait_ge(sV.h, w)
            e.dma_start(out=dbg["ao"].rearrange("p (c m) -> p c m", c=8
                                                )[:, 0:4, :],
                        in_=aoT[:, 0:4, :]).then_inc(dD2.h, 16)
            e.dma_start(out=dbg["e"].rearrange("p (a b) -> p a b", a=16),
                        in_=e_sb[0][:].rearrange("p a b m -> p a (b m)")
                        ).then_inc(dD2.h, 16)
            e.dma_start(out=dbg["qf"].rearrange("p (c m) -> p c m", c=4),
                        in_=qf[1][:]).then_inc(dD2.h, 16)
        dD2.bump(3)


def _dev_out(nc, P, out_hbm):
    dOut = P.dsem("dOut")
    zb = P.st.enter_context(nc.sbuf_tensor("z_out", [128, D], F32))
    @P.on("V")
    def _(e):
        e.memset(zb[:], 0.0)
        e.drain()
        e.nop().then_inc(P.sem["V"].h, 1)
    w = P.sem["V"].bump()
    P.wait("Y", P.sem["V"], w)
    @P.on("Y")
    def _(e):
        for t in range(NT):
            e.dma_start(out=out_hbm[128 * t:128 * (t + 1), :],
                        in_=zb[:]).then_inc(dOut.h, 16)
        e.wait_ge(dOut.h, 16 * NT)
    dOut.bump(NT)


def kernel(L0_attn_x, L0_attn_Wq, L0_attn_Wk1, L0_attn_Wk2, L0_attn_Wv,
           L0_attn_Wo, L0_attn_lambda_logit, L0_ffn_w1, L0_ffn_conv_w,
           L0_ffn_conv_b, L0_ffn_w2, _stage=99, _res=None):
    consts = _consts()
    nc = build(_stage)
    cw = np.asarray(L0_ffn_conv_w, np.float32).reshape(DH, 3)
    base = {
        "WqT": np.ascontiguousarray(np.asarray(L0_attn_Wq, np.float32).T),
        "Wk1T": np.ascontiguousarray(np.asarray(L0_attn_Wk1, np.float32).T),
        "Wk2T": np.ascontiguousarray(np.asarray(L0_attn_Wk2, np.float32).T),
        "WvT": np.ascontiguousarray(np.asarray(L0_attn_Wv, np.float32).T),
        "WoT": np.ascontiguousarray(np.asarray(L0_attn_Wo, np.float32).T),
        "lam": np.ascontiguousarray(L0_attn_lambda_logit, np.float32
                                    ).reshape(NH, 1),
        "w1T": np.ascontiguousarray(np.asarray(L0_ffn_w1, np.float32).T),
        "convw": np.ascontiguousarray(cw.reshape(32, 128, 3
                                                 ).transpose(1, 2, 0)),
        "convb": np.ascontiguousarray(
            np.asarray(L0_ffn_conv_b, np.float32).reshape(32, 128).T),
        "w2T": np.ascontiguousarray(np.asarray(L0_ffn_w2, np.float32).T),
    }
    base.update({"h_" + k: v for k, v in consts.items()})
    in_maps = []
    xs = np.asarray(L0_attn_x)
    for c in range(B):
        m = dict(base)
        m["x"] = np.ascontiguousarray(xs[c], np.float32)
        in_maps.append(m)
    res = run_bass_kernel_spmd(nc, in_maps, core_ids=list(range(B)))
    if _res is not None:
        _res.append(res)
    return np.stack([res.results[c]["out"] for c in range(B)], axis=0)


# version 12
# speedup vs baseline: 1.0392x; 1.0392x over previous
import contextlib
import os
import numpy as np
import ml_dtypes
import concourse.bass as bass
import concourse.mybir as mybir
from concourse.bass_utils import run_bass_kernel_spmd

F32 = mybir.dt.float32
BF16 = mybir.dt.bfloat16
FP16 = mybir.dt.float16
AF = mybir.ActivationFunctionType
ALU = mybir.AluOpType
AX = mybir.AxisListType

B = 8
T = 2048
D = 1024
NH = 16
HD = 64
DH = 4096
NT = T // 128
NDC = D // 128
SCALE = HD ** -0.5
MAGIC = float(np.float32(1.5 * 2.0 ** 23))
EPS = 1e-5
QG = 256
NQG = T // QG
NHIDC = DH // 128
_NLN = int(os.environ.get("KLN_NT", str(NT)))
_NQG = int(os.environ.get("KLN_NQG", str(NQG)))
_NPAIR = int(os.environ.get("KLN_NPAIR", "4"))


def _consts():
    c = {}
    kk = np.arange(128)[:, None]
    qq = np.arange(128)[None, :]
    tri = (kk <= qq).astype(np.float32)
    ev = np.concatenate([tri, np.ones((128, 128), np.float32)], axis=1)
    od = np.concatenate([np.zeros((128, 128), np.float32), tri], axis=1)
    mask = np.concatenate([np.tile(ev, (1, 4)), np.tile(od, (1, 4))], axis=1)
    c["maskEO"] = mask.astype(ml_dtypes.bfloat16)
    o3 = np.zeros((128, 16, 16), dtype=np.float32)
    for p in range(16):
        o3[:, p, p] = 1
    c["onehot3"] = o3.reshape(128, 256).astype(ml_dtypes.bfloat16)
    c["ones_c"] = np.ones((128, 128), dtype=ml_dtypes.bfloat16)
    c["onesf"] = np.ones((128, 256), dtype=np.float32)
    s2 = np.zeros((2, 128), dtype=np.float32)
    s2[0, 0:64] = 1
    s2[1, 64:128] = 1
    c["sel2"] = s2.astype(ml_dtypes.bfloat16)
    return c


class Sem:
    def __init__(self, nc, st, name):
        self.h = st.enter_context(nc.semaphore(name))
        self.n = 0

    def bump(self, k=1):
        self.n += k
        return self.n


class Prog:
    ENGS = ("T", "V", "S", "G", "Y")

    def __init__(self, nc, st):
        self.nc = nc
        self.st = st
        self.steps = []
        self.sem = {e: Sem(nc, st, f"s{e}") for e in self.ENGS}
        self.dsems = []
        self.bar = Sem(nc, st, "bar")
        self.nbar = 0

    def dsem(self, name):
        s = Sem(self.nc, self.st, name)
        self.dsems.append(s)
        return s

    def on(self, eng):
        def reg(fn):
            self.steps.append((eng, fn))
            return fn
        return reg

    def wait(self, eng, sem, val):
        if val <= 0:
            return
        @self.on(eng)
        def _(e, sem=sem, val=val):
            e.wait_ge(sem.h, val)

    def barrier(self):
        self.nbar += 1
        n = self.nbar
        tgt = [(s.h, 16 * s.n) for s in self.dsems if s.n]
        for eng in self.ENGS:
            @self.on(eng)
            def _(e, tgt=tuple(tgt), n=n):
                for h, v in tgt:
                    e.wait_ge(h, v)
                e.drain()
                e.nop().then_inc(self.bar.h, 1)
                e.wait_ge(self.bar.h, 5 * n)

    def finalize(self, block):
        for eng, attr in (("T", "tensor"), ("V", "vector"), ("S", "scalar"),
                          ("G", "gpsimd"), ("Y", "sync")):
            def mk(eng):
                def body(e):
                    for name, fn in self.steps:
                        if name == eng:
                            fn(e)
                return body
            getattr(block, attr)(mk(eng))


def build(stage=99):
    nc = bass.Bass("TRN2", target_bir_lowering=False)
    x_hbm = nc.dram_tensor("x", [T, D], F32, kind="ExternalInput").ap()
    W_hbm = {w: nc.dram_tensor(w + "T", [D, D], F32, kind="ExternalInput").ap()
             for w in ("Wq", "Wk1", "Wk2", "Wv", "Wo")}
    lam_hbm = nc.dram_tensor("lam", [NH, 1], F32, kind="ExternalInput").ap()
    w1_hbm = nc.dram_tensor("w1T", [D, DH], F32, kind="ExternalInput").ap()
    convw_hbm = nc.dram_tensor("convw", [128, 3, 32], F32,
                               kind="ExternalInput").ap()
    convb_hbm = nc.dram_tensor("convb", [128, 32], F32,
                               kind="ExternalInput").ap()
    w2_hbm = nc.dram_tensor("w2T", [DH, D], F32, kind="ExternalInput").ap()
    cvals = _consts()
    C_hbm = {k: nc.dram_tensor("h_" + k, list(v.shape),
                               BF16 if v.dtype == ml_dtypes.bfloat16 else F32,
                               kind="ExternalInput").ap()
             for k, v in cvals.items()}
    out_hbm = nc.dram_tensor("out", [T, D], F32, kind="ExternalOutput").ap()
    ik = "Internal" if stage >= 99 else "ExternalOutput"
    xnew_hbm = nc.dram_tensor("s_xnew", [T, D], F32, kind=ik).ap()
    lamrt_hbm = nc.dram_tensor("s_lamrt", [16, 2], F32, kind="Internal").ap()
    dqrt_hbm = nc.dram_tensor("s_dqrt", [NT, 128], F32, kind="Internal").ap()
    s2rt_hbm = nc.dram_tensor("s_s2rt", [NT, 128], F32, kind="Internal").ap()
    w1q_hbm = nc.dram_tensor("s_w1q", [D, DH], BF16, kind=ik).ap()
    w2q_hbm = nc.dram_tensor("s_w2q", [DH, D], BF16, kind=ik).ap()
    y_hbm = nc.dram_tensor("s_y", [DH, T], BF16, kind=ik).ap()
    dbg_hbm = {}
    if stage < 99:
        for k, shape, dt in (
            ("lnxT", [128, NDC * T], BF16), ("k1f", [128, 4 * T], BF16),
            ("k2f", [128, 4 * T], BF16), ("vtk", [128, NT * 512], BF16),
            ("qf", [128, 4 * QG], BF16), ("ao", [128, 8 * T], BF16),
            ("e", [128, 16 * 1024], BF16), ("row", [128, 2048], F32),
            ("xq", [128, NDC * T], BF16), ("h3", [128, NHIDC * T], FP16),
        ):
            dbg_hbm[k] = nc.dram_tensor("d_" + k, shape, dt,
                                        kind="ExternalOutput").ap()
    st = contextlib.ExitStack()
    with st:
        P = Prog(nc, st)
        _body(nc, st, P, stage, locals())
        with nc.Block() as block:
            P.finalize(block)
    return nc


def _body(nc, st, P, stage, tn):
    x_hbm = tn["x_hbm"]; W_hbm = tn["W_hbm"]
    lam_hbm = tn["lam_hbm"]; C_hbm = tn["C_hbm"]; out_hbm = tn["out_hbm"]
    w1_hbm = tn["w1_hbm"]; w2_hbm = tn["w2_hbm"]
    convw_hbm = tn["convw_hbm"]; convb_hbm = tn["convb_hbm"]
    xnew_hbm = tn["xnew_hbm"]; lamrt_hbm = tn["lamrt_hbm"]
    w1q_hbm = tn["w1q_hbm"]; w2q_hbm = tn["w2q_hbm"]; y_hbm = tn["y_hbm"]
    dqrt_hbm = tn["dqrt_hbm"]; s2rt_hbm = tn["s2rt_hbm"]
    dbg = tn["dbg_hbm"]

    sb = lambda name, shape, dt, **kw: st.enter_context(
        nc.sbuf_tensor(name, shape, dt, **kw))
    T_, V_, S_, G_, Y_ = "T", "V", "S", "G", "Y"
    sT, sV, sS, sG, sY = (P.sem[k] for k in ("T", "V", "S", "G", "Y"))

    lnxT = sb("c_lnxT", [128, NDC, T], BF16, is_transpose_dest=True)
    dq_all = sb("c_dq", [128, NT], F32)
    dq2_all = sb("c_dq2", [128, NT], F32)
    ones_c = sb("c_ones", [128, 128], BF16)
    lamcol = sb("c_lamcol", [128, 1], F32)
    lamsc = sb("c_lamsc", [1, 32], F32)
    convw_sb = sb("c_convw", [128, 3, NHIDC], F32)
    convb_sb = sb("c_convb", [128, NHIDC], F32)
    colscr = sb("c_colscr", [128, 16], F32)
    wcols = sb("c_wcols", [128, 8], F32)
    colhist = None
    phBC = contextlib.ExitStack()
    sbc = lambda name, shape, dt, **kw: phBC.enter_context(
        nc.sbuf_tensor(name, shape, dt, **kw))
    maskEO = sbc("c_maskEO", [128, 2048], BF16)
    onehot3 = sbc("c_onehot3", [128, 256], BF16)
    onesf = sbc("c_onesf", [128, 256], F32)
    sel2 = sbc("c_sel2", [2, 128], BF16)
    aoT = sbc("c_aoT", [128, 2 * 4, T], BF16)

    phA = contextlib.ExitStack()
    xfull = phA.enter_context(nc.sbuf_tensor("a_xfull", [128, NT, D], F32))
    sq_scr = phA.enter_context(nc.sbuf_tensor("a_sqscr", [128, D], F32))
    ltile = [phA.enter_context(nc.sbuf_tensor(f"a_ltile{i}", [128, D], BF16))
             for i in range(2)]
    dCon = P.dsem("dCon")
    dXf = P.dsem("dXf")
    dLamW = P.dsem("dLamW")
    dLamR = P.dsem("dLamR")
    dTr = P.dsem("dTr")
    dDbg = P.dsem("dDbg")

    @P.on(V_)
    def _(e):
        e.memset(colscr[:], 0.0)
        e.memset(lamcol[:], 1.0)
        e.memset(wcols[:], 0.0)
        e.drain()
        e.nop().then_inc(sV.h, 1)
    ms_v = sV.bump()
    P.wait(Y_, sV, ms_v)

    @P.on(Y_)
    def _(e):
        e.dma_start(out=maskEO[:], in_=C_hbm["maskEO"]).then_inc(dCon.h, 16)
        e.dma_start(out=onehot3[:], in_=C_hbm["onehot3"]).then_inc(dCon.h, 16)
        e.dma_start(out=ones_c[:], in_=C_hbm["ones_c"]).then_inc(dCon.h, 16)
        e.dma_start(out=onesf[:], in_=C_hbm["onesf"]).then_inc(dCon.h, 16)
        e.dma_start(out=sel2[:], in_=C_hbm["sel2"]).then_inc(dCon.h, 16)
        e.dma_start(out=convw_sb[:], in_=convw_hbm).then_inc(dCon.h, 16)
        e.dma_start(out=convb_sb[:], in_=convb_hbm).then_inc(dCon.h, 16)
        e.dma_start(out=colscr[0:16, 0:1], in_=lam_hbm).then_inc(dCon.h, 16)
        e.dma_start(out=xfull[:],
                    in_=x_hbm.rearrange("(n p) m -> p n m", p=128)
                    ).then_inc(dXf.h, 16)
    consts_done = 16 * dCon.bump(8)
    xfull_done = 16 * dXf.bump()

    P.wait(S_, dCon, consts_done)
    @P.on(S_)
    def _(e):
        e.activation(out=colscr[:, 1:2], in_=colscr[:, 0:1], func=AF.Exp,
                     scale=-1.0).then_inc(sS.h, 1)
    v = sS.bump()
    P.wait(V_, sS, v)
    @P.on(V_)
    def _(e):
        e.tensor_scalar(out=colscr[:, 2:3], in0=colscr[:, 1:2],
                        scalar1=1.0, scalar2=1e-30, op0=ALU.add, op1=ALU.max)
        e.drain()
        e.reciprocal(out=lamcol[:], in_=colscr[:, 2:3]).then_inc(sV.h, 1)
    v = sV.bump()
    P.wait(S_, sV, v)
    @P.on(S_)
    def _(e):
        e.drain()
        e.activation(out=colscr[:, 3:4], in_=lamcol[:],
                     func=AF.Ln).then_inc(sS.h, 1)
    v = sS.bump()
    P.wait(V_, sS, v)
    @P.on(V_)
    def _(e):
        e.drain()
        e.tensor_scalar(out=colscr[:, 4:5], in0=lamcol[:], scalar1=-1.0,
                        scalar2=1.0, op0=ALU.mult, op1=ALU.add
                        ).then_inc(sV.h, 1)
    v = sV.bump()
    P.wait(Y_, sV, v)
    @P.on(Y_)
    def _(e):
        e.dma_start(out=lamrt_hbm, in_=colscr[0:16, 3:5]).then_inc(dLamW.h, 16)
    P.wait(Y_, dLamW, 16 * dLamW.bump())
    @P.on(Y_)
    def _(e):
        e.dma_start(out=lamsc[:], in_=lamrt_hbm.rearrange(
            "a b -> (a b)")[None, :]).then_inc(dLamR.h, 16)
    lam_rd2 = 16 * dLamR.bump()
    lam_done_v = sV.n

    ln_s = []
    ln_v = []
    P.wait(V_, dXf, xfull_done)
    P.wait(V_, dLamW, 16)
    P.wait(S_, dXf, xfull_done)
    for t in range(_NLN):
        xt = xfull[:, t, :]
        @P.on(V_)
        def _(e, xt=xt):
            e.drain()
            e.tensor_reduce(out=colscr[:, 0:1], in_=xt, axis=AX.X,
                            op=ALU.add).then_inc(sV.h, 1)
        v_red = sV.bump()
        if t > 0:
            P.wait(S_, sV, ln_v[t - 1])
        @P.on(S_)
        def _(e, xt=xt):
            e.drain()
            e.activation(out=sq_scr[:], in_=xt, func=AF.Square,
                         accum_out=colscr[:, 1:2]).then_inc(sS.h, 1)
        s_sq = sS.bump()
        P.wait(V_, sS, s_sq)
        @P.on(V_)
        def _(e):
            e.drain()
            e.tensor_scalar(out=colscr[:, 2:3], in0=colscr[:, 0:1],
                            scalar1=1.0 / D, scalar2=None, op0=ALU.mult)
            e.tensor_scalar(out=colscr[:, 3:4], in0=colscr[:, 1:2],
                            scalar1=1.0 / D, scalar2=EPS, op0=ALU.mult,
                            op1=ALU.add)
            e.drain()
            e.tensor_tensor(out=colscr[:, 4:5], in0=colscr[:, 2:3],
                            in1=colscr[:, 2:3], op=ALU.mult)
            e.drain()
            e.tensor_tensor(out=colscr[:, 5:6], in0=colscr[:, 3:4],
                            in1=colscr[:, 4:5], op=ALU.subtract
                            ).then_inc(sV.h, 1)
        v_var = sV.bump()
        P.wait(S_, sV, v_var)
        @P.on(S_)
        def _(e):
            e.drain()
            e.activation(out=colscr[:, 8:9], in_=colscr[:, 5:6], func=AF.Ln)
            e.drain()
            e.activation(out=colscr[:, 6:7], in_=colscr[:, 8:9], func=AF.Exp,
                         scale=-0.5).then_inc(sS.h, 1)
        s_rstd = sS.bump()
        P.wait(V_, sS, s_rstd)
        @P.on(V_)
        def _(e):
            e.drain()
            e.scalar_tensor_tensor(out=colscr[:, 7:8], in0=colscr[:, 2:3],
                                   scalar=-1.0, in1=colscr[:, 6:7],
                                   op0=ALU.mult, op1=ALU.mult).then_inc(sV.h, 1)
        v_nmr = sV.bump()
        ln_v.append(sV.n)
        P.wait(S_, sV, v_nmr)
        if t >= 2:
            P.wait(S_, dTr, 16 * (t - 1))
        @P.on(S_)
        def _(e, t=t, xt=xt):
            e.drain()
            e.activation(out=ltile[t % 2][:], in_=xt,
                         func=AF.Identity, scale=colscr[:, 6:7],
                         bias=colscr[:, 7:8]).then_inc(sS.h, 1)
        s_lnx = sS.bump()
        ln_s.append(s_lnx)
        if t == 0:
            P.wait(S_, dCon, consts_done)
            P.wait(S_, dLamR, lam_rd2)
        @P.on(S_)
        def _(e, t=t, w=s_lnx):
            e.wait_ge(sS.h, w)
            e.wait_ge(dTr.h, 16 * t)
            e.dma_start(out=lnxT[:, :, 128 * t:128 * (t + 1)],
                        in_=ltile[t % 2][:], transpose=True).then_inc(dTr.h, 16)
        dTr.bump()
    lnxT_done = 16 * dTr.n

    if "lnxT" in dbg:
        P.wait(Y_, dTr, lnxT_done)
        @P.on(Y_)
        def _(e, w=sV.n):
            e.wait_ge(sV.h, w)
            e.dma_start(out=dbg["lnxT"].rearrange("p (c m) -> p c m", c=NDC
                                                  )[:, :, 0:128 * _NLN],
                        in_=lnxT[:, :, 0:128 * _NLN]).then_inc(dDbg.h, 16)
        dDbg.bump(1)

    P.barrier()
    phA.close()
    if stage <= 1:
        _dev_out(nc, P, out_hbm)
        return

    phB = contextlib.ExitStack()
    ps_sc = phB.enter_context(nc.psum_tensor("b_sc", [128, 4, 512], F32))
    ps_ev = phB.enter_context(nc.psum_tensor("b_ev", [128, 2, 512], F32))
    ps_r = phB.enter_context(nc.psum_tensor("b_r", [128, 2, 512], F32))
    k1f = phB.enter_context(nc.sbuf_tensor("b_k1f", [128, 4, T], BF16))
    k2f = phB.enter_context(nc.sbuf_tensor("b_k2f", [128, 4, T], BF16))
    vtk = phB.enter_context(nc.sbuf_tensor("b_vtk", [128, NT, 512], BF16))
    Wk_sb = phB.enter_context(nc.sbuf_tensor("b_wk", [128, NDC, 512], BF16))
    Wq_sb = phB.enter_context(nc.sbuf_tensor("b_wq", [128, NDC, 512], BF16))
    qf = [phB.enter_context(nc.sbuf_tensor(f"b_qf{i}", [128, 4, QG], BF16))
          for i in range(2)]
    e_sb = [phB.enter_context(nc.sbuf_tensor("b_e0", [128, 16, 4, 256],
                                             BF16))] * 2
    tbuf = phB.enter_context(nc.sbuf_tensor("b_tbuf", [128, 1024], BF16))
    cbuf = phB.enter_context(nc.sbuf_tensor("b_cbuf", [128, 2, 1024], BF16))
    rcbuf = phB.enter_context(nc.sbuf_tensor("b_rcbuf", [128, 2, 1024], BF16))
    Gb = [phB.enter_context(nc.sbuf_tensor("b_Gb0", [128, 1024], BF16))] * 2
    Db = [phB.enter_context(nc.sbuf_tensor("b_Db0", [128, 512], BF16))] * 2
    srow = phB.enter_context(nc.sbuf_tensor("b_srow", [1, 4 * 512], F32))
    grow = phB.enter_context(nc.sbuf_tensor("b_grow", [1, 512], BF16))
    dinvrow = phB.enter_context(nc.sbuf_tensor("b_dinvr", [1, 512], BF16))
    dWh = P.dsem("dWh")

    if stage == 2 or _NQG < NQG or _NPAIR < 4:
        @P.on("V")
        def _(e):
            e.memset(aoT[:], 0.0)
            e.memset(e_sb[0][:], 0.0)
            e.drain()
            e.nop().then_inc(P.sem["V"].h, 1)
        P.sem["V"].bump()

    ctx = dict(
        P=P, nc=nc, lnxT=lnxT, k1f=k1f, k2f=k2f, vtk=vtk, Wk_sb=Wk_sb,
        Wq_sb=Wq_sb, qf=qf, e_sb=e_sb, aoT=aoT, tbuf=tbuf, cbuf=cbuf,
        rcbuf=rcbuf, Gb=Gb, Db=Db, srow=srow, grow=grow, dinvrow=dinvrow, lamsc=lamsc,
        ps_sc=ps_sc, ps_ev=ps_ev, ps_r=ps_r, W_hbm=W_hbm, dWh=dWh,
        maskEO=maskEO, onehot3=onehot3, ones_c=ones_c, dbg=dbg, stage=stage,
    )
    for H in range(2 if stage >= 3 else 1):
        _attn_half(ctx, H)
        if stage == 2:
            break

    if stage == 3 and "ao" in dbg:
        dD3 = P.dsem("dDbg3")
        @P.on("Y")
        def _(e, w=P.sem["V"].n):
            e.wait_ge(P.sem["V"].h, w)
            e.dma_start(out=dbg["ao"].rearrange("p (c m) -> p c m", c=8),
                        in_=aoT[:]).then_inc(dD3.h, 16)
        dD3.bump(1)

    P.barrier()
    phB.close()
    if stage <= 3:
        _dev_out(nc, P, out_hbm)
        return

    phC = contextlib.ExitStack()
    psC = phC.enter_context(nc.psum_tensor("c3_ps", [128, 4, 512], F32))
    WoT_sb = phC.enter_context(nc.sbuf_tensor("c3_wo", [128, NDC, D], BF16))
    xtl = [phC.enter_context(nc.sbuf_tensor(f"c3_xt{i}", [128, D], F32))
           for i in range(2)]
    xmidb = [phC.enter_context(nc.sbuf_tensor(f"c3_xm{i}", [128, D], F32))
             for i in range(2)]
    lx2b = [phC.enter_context(nc.sbuf_tensor(f"c3_lx{i}", [128, D], F32))
            for i in range(2)]
    sqscrC = phC.enter_context(nc.sbuf_tensor("c3_sq", [128, D], F32))
    qscrC = phC.enter_context(nc.sbuf_tensor("c3_qs", [128, D], F32))
    xqfull = phC.enter_context(nc.sbuf_tensor("c3_xqf", [128, NT, D], BF16))
    dWo = P.dsem("dWo")
    dXc = P.dsem("dXc")
    dXw = P.dsem("dXw")
    dTr2 = P.dsem("dTr2")

    @P.on(G_)
    def _(e):
        e.dma_start(out=WoT_sb[:], in_=W_hbm["Wo"].rearrange(
            "(c p) m -> p c m", p=128)).then_inc(dWo.h, 16)
    wo_ld = 16 * dWo.bump()
    P.wait(T_, dWo, wo_ld)

    v_xmid = []
    v_q = []
    s_lx = []
    for t in range(NT):
        if t >= 2:
            P.wait(Y_, sV, v_xmid[t - 2])
        P.wait(Y_, dXc, 16 * dXc.n)
        @P.on(Y_)
        def _(e, t=t):
            e.dma_start(out=xtl[t % 2][:],
                        in_=x_hbm[128 * t:128 * (t + 1), :]).then_inc(dXc.h, 16)
        xld = 16 * dXc.bump()
        if t >= 2:
            P.wait(T_, sV, v_xmid[t - 2])
        @P.on(T_)
        def _(e, t=t):
            for half in range(2):
                for c2 in range(NDC):
                    mm = e.matmul(psC[:, 2 * (t % 2) + half, :],
                                  aoT[:, c2, 128 * t:128 * (t + 1)],
                                  WoT_sb[:, c2, 512 * half:512 * (half + 1)],
                                  start=(c2 == 0), stop=(c2 == NDC - 1))
            mm.then_inc(sT.h, 1)
        t_mm = sT.bump()
        P.wait(V_, sT, t_mm)
        P.wait(V_, dXc, xld)
        if t >= 2:
            P.wait(V_, sS, s_lx[t - 2])
            P.wait(V_, dXw, 16 * (t - 1))
        @P.on(V_)
        def _(e, t=t):
            e.drain()
            e.tensor_tensor(
                out=xmidb[t % 2][:].rearrange("p (a m) -> p a m", a=2),
                in0=psC[:, 2 * (t % 2):2 * (t % 2) + 2, :],
                in1=xtl[t % 2][:].rearrange("p (a m) -> p a m", a=2),
                op=ALU.add).then_inc(sV.h, 1)
        v_xmid.append(sV.bump())
        P.wait(G_, sV, v_xmid[t])
        P.wait(G_, dXw, 16 * dXw.n)
        @P.on(G_)
        def _(e, t=t):
            e.dma_start(out=xnew_hbm[128 * t:128 * (t + 1), :],
                        in_=xmidb[t % 2][:]).then_inc(dXw.h, 16)
        dXw.bump()
        @P.on(V_)
        def _(e, t=t):
            e.drain()
            e.tensor_reduce(out=colscr[:, 0:1], in_=xmidb[t % 2][:], axis=AX.X,
                            op=ALU.add).then_inc(sV.h, 1)
        v_red = sV.bump()
        P.wait(S_, sV, v_red)
        @P.on(S_)
        def _(e, t=t):
            e.drain()
            e.activation(out=sqscrC[:], in_=xmidb[t % 2][:], func=AF.Square,
                         accum_out=colscr[:, 1:2]).then_inc(sS.h, 1)
        s_sq = sS.bump()
        P.wait(V_, sS, s_sq)
        @P.on(V_)
        def _(e):
            e.drain()
            e.tensor_scalar(out=colscr[:, 2:3], in0=colscr[:, 0:1],
                            scalar1=1.0 / D, scalar2=None, op0=ALU.mult)
            e.tensor_scalar(out=colscr[:, 3:4], in0=colscr[:, 1:2],
                            scalar1=1.0 / D, scalar2=EPS, op0=ALU.mult,
                            op1=ALU.add)
            e.drain()
            e.tensor_tensor(out=colscr[:, 4:5], in0=colscr[:, 2:3],
                            in1=colscr[:, 2:3], op=ALU.mult)
            e.drain()
            e.tensor_tensor(out=colscr[:, 5:6], in0=colscr[:, 3:4],
                            in1=colscr[:, 4:5], op=ALU.subtract
                            ).then_inc(sV.h, 1)
        v_var = sV.bump()
        P.wait(S_, sV, v_var)
        @P.on(S_)
        def _(e):
            e.drain()
            e.activation(out=colscr[:, 8:9], in_=colscr[:, 5:6], func=AF.Ln)
            e.drain()
            e.activation(out=colscr[:, 6:7], in_=colscr[:, 8:9], func=AF.Exp,
                         scale=-0.5).then_inc(sS.h, 1)
        s_rstd = sS.bump()
        P.wait(V_, sS, s_rstd)
        @P.on(V_)
        def _(e):
            e.drain()
            e.scalar_tensor_tensor(out=colscr[:, 7:8], in0=colscr[:, 2:3],
                                   scalar=-1.0, in1=colscr[:, 6:7],
                                   op0=ALU.mult, op1=ALU.mult).then_inc(sV.h, 1)
        v_nmr = sV.bump()
        P.wait(S_, sV, v_nmr)
        if t >= 2:
            P.wait(S_, sV, v_q[t - 2])
        @P.on(S_)
        def _(e, t=t):
            e.drain()
            e.activation(out=lx2b[t % 2][:], in_=xmidb[t % 2][:],
                         func=AF.Identity, scale=colscr[:, 6:7],
                         bias=colscr[:, 7:8]).then_inc(sS.h, 1)
        s_lx.append(sS.bump())
        P.wait(V_, sS, s_lx[t])
        @P.on(V_)
        def _(e, t=t):
            e.drain()
            e.tensor_reduce(out=colscr[:, 9:10], in_=lx2b[t % 2][:], axis=AX.X,
                            op=ALU.max, apply_absolute_value=True)
            e.drain()
            e.tensor_scalar(out=colscr[:, 10:11], in0=colscr[:, 9:10],
                            scalar1=1e-5, scalar2=None, op0=ALU.max)
            e.drain()
            e.reciprocal(out=colscr[:, 11:12], in_=colscr[:, 10:11])
            e.tensor_scalar(out=colscr[:, 13:14], in0=colscr[:, 10:11],
                            scalar1=1.0 / 127.0, scalar2=None, op0=ALU.mult)
            e.drain()
            e.tensor_scalar(out=colscr[:, 12:13], in0=colscr[:, 11:12],
                            scalar1=127.0, scalar2=None, op0=ALU.mult)
            e.tensor_copy(out=dq_all[:, t:t + 1], in_=colscr[:, 13:14])
            e.drain()
            e.tensor_scalar(out=qscrC[:], in0=lx2b[t % 2][:],
                            scalar1=colscr[:, 12:13], scalar2=None,
                            op0=ALU.mult)
            e.drain()
            e.tensor_scalar(out=xqfull[:, t, :], in0=qscrC[:],
                            scalar1=MAGIC, scalar2=MAGIC, op0=ALU.add,
                            op1=ALU.subtract).then_inc(sV.h, 1)
        v_q.append(sV.bump())

    P.wait(S_, sV, v_q[NT - 1])
    P.wait(S_, dXc, 16 * dXc.n)
    P.wait(S_, dXw, 16 * dXw.n)
    P.wait(S_, dWo, wo_ld)
    for t in range(NT):
        @P.on(S_)
        def _(e, t=t):
            e.wait_ge(dTr2.h, 16 * t)
            e.dma_start(out=lnxT[:, :, 128 * t:128 * (t + 1)],
                        in_=xqfull[:, t, :], transpose=True
                        ).then_inc(dTr2.h, 16)
        dTr2.bump()

    if stage == 4:
        dD4 = P.dsem("dDbg4")
        P.wait(Y_, dTr2, 16 * dTr2.n)
        @P.on(Y_)
        def _(e, w=sV.n):
            e.wait_ge(sV.h, w)
            e.dma_start(out=dbg["xq"].rearrange("p (c m) -> p c m", c=NDC),
                        in_=lnxT[:]).then_inc(dD4.h, 16)
            e.dma_start(out=dbg["row"][:, 0:NT], in_=dq_all[:]
                        ).then_inc(dD4.h, 16)
        dD4.bump(2)

    P.barrier()
    phC.close()
    phBC.close()
    if stage <= 4:
        _dev_out(nc, P, out_hbm)
        return

    phW = contextlib.ExitStack()
    psW = phW.enter_context(nc.psum_tensor("w_ps", [128, 1, 512], F32))
    wbuf = [phW.enter_context(nc.sbuf_tensor(f"w_wb{i}", [128, DH], F32))
            for i in range(2)]
    wqb = [phW.enter_context(nc.sbuf_tensor(f"w_wq{i}", [128, DH], BF16))
           for i in range(2)]
    qscrW = phW.enter_context(nc.sbuf_tensor("w_qs", [128, DH], F32))
    gscrW = phW.enter_context(nc.sbuf_tensor("w_gs", [128, DH], F32))
    wacc = phW.enter_context(nc.sbuf_tensor("w_acc", [128, 8], F32))
    wtb = phW.enter_context(nc.sbuf_tensor("w_tb", [128, 1], BF16))
    wtl = phW.enter_context(nc.sbuf_tensor("w_tl", [128, 1], BF16))
    dWL = P.dsem("dWL")
    dWS = P.dsem("dWS")

    w1_src = w1_hbm.rearrange("(c p) m -> p c m", p=128)
    w1_dst = w1q_hbm.rearrange("(c p) m -> p c m", p=128)
    w2_src = w2_hbm.rearrange("(c p) m -> p c m", p=128)
    w2_dst = w2q_hbm.rearrange("(c p) m -> p c m", p=128)

    for wi, (src, dst, swc, invc) in enumerate((
            (w1_src, w1_dst, 0, 1), (w2_src, w2_dst, 2, 3))):
        def chunk(ap, i, wi=wi):
            if wi == 0:
                return ap[:, i, :]
            return ap[:, 4 * i:4 * (i + 1), :]
        vshape = (lambda b: b[:]) if wi == 0 else (
            lambda b: b[:].rearrange("p (a m) -> p a m", a=4))
        P.wait(G_, sV, sV.n)
        v_r = []
        for i in range(8):
            if i >= 2:
                P.wait(G_, sV, v_r[i - 2])
            P.wait(G_, dWL, 16 * dWL.n)
            @P.on(G_)
            def _(e, i=i, src=src, chunk=chunk, vshape=vshape):
                e.dma_start(out=vshape(wbuf[i % 2]), in_=chunk(src, i)
                            ).then_inc(dWL.h, 16)
            ld = 16 * dWL.bump()
            P.wait(V_, dWL, ld)
            @P.on(V_)
            def _(e, i=i):
                e.drain()
                e.tensor_reduce(out=wacc[:, i:i + 1], in_=wbuf[i % 2][:],
                                axis=AX.X, op=ALU.add,
                                apply_absolute_value=True).then_inc(sV.h, 1)
            v_r.append(sV.bump())
        @P.on(V_)
        def _(e):
            e.drain()
            e.tensor_reduce(out=colscr[:, 0:1], in_=wacc[:], axis=AX.X,
                            op=ALU.add)
            e.drain()
            e.tensor_copy(out=wtb[:], in_=colscr[:, 0:1])
            e.drain()
            e.tensor_tensor(out=wtl[:], in0=colscr[:, 0:1], in1=wtb[:],
                            op=ALU.subtract).then_inc(sV.h, 1)
        v_tot = sV.bump()
        P.wait(T_, sV, v_tot)
        @P.on(T_)
        def _(e):
            e.matmul(psW[:, 0, 0:1], ones_c[:], wtb[:], start=True,
                     stop=False)
            e.matmul(psW[:, 0, 0:1], ones_c[:], wtl[:], start=False,
                     stop=True).then_inc(sT.h, 1)
        t_tot = sT.bump()
        P.wait(V_, sT, t_tot)
        @P.on(V_)
        def _(e, swc=swc, invc=invc):
            e.drain()
            e.tensor_scalar(out=wcols[:, invc:invc + 1], in0=psW[:, 0, 0:1],
                            scalar1=1.0 / (D * DH), scalar2=1e-5,
                            op0=ALU.mult, op1=ALU.max)
            e.drain()
            e.reciprocal(out=wcols[:, swc:swc + 1],
                         in_=wcols[:, invc:invc + 1]).then_inc(sV.h, 1)
        v_sw = sV.bump()
        g_q = []
        for i in range(8):
            P.wait(G_, sV, v_sw if i < 2 else 0)
            if i >= 2:
                P.wait(G_, sV, v_r2_list[i - 2])
            P.wait(G_, dWL, 16 * dWL.n)
            @P.on(G_)
            def _(e, i=i, src=src, chunk=chunk, vshape=vshape):
                e.dma_start(out=vshape(wbuf[i % 2]), in_=chunk(src, i)
                            ).then_inc(dWL.h, 16)
            ld = 16 * dWL.bump()
            P.wait(V_, dWL, ld)
            if i >= 2:
                P.wait(V_, dWS, 16 * (dWS.n - 1))
            @P.on(V_)
            def _(e, i=i, swc=swc):
                e.drain()
                e.tensor_scalar(out=qscrW[:], in0=wbuf[i % 2][:],
                                scalar1=wcols[:, swc:swc + 1], scalar2=MAGIC,
                                op0=ALU.mult, op1=ALU.add).then_inc(sV.h, 1)
            vq2 = sV.bump()
            if i == 0:
                v_r2_list = []
            v_r2_list.append(vq2)
            @P.on(V_)
            def _(e, i=i):
                e.drain()
                e.tensor_scalar(out=gscrW[:], in0=qscrW[:], scalar1=MAGIC,
                                scalar2=1.0, op0=ALU.subtract, op1=ALU.min)
                e.drain()
                e.tensor_scalar(out=wqb[i % 2][:], in0=gscrW[:], scalar1=-1.0,
                                scalar2=None, op0=ALU.max
                                ).then_inc(sV.h, 1)
            g_q.append(sV.bump())
            P.wait(Y_, sV, g_q[i])
            P.wait(Y_, dWS, 16 * dWS.n)
            @P.on(Y_)
            def _(e, i=i, dst=dst, chunk=chunk, vshape=vshape):
                e.dma_start(out=chunk(dst, i), in_=vshape(wqb[i % 2])
                            ).then_inc(dWS.h, 16)
            dWS.bump()

    if stage == 5:
        dD5 = P.dsem("dDbg5")
        P.wait(Y_, dWS, 16 * dWS.n)
        @P.on(Y_)
        def _(e, w=sV.n):
            e.wait_ge(sV.h, w)
            e.dma_start(out=dbg["row"][:, 16:24], in_=wcols[:]
                        ).then_inc(dD5.h, 16)
        dD5.bump(1)

    P.barrier()
    phW.close()
    if stage <= 5:
        _dev_out(nc, P, out_hbm)
        return

    phD = contextlib.ExitStack()
    psD = phD.enter_context(nc.psum_tensor("d_ps", [128, 8, 512], F32))
    h3 = phD.enter_context(nc.sbuf_tensor("ds_h3", [128, NHIDC, T], FP16))
    phDq = contextlib.ExitStack()
    dqbc = phDq.enter_context(nc.sbuf_tensor("d_dqbc", [128, T], F32))
    phDp = contextlib.ExitStack()
    dqrow = phDp.enter_context(nc.sbuf_tensor("d_dqr", [1, T], F32))
    dqhi = phDp.enter_context(nc.sbuf_tensor("d_dqh", [1, T], BF16))
    dqlo = phDp.enter_context(nc.sbuf_tensor("d_dql", [1, T], BF16))
    dDq = P.dsem("dDq")
    dW1q = P.dsem("dW1q")

    @P.on(Y_)
    def _(e):
        for c2 in range(NT):
            e.dma_start(out=dqrt_hbm[c2:c2 + 1, :].rearrange("a b -> b a"),
                        in_=dq_all[:, c2:c2 + 1]).then_inc(dDq.h, 16)
        e.wait_ge(dDq.h, 16 * NT)
        e.dma_start(out=dqrow[:], in_=dqrt_hbm.rearrange("a b -> (a b)"
                                                         )[None, :]
                    ).then_inc(dDq.h, 16)
    dDq.bump(NT + 1)
    P.wait(V_, dDq, 16 * (NT + 1))
    @P.on(V_)
    def _(e):
        e.drain()
        e.tensor_copy(out=dqhi[:], in_=dqrow[:])
        e.drain()
        e.tensor_tensor(out=dqlo[:], in0=dqrow[:], in1=dqhi[:],
                        op=ALU.subtract).then_inc(sV.h, 1)
    v_split = sV.bump()
    P.wait(T_, sV, v_split)
    @P.on(T_)
    def _(e):
        for s4 in range(4):
            e.matmul(psD[:, s4, :], ones_c[0:1, :],
                     dqhi[0:1, 512 * s4:512 * (s4 + 1)], start=True,
                     stop=False)
            mm = e.matmul(psD[:, s4, :], ones_c[0:1, :],
                          dqlo[0:1, 512 * s4:512 * (s4 + 1)], start=False,
                          stop=True)
        mm.then_inc(sT.h, 1)
    t_bc = sT.bump()
    P.wait(V_, sT, t_bc)
    @P.on(V_)
    def _(e):
        e.drain()
        e.tensor_copy(out=dqbc[:].rearrange("p (a m) -> p a m", a=4),
                      in_=psD[:, 0:4, :]).then_inc(sV.h, 1)
    v_dqbc = sV.bump()
    P.barrier()
    phDp.close()

    phD2 = contextlib.ExitStack()
    hbuf = phD2.enter_context(nc.sbuf_tensor("d_hb", [128, T + 2], F32))
    g0 = phD2.enter_context(nc.sbuf_tensor("d_g0", [128, T], F32))
    g1 = phD2.enter_context(nc.sbuf_tensor("d_g1", [128, T], F32))
    w1qsb = [phD2.enter_context(nc.sbuf_tensor(f"d_w1q{i}", [128, NDC, 128],
                                               BF16)) for i in range(2)]
    @P.on(V_)
    def _(e):
        e.memset(hbuf[:, 0:1], 0.0)
        e.memset(hbuf[:, T + 1:T + 2], 0.0)
        e.drain()
        e.nop().then_inc(sV.h, 1)
    v_init = sV.bump()
    v_evac = []
    s_silu = []
    for hc in range(NHIDC):
        if hc >= 2:
            P.wait(G_, sT, t_mm1[hc - 2])
        P.wait(G_, dW1q, 16 * dW1q.n)
        @P.on(G_)
        def _(e, hc=hc):
            e.dma_start(out=w1qsb[hc % 2][:],
                        in_=w1q_hbm.rearrange("(c p) m -> p c m", p=128
                                              )[:, :, 128 * hc:128 * (hc + 1)]
                        ).then_inc(dW1q.h, 16)
        w1ld = 16 * dW1q.bump()
        P.wait(T_, dW1q, w1ld)
        P.wait(T_, sV, v_dqbc if hc == 0 else
               (v_evac[hc - 2] if hc >= 2 else 0))
        @P.on(T_)
        def _(e, hc=hc):
            for ts4 in range(4):
                for dc in range(NDC):
                    mm = e.matmul(psD[:, 4 * (hc % 2) + ts4, :],
                                  w1qsb[hc % 2][:, dc, :],
                                  lnxT[:, dc, 512 * ts4:512 * (ts4 + 1)],
                                  start=(dc == 0), stop=(dc == NDC - 1))
            mm.then_inc(sT.h, 1)
        if hc == 0:
            t_mm1 = []
        t_mm1.append(sT.bump())
        P.wait(V_, sT, t_mm1[hc])
        if hc >= 1:
            P.wait(V_, sS, s_silu[hc - 1])
        @P.on(V_)
        def _(e, hc=hc):
            e.drain()
            e.tensor_tensor(
                out=hbuf[:, 1:T + 1].rearrange("p (a m) -> p a m", a=4),
                in0=psD[:, 4 * (hc % 2):4 * (hc % 2) + 4, :],
                in1=dqbc[:].rearrange("p (a m) -> p a m", a=4), op=ALU.mult)
            e.drain()
            e.tensor_scalar(out=g1[:], in0=hbuf[:, 1:T + 1],
                            scalar1=convw_sb[:, 1, hc:hc + 1], scalar2=None,
                            op0=ALU.mult)
            e.drain()
            e.scalar_tensor_tensor(out=g0[:], in0=hbuf[:, 0:T],
                                   scalar=convw_sb[:, 0, hc:hc + 1],
                                   in1=g1[:], op0=ALU.mult, op1=ALU.add)
            e.drain()
            e.scalar_tensor_tensor(out=g1[:], in0=hbuf[:, 2:T + 2],
                                   scalar=convw_sb[:, 2, hc:hc + 1],
                                   in1=g0[:], op0=ALU.mult, op1=ALU.add
                                   ).then_inc(sV.h, 1)
        v_evac.append(sV.bump())
        P.wait(S_, sV, v_evac[hc])
        @P.on(S_)
        def _(e, hc=hc):
            e.drain()
            e.activation(out=h3[:, hc, :], in_=g1[:], func=AF.Silu,
                         scale=wcols[:, 1:2], bias=convb_sb[:, hc:hc + 1]
                         ).then_inc(sS.h, 1)
        s_silu.append(sS.bump())

    P.barrier()
    phD2.close()
    phDq.close()

    phD3a = contextlib.ExitStack()
    s2bc = phD3a.enter_context(nc.sbuf_tensor("d_s2bc", [128, T], F32))
    s2all = phD3a.enter_context(nc.sbuf_tensor("d_s2a", [128, NT], F32))
    phD3b = contextlib.ExitStack()
    mxp = [phD3b.enter_context(nc.sbuf_tensor(f"d_mx{i}", [128, T], FP16))
           for i in range(2)]
    mnp = [phD3b.enter_context(nc.sbuf_tensor(f"d_mn{i}", [128, T], FP16))
           for i in range(2)]
    mxT = phD3b.enter_context(nc.sbuf_tensor("d_mxT", [128, NT, 128], FP16,
                                             is_transpose_dest=True))
    s2row = phD3b.enter_context(nc.sbuf_tensor("d_s2r", [1, T], F32))
    s2hi = phD3b.enter_context(nc.sbuf_tensor("d_s2h", [1, T], BF16))
    s2lo = phD3b.enter_context(nc.sbuf_tensor("d_s2l", [1, T], BF16))
    dS2 = P.dsem("dS2")
    dYw = P.dsem("dYw")

    @P.on(V_)
    def _(e):
        e.memset(mxp[0][:], 0.0)
        e.memset(mnp[0][:], 0.0)
        e.drain()
        for hc in range(NHIDC):
            e.tensor_tensor(out=mxp[(hc + 1) % 2][:], in0=mxp[hc % 2][:],
                            in1=h3[:, hc, :], op=ALU.max)
            e.tensor_tensor(out=mnp[(hc + 1) % 2][:], in0=mnp[hc % 2][:],
                            in1=h3[:, hc, :], op=ALU.min)
            e.drain()
        e.nop().then_inc(sV.h, 1)
    v_mn = sV.bump()
    @P.on(V_)
    def _(e):
        e.tensor_scalar(out=mnp[1][:], in0=mnp[0][:], scalar1=-1.0,
                        scalar2=None, op0=ALU.mult)
        e.drain()
        e.tensor_tensor(out=mxp[1][:], in0=mxp[0][:], in1=mnp[1][:],
                        op=ALU.max).then_inc(sV.h, 1)
    v_comb = sV.bump()
    P.wait(S_, sV, v_comb)
    @P.on(S_)
    def _(e):
        e.dma_start(out=mxT[:], in_=mxp[1][:], transpose=True
                    ).then_inc(dS2.h, 16)
    P.wait(V_, dS2, 16 * dS2.bump())
    @P.on(V_)
    def _(e):
        e.drain()
        e.tensor_reduce(out=colscr[:].rearrange("p (c u) -> p c u", u=1),
                        in_=mxT[:], axis=AX.X, op=ALU.max)
        e.drain()
        e.tensor_scalar(out=dq2_all[:], in0=colscr[:], scalar1=1e-5,
                        scalar2=1.0 / 127.0, op0=ALU.max, op1=ALU.mult)
        e.tensor_scalar(out=s2all[:], in0=colscr[:], scalar1=1e-5,
                        scalar2=None, op0=ALU.max)
        e.drain()
        e.reciprocal(out=s2all[:], in_=s2all[:])
        e.drain()
        e.tensor_scalar(out=s2all[:], in0=s2all[:], scalar1=127.0,
                        scalar2=None, op0=ALU.mult).then_inc(sV.h, 1)
    v_s2 = sV.bump()
    P.wait(Y_, sV, v_s2)
    @P.on(Y_)
    def _(e):
        for c2 in range(NT):
            e.dma_start(out=s2rt_hbm[c2:c2 + 1, :].rearrange("a b -> b a"),
                        in_=s2all[:, c2:c2 + 1]).then_inc(dS2.h, 16)
        e.wait_ge(dS2.h, 16 * (NT + 1))
        e.dma_start(out=s2row[:], in_=s2rt_hbm.rearrange("a b -> (a b)"
                                                         )[None, :]
                    ).then_inc(dS2.h, 16)
    dS2.bump(NT + 1)
    P.wait(V_, dS2, 16 * (NT + 2))
    @P.on(V_)
    def _(e):
        e.drain()
        e.tensor_copy(out=s2hi[:], in_=s2row[:])
        e.drain()
        e.tensor_tensor(out=s2lo[:], in0=s2row[:], in1=s2hi[:],
                        op=ALU.subtract).then_inc(sV.h, 1)
    v_s2s = sV.bump()
    P.wait(T_, sV, v_s2s)
    @P.on(T_)
    def _(e):
        for s4 in range(4):
            e.matmul(psD[:, s4, :], ones_c[0:1, :],
                     s2hi[0:1, 512 * s4:512 * (s4 + 1)], start=True,
                     stop=False)
            mm = e.matmul(psD[:, s4, :], ones_c[0:1, :],
                          s2lo[0:1, 512 * s4:512 * (s4 + 1)], start=False,
                          stop=True)
        mm.then_inc(sT.h, 1)
    t_s2bc = sT.bump()
    P.wait(V_, sT, t_s2bc)
    @P.on(V_)
    def _(e):
        e.drain()
        e.tensor_copy(out=s2bc[:].rearrange("p (a m) -> p a m", a=4),
                      in_=psD[:, 0:4, :]).then_inc(sV.h, 1)
    v_s2bc = sV.bump()
    P.barrier()
    phD3b.close()

    phD3c = contextlib.ExitStack()
    qscrD = phD3c.enter_context(nc.sbuf_tensor("d_qsD", [128, T], F32))
    hqrow = [phD3c.enter_context(nc.sbuf_tensor(f"d_hq{i}", [128, T], BF16))
             for i in range(2)]
    g_hq = []
    for hc in range(NHIDC):
        if hc >= 2:
            P.wait(V_, dYw, 16 * (hc - 1))
        @P.on(V_)
        def _(e, hc=hc):
            e.drain()
            e.tensor_tensor(out=qscrD[:], in0=h3[:, hc, :], in1=s2bc[:],
                            op=ALU.mult)
            e.drain()
            e.tensor_scalar(out=hqrow[hc % 2][:], in0=qscrD[:], scalar1=MAGIC,
                            scalar2=MAGIC, op0=ALU.add, op1=ALU.subtract
                            ).then_inc(sV.h, 1)
        g_hq.append(sV.bump())
        P.wait(Y_, sV, g_hq[hc])
        P.wait(Y_, dYw, 16 * dYw.n)
        @P.on(Y_)
        def _(e, hc=hc):
            e.dma_start(out=y_hbm.rearrange("(c p) m -> p c m", p=128
                                            )[:, hc, :],
                        in_=hqrow[hc % 2][:]).then_inc(dYw.h, 16)
        dYw.bump()

    if stage == 6:
        dD6 = P.dsem("dDbg6")
        P.wait(Y_, dYw, 16 * dYw.n)
        @P.on(Y_)
        def _(e, w=sV.n):
            e.wait_ge(sV.h, w)
            e.dma_start(out=dbg["h3"].rearrange("p (c m) -> p c m", c=NHIDC),
                        in_=h3[:]).then_inc(dD6.h, 16)
            e.dma_start(out=dbg["row"][:, 32:48], in_=dq2_all[:]
                        ).then_inc(dD6.h, 16)
        dD6.bump(2)

    P.barrier()
    phD3c.close()
    phD3a.close()
    phD.close()
    if stage <= 6:
        _dev_out(nc, P, out_hbm)
        return

    phE = contextlib.ExitStack()
    psE = phE.enter_context(nc.psum_tensor("e_ps", [128, 4, 512], F32))
    w2qsb = phE.enter_context(nc.sbuf_tensor("e_w2q", [128, NHIDC, D], BF16))
    hqt = [phE.enter_context(nc.sbuf_tensor(f"e_hq{i}", [128, NHIDC, 128],
                                            BF16)) for i in range(2)]
    xmt = [phE.enter_context(nc.sbuf_tensor(f"e_xm{i}", [128, D], F32))
           for i in range(2)]
    outb = [phE.enter_context(nc.sbuf_tensor(f"e_ob{i}", [128, D], F32))
            for i in range(2)]
    dq2w = phE.enter_context(nc.sbuf_tensor("e_dq2w", [128, NT], F32))
    dW2q = P.dsem("dW2q")
    dHq = P.dsem("dHq")
    dXm = P.dsem("dXm")
    dOut = P.dsem("dOut")

    @P.on(G_)
    def _(e):
        e.dma_start(out=w2qsb[:], in_=w2q_hbm.rearrange("(c p) m -> p c m",
                                                        p=128)
                    ).then_inc(dW2q.h, 16)
    w2ld = 16 * dW2q.bump()
    @P.on(V_)
    def _(e):
        e.drain()
        e.tensor_scalar(out=dq2w[:], in0=dq2_all[:], scalar1=wcols[:, 3:4],
                        scalar2=None, op0=ALU.mult).then_inc(sV.h, 1)
    v_dq2w = sV.bump()
    P.wait(T_, dW2q, w2ld)

    y_src = y_hbm.rearrange("(c p) (n m) -> p c n m", p=128, m=128)
    v_out = []
    for t in range(NT):
        if t >= 2:
            P.wait(Y_, sT, t_mm2[t - 2])
            P.wait(Y_, sV, v_out[t - 2])
        P.wait(Y_, dHq, 16 * dHq.n)
        P.wait(Y_, dXm, 16 * dXm.n)
        @P.on(Y_)
        def _(e, t=t):
            e.dma_start(out=hqt[t % 2][:], in_=y_src[:, :, t, :]
                        ).then_inc(dHq.h, 16)
            e.dma_start(out=xmt[t % 2][:],
                        in_=xnew_hbm[128 * t:128 * (t + 1), :]
                        ).then_inc(dXm.h, 16)
        ld_hq = 16 * dHq.bump()
        ld_xm = 16 * dXm.bump()
        P.wait(T_, dHq, ld_hq)
        if t >= 2:
            P.wait(T_, sV, v_out[t - 2])
        @P.on(T_)
        def _(e, t=t):
            for half in range(2):
                for hc in range(NHIDC):
                    mm = e.matmul(psE[:, 2 * (t % 2) + half, :],
                                  hqt[t % 2][:, hc, :],
                                  w2qsb[:, hc, 512 * half:512 * (half + 1)],
                                  start=(hc == 0), stop=(hc == NHIDC - 1))
            mm.then_inc(sT.h, 1)
        if t == 0:
            t_mm2 = []
        t_mm2.append(sT.bump())
        P.wait(V_, sT, t_mm2[t])
        P.wait(V_, dXm, ld_xm)
        if t >= 2:
            P.wait(V_, dOut, 16 * (t - 1))
        @P.on(V_)
        def _(e, t=t):
            e.drain()
            e.scalar_tensor_tensor(
                out=outb[t % 2][:].rearrange("p (a m) -> p a m", a=2),
                in0=psE[:, 2 * (t % 2):2 * (t % 2) + 2, :],
                scalar=dq2w[:, t:t + 1],
                in1=xmt[t % 2][:].rearrange("p (a m) -> p a m", a=2),
                op0=ALU.mult, op1=ALU.add).then_inc(sV.h, 1)
        v_out.append(sV.bump())
        P.wait(G_, sV, v_out[t])
        P.wait(G_, dOut, 16 * dOut.n)
        @P.on(G_)
        def _(e, t=t):
            e.dma_start(out=out_hbm[128 * t:128 * (t + 1), :],
                        in_=outb[t % 2][:]).then_inc(dOut.h, 16)
        dOut.bump()
    P.wait(G_, dOut, 16 * NT)
    P.wait(Y_, dOut, 16 * NT)
    phE.close()


def _attn_half(c, H):
    P = c["P"]
    sT, sV, sS, sG = (P.sem[k] for k in ("T", "V", "S", "G"))
    T_, V_, S_, G_, Y_ = "T", "V", "S", "G", "Y"
    lnxT = c["lnxT"]; k1f = c["k1f"]; k2f = c["k2f"]; vtk = c["vtk"]
    Wk_sb = c["Wk_sb"]; Wq_sb = c["Wq_sb"]; qf = c["qf"]; e_sb = c["e_sb"]
    aoT = c["aoT"]; tbuf = c["tbuf"]; cbuf = c["cbuf"]; rcbuf = c["rcbuf"]
    Gb = c["Gb"]; Db = c["Db"]; srow = c["srow"]; grow = c["grow"]
    dinvrow = c["dinvrow"]; lamsc = c["lamsc"]
    ps_sc = c["ps_sc"]; ps_ev = c["ps_ev"]
    ps_r = c["ps_r"]; W_hbm = c["W_hbm"]; dWh = c["dWh"]
    maskEO = c["maskEO"]; onehot3 = c["onehot3"]; ones_c = c["ones_c"]
    dbg = c["dbg"]

    cols = slice(512 * H, 512 * H + 512)
    dWq = P.dsem(f"dWq{H}")
    entry_mm = P.sem["T"].n

    def warm(n, ev=False):
        @P.on(T_)
        def _(e, n=n, ev=ev):
            for _ in range(n):
                if ev:
                    e.matmul(ps_ev[0:64, 1, 0:448], ones_c[0:64, 0:64],
                             maskEO[0:64, 0:448], start=True, stop=True,
                             tile_position=(0, 0))
                else:
                    e.matmul(ps_sc[0:64, 0, 0:448], ones_c[0:64, 0:64],
                             maskEO[0:64, 0:448], start=True, stop=True,
                             tile_position=(0, 0))

    P.wait(G_, sT, entry_mm)
    @P.on(G_)
    def _(e):
        e.dma_start(out=Wq_sb[:],
                    in_=W_hbm["Wq"].rearrange("(c p) m -> p c m", p=128
                                              )[:, :, cols]).then_inc(dWq.h, 16)
    wq_ld = 16 * dWq.bump()

    def proj(name, which, prev_mm):
        if prev_mm:
            P.wait(G_, sT, prev_mm)
        @P.on(G_)
        def _(e, name=name):
            e.dma_start(out=Wk_sb[:],
                        in_=W_hbm[name].rearrange("(c p) m -> p c m", p=128
                                                  )[:, :, cols]
                        ).then_inc(dWh.h, 16)
        ld = 16 * dWh.bump()
        P.wait(T_, dWh, ld)
        evs = []
        if which in ("k1", "k2"):
            dst = k1f if which == "k1" else k2f
            for tg in range(4):
                for dc in range(4):
                    if len(evs) >= 4:
                        P.wait(T_, sV, evs[-4])
                    @P.on(T_)
                    def _(e, dc=dc, tg=tg):
                        for dinc in range(NDC):
                            mm = e.matmul(
                                ps_sc[:, dc, 0:512],
                                Wk_sb[:, dinc, 128 * dc:128 * (dc + 1)],
                                lnxT[:, dinc, 512 * tg:512 * (tg + 1)],
                                start=(dinc == 0), stop=(dinc == NDC - 1))
                        mm.then_inc(sT.h, 1)
                    mm = sT.bump()
                    @P.on(V_)
                    def _(e, dc=dc, tg=tg, mm=mm, dst=dst):
                        e.wait_ge(sT.h, mm)
                        e.drain()
                        e.tensor_copy(out=dst[:, dc, 512 * tg:512 * (tg + 1)],
                                      in_=ps_sc[:, dc, 0:512]).then_inc(sV.h, 1)
                    evs.append(sV.bump())
            P.wait(T_, sV, evs[-1])
        else:
            for tokt in range(NT):
                if len(evs) >= 4:
                    P.wait(T_, sV, evs[-4])
                @P.on(T_)
                def _(e, tokt=tokt):
                    for dinc in range(NDC):
                        mm = e.matmul(ps_sc[:, tokt % 4, 0:512],
                                      lnxT[:, dinc, 128 * tokt:128 * (tokt + 1)],
                                      Wk_sb[:, dinc, 0:512],
                                      start=(dinc == 0), stop=(dinc == NDC - 1))
                    mm.then_inc(sT.h, 1)
                mm = sT.bump()
                @P.on(V_)
                def _(e, tokt=tokt, mm=mm):
                    e.wait_ge(sT.h, mm)
                    e.drain()
                    e.tensor_copy(out=vtk[:, tokt, :],
                                  in_=ps_sc[:, tokt % 4, 0:512]).then_inc(sV.h, 1)
                evs.append(sV.bump())
            P.wait(T_, sV, evs[-1])
        return evs[-1], sT.n

    pk1, m1 = proj("Wk1", "k1", entry_mm)
    pk2, m2 = proj("Wk2", "k2", m1)
    pv, m3 = proj("Wv", "v", m2)
    P.wait(T_, dWq, wq_ld)

    if c["stage"] == 2 and H == 0 and "k1f" in dbg:
        dD = P.dsem(f"dDbgB{H}")
        @P.on(Y_)
        def _(e, w=pv):
            e.wait_ge(sV.h, w)
            e.dma_start(out=dbg["k1f"].rearrange("p (c m) -> p c m", c=4),
                        in_=k1f[:]).then_inc(dD.h, 16)
            e.dma_start(out=dbg["k2f"].rearrange("p (c m) -> p c m", c=4),
                        in_=k2f[:]).then_inc(dD.h, 16)
            e.dma_start(out=dbg["vtk"].rearrange("p (c m) -> p c m", c=NT),
                        in_=vtk[:]).then_inc(dD.h, 16)
        dD.bump(3)

    def qproj_T(qg2):
        qs2 = slice(QG * qg2, QG * (qg2 + 1))
        @P.on(T_)
        def _(e, qs2=qs2):
            for dc in range(4):
                for dinc in range(NDC):
                    mm = e.matmul(ps_sc[:, dc, 0:256],
                                  Wq_sb[:, dinc, 128 * dc:128 * (dc + 1)],
                                  lnxT[:, dinc, qs2],
                                  start=(dinc == 0), stop=(dinc == NDC - 1))
            mm.then_inc(sT.h, 1)
        return sT.bump()

    def qproj_S(qmm2, qg2):
        P.wait(S_, sT, qmm2)
        @P.on(S_)
        def _(e, qbuf=qf[qg2 % 2]):
            e.drain()
            for dc in range(4):
                act = e.activation(out=qbuf[:, dc, :], in_=ps_sc[:, dc, 0:256],
                                   func=AF.Identity, scale=SCALE)
            act.then_inc(sS.h, 1)
        return sS.bump()

    qev_pending = qproj_S(qproj_T(0), 0)
    for qg in range(_NQG):
        qs = slice(QG * qg, QG * (qg + 1))
        qbuf = qf[qg % 2]
        qev = qev_pending
        P.wait(T_, sS, qev)

        nkc = 2 * qg + 2
        for p in range(_NPAIR):
            par = p % 2
            ebuf = e_sb[p % 2]
            gbuf = Gb[par]
            dbuf = Db[par]
            exp_s = []
            for kc2 in range(0, nkc, 2):
                for kc in (kc2, kc2 + 1):
                    b = 2 * (kc % 2)
                    ks = slice(128 * kc, 128 * (kc + 1))
                    @P.on(T_)
                    def _(e, b=b, ks=ks, p=p, qs=qs, qbuf=qbuf):
                        e.matmul(ps_sc[:, b, 0:256], k1f[0:64, p, ks],
                                 qbuf[0:64, p, :], start=True, stop=True,
                                 tile_position=(0, 0))
                        e.matmul(ps_sc[:, b + 1, 0:256], k1f[64:128, p, ks],
                                 qbuf[64:128, p, :], start=True, stop=True,
                                 tile_position=(64, 0))
                        e.matmul(ps_sc[:, b, 256:512], k2f[0:64, p, ks],
                                 qbuf[0:64, p, :], start=True, stop=True,
                                 tile_position=(0, 0))
                        e.matmul(ps_sc[:, b + 1, 256:512], k2f[64:128, p, ks],
                                 qbuf[64:128, p, :], start=True, stop=True,
                                 tile_position=(64, 0)).then_inc(sT.h, 1)
                    sT.bump()
                P.wait(S_, sT, sT.n)
                @P.on(S_)
                def _(e, kc2=kc2, ebuf=ebuf):
                    e.drain()
                    e.activation(out=ebuf[:, kc2:kc2 + 2, :, :],
                                 in_=ps_sc[:], func=AF.Exp).then_inc(sS.h, 1)
                ex = sS.bump()
                exp_s.append(ex)
                if kc2 + 2 >= nkc:
                    P.wait(V_, sS, ex)
                    @P.on(V_)
                    def _(e, kc2=kc2, ebuf=ebuf):
                        e.drain()
                        e.tensor_tensor(
                            out=ebuf[:, kc2:kc2 + 2, :, :].rearrange(
                                "p a b m -> p (a b m)"),
                            in0=ebuf[:, kc2:kc2 + 2, :, :].rearrange(
                                "p a b m -> p (a b m)"),
                            in1=maskEO[:], op=ALU.mult).then_inc(sV.h, 1)
                    sV.bump()
                    P.wait(T_, sV, sV.n)
                else:
                    P.wait(T_, sS, ex)
                @P.on(T_)
                def _(e, kc2=kc2, p=p, ebuf=ebuf, first=(kc2 == 0),
                      last=(kc2 + 2 >= nkc)):
                    for i, kc in enumerate((kc2, kc2 + 1)):
                        st = first and i == 0
                        sp = last and i == 1
                        e.matmul(ps_r[0:16, 0, 0:512],
                                 onehot3[:, 0:16],
                                 ebuf[:, kc, 0:2, :], start=st, stop=sp,
                                 tile_position=(0, 0))
                        mm = e.matmul(ps_r[0:16, 1, 0:512],
                                      onehot3[:, 0:16],
                                      ebuf[:, kc, 2:4, :], start=st, stop=sp,
                                      tile_position=(0, 0))
                    mm.then_inc(sT.h, 1)
                sT.bump()
            r_done = sT.n

            P.wait(S_, sT, r_done)
            @P.on(S_)
            def _(e, p=p):
                e.drain()
                e.activation(out=srow[0:1, 0:512], in_=ps_r[0:1, 0, 0:512],
                             func=AF.Ln)
                e.activation(out=srow[0:1, 512:1024], in_=ps_r[0:1, 1, 0:512],
                             func=AF.Ln).then_inc(sS.h, 1)
            ln_r = sS.bump()
            P.wait(V_, sS, ln_r)
            @P.on(V_)
            def _(e, p=p, H=H):
                e.drain()
                e.tensor_tensor(out=srow[0:1, 1024:1280], in0=srow[0:1, 0:256],
                                in1=srow[0:1, 256:512], op=ALU.subtract)
                e.tensor_tensor(out=srow[0:1, 1280:1536], in0=srow[0:1, 512:768],
                                in1=srow[0:1, 768:1024], op=ALU.subtract)
                hA = 8 * H + 2 * p
                e.drain()
                e.tensor_scalar(out=srow[0:1, 0:256],
                                in0=ps_r[0:1, 0, 0:256],
                                scalar1=lamsc[0:1, 2 * hA + 1:2 * hA + 2],
                                scalar2=None, op0=ALU.mult)
                e.tensor_scalar(out=srow[0:1, 256:512],
                                in0=ps_r[0:1, 1, 0:256],
                                scalar1=lamsc[0:1, 2 * hA + 3:2 * hA + 4],
                                scalar2=None, op0=ALU.mult)
                e.tensor_scalar(
                    out=srow[0:1, 512:1024].rearrange("p (a m) -> p a m", a=2),
                    in0=ps_r[0:1, 0:2, 0:256],
                    scalar1=1e-6, scalar2=None, op0=ALU.mult
                    ).then_inc(sV.h, 1)
            v_t2 = sV.bump()
            P.wait(S_, sV, v_t2)
            @P.on(S_)
            def _(e, p=p, H=H):
                hA = 8 * H + 2 * p
                e.drain()
                e.activation(out=grow[0:1, 0:256], in_=srow[0:1, 1024:1280],
                             func=AF.Exp, bias=lamsc[0:1, 2 * hA:2 * hA + 1])
                e.activation(out=grow[0:1, 256:512], in_=srow[0:1, 1280:1536],
                             func=AF.Exp,
                             bias=lamsc[0:1, 2 * hA + 2:2 * hA + 3]
                             ).then_inc(sS.h, 1)
            s_g = sS.bump()
            P.wait(T_, sS, s_g)
            @P.on(T_)
            def _(e):
                e.matmul(ps_r[:, 1, :], ones_c[0:1, :], grow[0:1, :],
                         start=True, stop=True).then_inc(sT.h, 1)
            t_gb = sT.bump()
            P.wait(V_, sT, t_gb)
            @P.on(V_)
            def _(e, gbuf=gbuf):
                e.drain()
                e.tensor_copy(out=gbuf[:, 0:512], in_=ps_r[:, 1, :])
                e.tensor_copy(out=gbuf[:, 512:1024], in_=ps_r[:, 1, :]
                              ).then_inc(sV.h, 1)
            v_gb = sV.bump()

            P.wait(V_, sS, exp_s[-1])
            ev_mms = []
            for g in range(nkc // 2):
                par2 = g % 2
                if g >= 2:
                    P.wait(V_, sT, ev_mms[g - 2])
                @P.on(V_)
                def _(e, g=g, par2=par2, ebuf=ebuf, gbuf=gbuf, w=v_gb):
                    e.wait_ge(sV.h, w)
                    e.drain()
                    e.tensor_tensor(
                        out=tbuf[:].rearrange("p (a b m) -> p a b m",
                                              a=2, b=2),
                        in0=ebuf[:, 2 * g:2 * g + 2, 1:4:2, :],
                        in1=gbuf[:].rearrange("p (a b m) -> p a b m",
                                              a=2, b=2),
                        op=ALU.mult)
                    e.drain()
                    e.tensor_tensor(
                        out=cbuf[:, par2, :].rearrange(
                            "p (a b m) -> p a b m", a=2, b=2),
                        in0=ebuf[:, 2 * g:2 * g + 2, 0:4:2, :],
                        in1=tbuf[:].rearrange("p (a b m) -> p a b m",
                                              a=2, b=2),
                        op=ALU.subtract)
                    e.drain()
                    e.tensor_scalar(out=rcbuf[:, par2, :],
                                    in0=cbuf[:, par2, :], scalar1=0.0,
                                    scalar2=None, op0=ALU.max).then_inc(sV.h, 1)
                v_c = sV.bump()
                P.wait(T_, sV, v_c)
                @P.on(T_)
                def _(e, g=g, par2=par2, p=p, w=v_t2, stg=(g == 0),
                      spg=(g == nkc // 2 - 1)):
                    e.wait_ge(sV.h, w)
                    for j in range(2):
                        kc = 2 * g + j
                        st = stg and j == 0
                        sp = spg and j == 1
                        e.matmul(ps_ev[0:64, 0, 0:256],
                                 vtk[:, kc, 128 * p:128 * p + 64],
                                 cbuf[:, par2, 512 * j:512 * j + 256],
                                 start=st, stop=sp, tile_position=(0, 0))
                        e.matmul(ps_ev[64:128, 1, 0:256],
                                 vtk[:, kc, 128 * p + 64:128 * (p + 1)],
                                 cbuf[:, par2, 512 * j + 256:512 * (j + 1)],
                                 start=st, stop=sp, tile_position=(0, 64))
                        mm = e.matmul(ps_r[0:16, 0, 0:512],
                                      onehot3[:, 0:16],
                                      rcbuf[:, par2, 512 * j:512 * (j + 1)],
                                      start=st, stop=sp, tile_position=(0, 0))
                    mm.then_inc(sT.h, 1)
                ev_mms.append(sT.bump())
            ev_done = sT.n
            if p == _NPAIR - 1 and qg + 1 < _NQG:
                P.wait(T_, sS, exp_s[-1])
                qmm_next = qproj_T(qg + 1)
            P.wait(V_, sT, ev_done)

            @P.on(V_)
            def _(e, p=p):
                e.drain()
                e.scalar_tensor_tensor(out=srow[0:1, 1024:1536],
                                       in0=ps_r[0:1, 0, :],
                                       scalar=2.0, in1=srow[0:1, 0:512],
                                       op0=ALU.mult, op1=ALU.subtract)
                e.drain()
                e.tensor_tensor(out=srow[0:1, 1536:2048], in0=srow[0:1, 512:1024],
                                in1=srow[0:1, 1024:1536], op=ALU.max
                                ).then_inc(sV.h, 1)
            v_d = sV.bump()
            P.wait(S_, sV, v_d)
            @P.on(S_)
            def _(e):
                e.drain()
                e.activation(out=srow[0:1, 1024:1536], in_=srow[0:1, 1536:2048], func=AF.Ln)
                e.drain()
                e.activation(out=dinvrow[0:1, :], in_=srow[0:1, 1024:1536],
                             func=AF.Exp, scale=-1.0).then_inc(sS.h, 1)
            s_d = sS.bump()
            P.wait(T_, sS, s_d)
            @P.on(T_)
            def _(e):
                e.matmul(ps_r[:, 1, 0:256], ones_c[0:1, :], dinvrow[0:1, 0:256],
                         start=True, stop=True)
                e.matmul(ps_r[:, 1, 256:512], ones_c[0:1, :],
                         dinvrow[0:1, 256:512],
                         start=True, stop=True).then_inc(sT.h, 1)
            t_db = sT.bump()
            P.wait(V_, sT, t_db)
            @P.on(V_)
            def _(e, dbuf=dbuf):
                e.drain()
                e.tensor_copy(out=dbuf[:], in_=ps_r[:, 1, :]
                              ).then_inc(sV.h, 1)
            v_db = sV.bump()
            @P.on(V_)
            def _(e, p=p, H=H, qs=qs, dbuf=dbuf):
                e.drain()
                e.tensor_tensor(out=aoT[0:64, 4 * H + p, qs],
                                in0=ps_ev[0:64, 0, 0:256],
                                in1=dbuf[0:64, 0:256], op=ALU.mult)
                e.tensor_tensor(out=aoT[64:128, 4 * H + p, qs],
                                in0=ps_ev[64:128, 1, 0:256],
                                in1=dbuf[64:128, 256:512],
                                op=ALU.mult).then_inc(sV.h, 1)
            v_ao = sV.bump()
            if p == _NPAIR - 1 and qg + 1 < _NQG:
                qev_pending = qproj_S(qmm_next, qg + 1)
            P.wait(T_, sV, v_ao)
            P.wait(S_, sV, v_ao)

    if c["stage"] == 2 and "ao" in dbg:
        dD2 = P.dsem(f"dDbgB2{H}")
        @P.on(Y_)
        def _(e, w=sV.n):
            e.wait_ge(sV.h, w)
            e.dma_start(out=dbg["ao"].rearrange("p (c m) -> p c m", c=8
                                                )[:, 0:4, :],
                        in_=aoT[:, 0:4, :]).then_inc(dD2.h, 16)
            e.dma_start(out=dbg["e"].rearrange("p (a b) -> p a b", a=16),
                        in_=e_sb[0][:].rearrange("p a b m -> p a (b m)")
                        ).then_inc(dD2.h, 16)
            e.dma_start(out=dbg["qf"].rearrange("p (c m) -> p c m", c=4),
                        in_=qf[1][:]).then_inc(dD2.h, 16)
        dD2.bump(3)


def _dev_out(nc, P, out_hbm):
    dOut = P.dsem("dOut")
    zb = P.st.enter_context(nc.sbuf_tensor("z_out", [128, D], F32))
    @P.on("V")
    def _(e):
        e.memset(zb[:], 0.0)
        e.drain()
        e.nop().then_inc(P.sem["V"].h, 1)
    w = P.sem["V"].bump()
    P.wait("Y", P.sem["V"], w)
    @P.on("Y")
    def _(e):
        for t in range(NT):
            e.dma_start(out=out_hbm[128 * t:128 * (t + 1), :],
                        in_=zb[:]).then_inc(dOut.h, 16)
        e.wait_ge(dOut.h, 16 * NT)
    dOut.bump(NT)


def kernel(L0_attn_x, L0_attn_Wq, L0_attn_Wk1, L0_attn_Wk2, L0_attn_Wv,
           L0_attn_Wo, L0_attn_lambda_logit, L0_ffn_w1, L0_ffn_conv_w,
           L0_ffn_conv_b, L0_ffn_w2, _stage=99, _res=None):
    consts = _consts()
    nc = build(_stage)
    cw = np.asarray(L0_ffn_conv_w, np.float32).reshape(DH, 3)
    base = {
        "WqT": np.ascontiguousarray(np.asarray(L0_attn_Wq, np.float32).T),
        "Wk1T": np.ascontiguousarray(np.asarray(L0_attn_Wk1, np.float32).T),
        "Wk2T": np.ascontiguousarray(np.asarray(L0_attn_Wk2, np.float32).T),
        "WvT": np.ascontiguousarray(np.asarray(L0_attn_Wv, np.float32).T),
        "WoT": np.ascontiguousarray(np.asarray(L0_attn_Wo, np.float32).T),
        "lam": np.ascontiguousarray(L0_attn_lambda_logit, np.float32
                                    ).reshape(NH, 1),
        "w1T": np.ascontiguousarray(np.asarray(L0_ffn_w1, np.float32).T),
        "convw": np.ascontiguousarray(cw.reshape(32, 128, 3
                                                 ).transpose(1, 2, 0)),
        "convb": np.ascontiguousarray(
            np.asarray(L0_ffn_conv_b, np.float32).reshape(32, 128).T),
        "w2T": np.ascontiguousarray(np.asarray(L0_ffn_w2, np.float32).T),
    }
    base.update({"h_" + k: v for k, v in consts.items()})
    in_maps = []
    xs = np.asarray(L0_attn_x)
    for c in range(B):
        m = dict(base)
        m["x"] = np.ascontiguousarray(xs[c], np.float32)
        in_maps.append(m)
    res = run_bass_kernel_spmd(nc, in_maps, core_ids=list(range(B)))
    if _res is not None:
        _res.append(res)
    return np.stack([res.results[c]["out"] for c in range(B)], axis=0)


# version 13
# speedup vs baseline: 1.0651x; 1.0249x over previous
import contextlib
import os
import numpy as np
import ml_dtypes
import concourse.bass as bass
import concourse.mybir as mybir
from concourse.bass_utils import run_bass_kernel_spmd

F32 = mybir.dt.float32
BF16 = mybir.dt.bfloat16
FP16 = mybir.dt.float16
AF = mybir.ActivationFunctionType
ALU = mybir.AluOpType
AX = mybir.AxisListType

B = 8
T = 2048
D = 1024
NH = 16
HD = 64
DH = 4096
NT = T // 128
NDC = D // 128
SCALE = HD ** -0.5
MAGIC = float(np.float32(1.5 * 2.0 ** 23))
EPS = 1e-5
QG = 256
NQG = T // QG
NHIDC = DH // 128
_NLN = int(os.environ.get("KLN_NT", str(NT)))
_NQG = int(os.environ.get("KLN_NQG", str(NQG)))
_NPAIR = int(os.environ.get("KLN_NPAIR", "4"))


def _consts():
    c = {}
    kk = np.arange(128)[:, None]
    qq = np.arange(128)[None, :]
    tri = (kk <= qq).astype(np.float32)
    ev = np.concatenate([tri, np.ones((128, 128), np.float32)], axis=1)
    od = np.concatenate([np.zeros((128, 128), np.float32), tri], axis=1)
    mask = np.concatenate([np.tile(ev, (1, 4)), np.tile(od, (1, 4))], axis=1)
    c["maskEO"] = mask.astype(ml_dtypes.bfloat16)
    o3 = np.zeros((128, 16, 16), dtype=np.float32)
    for p in range(16):
        o3[:, p, p] = 1
    c["onehot3"] = o3.reshape(128, 256).astype(ml_dtypes.bfloat16)
    c["ones_c"] = np.ones((128, 128), dtype=ml_dtypes.bfloat16)
    c["onesf"] = np.ones((128, 256), dtype=np.float32)
    s2 = np.zeros((2, 128), dtype=np.float32)
    s2[0, 0:64] = 1
    s2[1, 64:128] = 1
    c["sel2"] = s2.astype(ml_dtypes.bfloat16)
    return c


class Sem:
    def __init__(self, nc, st, name):
        self.h = st.enter_context(nc.semaphore(name))
        self.n = 0

    def bump(self, k=1):
        self.n += k
        return self.n


class Prog:
    ENGS = ("T", "V", "S", "G", "Y")

    def __init__(self, nc, st):
        self.nc = nc
        self.st = st
        self.steps = []
        self.sem = {e: Sem(nc, st, f"s{e}") for e in self.ENGS}
        self.dsems = []
        self.bar = Sem(nc, st, "bar")
        self.nbar = 0

    def dsem(self, name):
        s = Sem(self.nc, self.st, name)
        self.dsems.append(s)
        return s

    def on(self, eng):
        def reg(fn):
            self.steps.append((eng, fn))
            return fn
        return reg

    def wait(self, eng, sem, val):
        if val <= 0:
            return
        @self.on(eng)
        def _(e, sem=sem, val=val):
            e.wait_ge(sem.h, val)

    def barrier(self):
        self.nbar += 1
        n = self.nbar
        tgt = [(s.h, 16 * s.n) for s in self.dsems if s.n]
        for eng in self.ENGS:
            @self.on(eng)
            def _(e, tgt=tuple(tgt), n=n):
                for h, v in tgt:
                    e.wait_ge(h, v)
                e.drain()
                e.nop().then_inc(self.bar.h, 1)
                e.wait_ge(self.bar.h, 5 * n)

    def finalize(self, block):
        for eng, attr in (("T", "tensor"), ("V", "vector"), ("S", "scalar"),
                          ("G", "gpsimd"), ("Y", "sync")):
            def mk(eng):
                def body(e):
                    for name, fn in self.steps:
                        if name == eng:
                            fn(e)
                return body
            getattr(block, attr)(mk(eng))


def build(stage=99):
    nc = bass.Bass("TRN2", target_bir_lowering=False)
    x_hbm = nc.dram_tensor("x", [T, D], F32, kind="ExternalInput").ap()
    W_hbm = {w: nc.dram_tensor(w + "T", [D, D], F32, kind="ExternalInput").ap()
             for w in ("Wq", "Wk1", "Wk2", "Wv", "Wo")}
    lam_hbm = nc.dram_tensor("lam", [NH, 1], F32, kind="ExternalInput").ap()
    w1_hbm = nc.dram_tensor("w1T", [D, DH], F32, kind="ExternalInput").ap()
    convw_hbm = nc.dram_tensor("convw", [128, 3, 32], F32,
                               kind="ExternalInput").ap()
    convb_hbm = nc.dram_tensor("convb", [128, 32], F32,
                               kind="ExternalInput").ap()
    w2_hbm = nc.dram_tensor("w2T", [DH, D], F32, kind="ExternalInput").ap()
    cvals = _consts()
    C_hbm = {k: nc.dram_tensor("h_" + k, list(v.shape),
                               BF16 if v.dtype == ml_dtypes.bfloat16 else F32,
                               kind="ExternalInput").ap()
             for k, v in cvals.items()}
    out_hbm = nc.dram_tensor("out", [T, D], F32, kind="ExternalOutput").ap()
    ik = "Internal" if stage >= 99 else "ExternalOutput"
    xnew_hbm = nc.dram_tensor("s_xnew", [T, D], F32, kind=ik).ap()
    lamrt_hbm = nc.dram_tensor("s_lamrt", [16, 2], F32, kind="Internal").ap()
    dqrt_hbm = nc.dram_tensor("s_dqrt", [NT, 128], F32, kind="Internal").ap()
    s2rt_hbm = nc.dram_tensor("s_s2rt", [NT, 128], F32, kind="Internal").ap()
    w1q_hbm = nc.dram_tensor("s_w1q", [D, DH], BF16, kind=ik).ap()
    w2q_hbm = nc.dram_tensor("s_w2q", [DH, D], BF16, kind=ik).ap()
    y_hbm = nc.dram_tensor("s_y", [DH, T], BF16, kind=ik).ap()
    dbg_hbm = {}
    if stage < 99:
        for k, shape, dt in (
            ("lnxT", [128, NDC * T], BF16), ("k1f", [128, 4 * T], BF16),
            ("k2f", [128, 4 * T], BF16), ("vtk", [128, NT * 512], BF16),
            ("qf", [128, 4 * QG], BF16), ("ao", [128, 8 * T], BF16),
            ("e", [128, 16 * 1024], BF16), ("row", [128, 2048], F32),
            ("xq", [128, NDC * T], BF16), ("h3", [128, NHIDC * T], FP16),
        ):
            dbg_hbm[k] = nc.dram_tensor("d_" + k, shape, dt,
                                        kind="ExternalOutput").ap()
    st = contextlib.ExitStack()
    with st:
        P = Prog(nc, st)
        _body(nc, st, P, stage, locals())
        with nc.Block() as block:
            P.finalize(block)
    return nc


def _body(nc, st, P, stage, tn):
    x_hbm = tn["x_hbm"]; W_hbm = tn["W_hbm"]
    lam_hbm = tn["lam_hbm"]; C_hbm = tn["C_hbm"]; out_hbm = tn["out_hbm"]
    w1_hbm = tn["w1_hbm"]; w2_hbm = tn["w2_hbm"]
    convw_hbm = tn["convw_hbm"]; convb_hbm = tn["convb_hbm"]
    xnew_hbm = tn["xnew_hbm"]; lamrt_hbm = tn["lamrt_hbm"]
    w1q_hbm = tn["w1q_hbm"]; w2q_hbm = tn["w2q_hbm"]; y_hbm = tn["y_hbm"]
    dqrt_hbm = tn["dqrt_hbm"]; s2rt_hbm = tn["s2rt_hbm"]
    dbg = tn["dbg_hbm"]

    sb = lambda name, shape, dt, **kw: st.enter_context(
        nc.sbuf_tensor(name, shape, dt, **kw))
    T_, V_, S_, G_, Y_ = "T", "V", "S", "G", "Y"
    sT, sV, sS, sG, sY = (P.sem[k] for k in ("T", "V", "S", "G", "Y"))

    lnxT = sb("c_lnxT", [128, NDC, T], BF16, is_transpose_dest=True)
    dq_all = sb("c_dq", [128, NT], F32)
    dq2_all = sb("c_dq2", [128, NT], F32)
    ones_c = sb("c_ones", [128, 128], BF16)
    lamcol = sb("c_lamcol", [128, 1], F32)
    lamsc = sb("c_lamsc", [1, 32], F32)
    convw_sb = sb("c_convw", [128, 3, NHIDC], F32)
    convb_sb = sb("c_convb", [128, NHIDC], F32)
    colscr = sb("c_colscr", [128, 16], F32)
    wcols = sb("c_wcols", [128, 8], F32)
    waccS = sb("w_sacc", [128, 128], F32)
    colhist = None
    phBC = contextlib.ExitStack()
    sbc = lambda name, shape, dt, **kw: phBC.enter_context(
        nc.sbuf_tensor(name, shape, dt, **kw))
    maskEO = sbc("c_maskEO", [128, 2048], BF16)
    onehot3 = sbc("c_onehot3", [128, 256], BF16)
    onesf = sbc("c_onesf", [128, 256], F32)
    sel2 = sbc("c_sel2", [2, 128], BF16)
    aoT = sbc("c_aoT", [128, 2 * 4, T], BF16)
    wbufS = [sbc(f"w_s{i}", [128, 512], F32) for i in range(2)]

    phA = contextlib.ExitStack()
    xfull = phA.enter_context(nc.sbuf_tensor("a_xfull", [128, NT, D], F32))
    sq_scr = phA.enter_context(nc.sbuf_tensor("a_sqscr", [128, D], F32))
    ltile = [phA.enter_context(nc.sbuf_tensor(f"a_ltile{i}", [128, D], BF16))
             for i in range(2)]
    dCon = P.dsem("dCon")
    dXf = P.dsem("dXf")
    dLamW = P.dsem("dLamW")
    dLamR = P.dsem("dLamR")
    dTr = P.dsem("dTr")
    dDbg = P.dsem("dDbg")

    @P.on(V_)
    def _(e):
        e.memset(colscr[:], 0.0)
        e.memset(lamcol[:], 1.0)
        e.memset(wcols[:], 0.0)
        e.drain()
        e.nop().then_inc(sV.h, 1)
    ms_v = sV.bump()
    P.wait(Y_, sV, ms_v)

    @P.on(Y_)
    def _(e):
        e.dma_start(out=maskEO[:], in_=C_hbm["maskEO"]).then_inc(dCon.h, 16)
        e.dma_start(out=onehot3[:], in_=C_hbm["onehot3"]).then_inc(dCon.h, 16)
        e.dma_start(out=ones_c[:], in_=C_hbm["ones_c"]).then_inc(dCon.h, 16)
        e.dma_start(out=onesf[:], in_=C_hbm["onesf"]).then_inc(dCon.h, 16)
        e.dma_start(out=sel2[:], in_=C_hbm["sel2"]).then_inc(dCon.h, 16)
        e.dma_start(out=convw_sb[:], in_=convw_hbm).then_inc(dCon.h, 16)
        e.dma_start(out=convb_sb[:], in_=convb_hbm).then_inc(dCon.h, 16)
        e.dma_start(out=colscr[0:16, 0:1], in_=lam_hbm).then_inc(dCon.h, 16)
        e.dma_start(out=xfull[:],
                    in_=x_hbm.rearrange("(n p) m -> p n m", p=128)
                    ).then_inc(dXf.h, 16)
    consts_done = 16 * dCon.bump(8)
    xfull_done = 16 * dXf.bump()

    P.wait(S_, dCon, consts_done)
    @P.on(S_)
    def _(e):
        e.activation(out=colscr[:, 1:2], in_=colscr[:, 0:1], func=AF.Exp,
                     scale=-1.0).then_inc(sS.h, 1)
    v = sS.bump()
    P.wait(V_, sS, v)
    @P.on(V_)
    def _(e):
        e.tensor_scalar(out=colscr[:, 2:3], in0=colscr[:, 1:2],
                        scalar1=1.0, scalar2=1e-30, op0=ALU.add, op1=ALU.max)
        e.drain()
        e.reciprocal(out=lamcol[:], in_=colscr[:, 2:3]).then_inc(sV.h, 1)
    v = sV.bump()
    P.wait(S_, sV, v)
    @P.on(S_)
    def _(e):
        e.drain()
        e.activation(out=colscr[:, 3:4], in_=lamcol[:],
                     func=AF.Ln).then_inc(sS.h, 1)
    v = sS.bump()
    P.wait(V_, sS, v)
    @P.on(V_)
    def _(e):
        e.drain()
        e.tensor_scalar(out=colscr[:, 4:5], in0=lamcol[:], scalar1=-1.0,
                        scalar2=1.0, op0=ALU.mult, op1=ALU.add
                        ).then_inc(sV.h, 1)
    v = sV.bump()
    P.wait(Y_, sV, v)
    @P.on(Y_)
    def _(e):
        e.dma_start(out=lamrt_hbm, in_=colscr[0:16, 3:5]).then_inc(dLamW.h, 16)
    P.wait(Y_, dLamW, 16 * dLamW.bump())
    @P.on(Y_)
    def _(e):
        e.dma_start(out=lamsc[:], in_=lamrt_hbm.rearrange(
            "a b -> (a b)")[None, :]).then_inc(dLamR.h, 16)
    lam_rd2 = 16 * dLamR.bump()
    lam_done_v = sV.n

    ln_s = []
    ln_v = []
    P.wait(V_, dXf, xfull_done)
    P.wait(V_, dLamW, 16)
    P.wait(S_, dXf, xfull_done)
    for t in range(_NLN):
        xt = xfull[:, t, :]
        @P.on(V_)
        def _(e, xt=xt):
            e.drain()
            e.tensor_reduce(out=colscr[:, 0:1], in_=xt, axis=AX.X,
                            op=ALU.add).then_inc(sV.h, 1)
        v_red = sV.bump()
        if t > 0:
            P.wait(S_, sV, ln_v[t - 1])
        @P.on(S_)
        def _(e, xt=xt):
            e.drain()
            e.activation(out=sq_scr[:], in_=xt, func=AF.Square,
                         accum_out=colscr[:, 1:2]).then_inc(sS.h, 1)
        s_sq = sS.bump()
        P.wait(V_, sS, s_sq)
        @P.on(V_)
        def _(e):
            e.drain()
            e.tensor_scalar(out=colscr[:, 2:3], in0=colscr[:, 0:1],
                            scalar1=1.0 / D, scalar2=None, op0=ALU.mult)
            e.tensor_scalar(out=colscr[:, 3:4], in0=colscr[:, 1:2],
                            scalar1=1.0 / D, scalar2=EPS, op0=ALU.mult,
                            op1=ALU.add)
            e.drain()
            e.tensor_tensor(out=colscr[:, 4:5], in0=colscr[:, 2:3],
                            in1=colscr[:, 2:3], op=ALU.mult)
            e.drain()
            e.tensor_tensor(out=colscr[:, 5:6], in0=colscr[:, 3:4],
                            in1=colscr[:, 4:5], op=ALU.subtract
                            ).then_inc(sV.h, 1)
        v_var = sV.bump()
        P.wait(S_, sV, v_var)
        @P.on(S_)
        def _(e):
            e.drain()
            e.activation(out=colscr[:, 8:9], in_=colscr[:, 5:6], func=AF.Ln)
            e.drain()
            e.activation(out=colscr[:, 6:7], in_=colscr[:, 8:9], func=AF.Exp,
                         scale=-0.5).then_inc(sS.h, 1)
        s_rstd = sS.bump()
        P.wait(V_, sS, s_rstd)
        @P.on(V_)
        def _(e):
            e.drain()
            e.scalar_tensor_tensor(out=colscr[:, 7:8], in0=colscr[:, 2:3],
                                   scalar=-1.0, in1=colscr[:, 6:7],
                                   op0=ALU.mult, op1=ALU.mult).then_inc(sV.h, 1)
        v_nmr = sV.bump()
        ln_v.append(sV.n)
        P.wait(S_, sV, v_nmr)
        if t >= 2:
            P.wait(S_, dTr, 16 * (t - 1))
        @P.on(S_)
        def _(e, t=t, xt=xt):
            e.drain()
            e.activation(out=ltile[t % 2][:], in_=xt,
                         func=AF.Identity, scale=colscr[:, 6:7],
                         bias=colscr[:, 7:8]).then_inc(sS.h, 1)
        s_lnx = sS.bump()
        ln_s.append(s_lnx)
        if t == 0:
            P.wait(S_, dCon, consts_done)
            P.wait(S_, dLamR, lam_rd2)
        @P.on(S_)
        def _(e, t=t, w=s_lnx):
            e.wait_ge(sS.h, w)
            e.wait_ge(dTr.h, 16 * t)
            e.dma_start(out=lnxT[:, :, 128 * t:128 * (t + 1)],
                        in_=ltile[t % 2][:], transpose=True).then_inc(dTr.h, 16)
        dTr.bump()
    lnxT_done = 16 * dTr.n

    if "lnxT" in dbg:
        P.wait(Y_, dTr, lnxT_done)
        @P.on(Y_)
        def _(e, w=sV.n):
            e.wait_ge(sV.h, w)
            e.dma_start(out=dbg["lnxT"].rearrange("p (c m) -> p c m", c=NDC
                                                  )[:, :, 0:128 * _NLN],
                        in_=lnxT[:, :, 0:128 * _NLN]).then_inc(dDbg.h, 16)
        dDbg.bump(1)

    P.barrier()
    phA.close()
    if stage <= 1:
        _dev_out(nc, P, out_hbm)
        return

    dWP = P.dsem("dWP")
    w1_srcS = w1_hbm.rearrange("(c p) m -> p c m", p=128)
    w2_srcS = w2_hbm.rearrange("(c p) m -> p c m", p=128)

    def _wchunk(i):
        if i < 64:
            return w1_srcS[:, i // 8, 512 * (i % 8):512 * (i % 8) + 512]
        j = i - 64
        return w2_srcS[:, j // 2, 512 * (j % 2):512 * (j % 2) + 512]

    wred = []

    def wstep1():
        k = dWP.n
        if k < 128:
            if k >= 2:
                P.wait(G_, sV, wred[k - 2])
            @P.on(G_)
            def _(e, k=k):
                e.dma_start(out=wbufS[k % 2][:], in_=_wchunk(k)
                            ).then_inc(dWP.h, 16)
            dWP.bump()
        j = k - 1
        if 0 <= j < 128:
            P.wait(V_, dWP, 16 * (j + 1))
            @P.on(V_)
            def _(e, j=j):
                e.drain()
                e.tensor_reduce(out=waccS[:, j:j + 1], in_=wbufS[j % 2][:],
                                axis=AX.X, op=ALU.add,
                                apply_absolute_value=True).then_inc(sV.h, 1)
            wred.append(sV.bump())

    def wstep():
        wstep1()
        wstep1()

    phB = contextlib.ExitStack()
    ps_sc = phB.enter_context(nc.psum_tensor("b_sc", [128, 4, 512], F32))
    ps_ev = phB.enter_context(nc.psum_tensor("b_ev", [128, 2, 512], F32))
    ps_r = phB.enter_context(nc.psum_tensor("b_r", [128, 2, 512], F32))
    k1f = phB.enter_context(nc.sbuf_tensor("b_k1f", [128, 4, T], BF16))
    k2f = phB.enter_context(nc.sbuf_tensor("b_k2f", [128, 4, T], BF16))
    vtk = phB.enter_context(nc.sbuf_tensor("b_vtk", [128, NT, 512], BF16))
    Wk_sb = phB.enter_context(nc.sbuf_tensor("b_wk", [128, NDC, 512], BF16))
    Wq_sb = phB.enter_context(nc.sbuf_tensor("b_wq", [128, NDC, 512], BF16))
    qf = [phB.enter_context(nc.sbuf_tensor(f"b_qf{i}", [128, 4, QG], BF16))
          for i in range(2)]
    e_sb = [phB.enter_context(nc.sbuf_tensor("b_e0", [128, 16, 4, 256],
                                             BF16))] * 2
    tbuf = phB.enter_context(nc.sbuf_tensor("b_tbuf", [128, 1024], BF16))
    cbuf = phB.enter_context(nc.sbuf_tensor("b_cbuf", [128, 2, 1024], BF16))
    rcbuf = phB.enter_context(nc.sbuf_tensor("b_rcbuf", [128, 2, 1024], BF16))
    Gb = [phB.enter_context(nc.sbuf_tensor("b_Gb0", [128, 1024], BF16))] * 2
    Db = [phB.enter_context(nc.sbuf_tensor("b_Db0", [128, 512], BF16))] * 2
    srow = phB.enter_context(nc.sbuf_tensor("b_srow", [1, 4 * 512], F32))
    grow = phB.enter_context(nc.sbuf_tensor("b_grow", [1, 512], BF16))
    dinvrow = phB.enter_context(nc.sbuf_tensor("b_dinvr", [1, 512], BF16))
    dWh = P.dsem("dWh")

    if stage == 2 or _NQG < NQG or _NPAIR < 4:
        @P.on("V")
        def _(e):
            e.memset(aoT[:], 0.0)
            e.memset(e_sb[0][:], 0.0)
            e.drain()
            e.nop().then_inc(P.sem["V"].h, 1)
        P.sem["V"].bump()

    ctx = dict(
        P=P, nc=nc, lnxT=lnxT, k1f=k1f, k2f=k2f, vtk=vtk, Wk_sb=Wk_sb,
        Wq_sb=Wq_sb, qf=qf, e_sb=e_sb, aoT=aoT, tbuf=tbuf, cbuf=cbuf,
        rcbuf=rcbuf, Gb=Gb, Db=Db, srow=srow, grow=grow, dinvrow=dinvrow, lamsc=lamsc,
        ps_sc=ps_sc, ps_ev=ps_ev, ps_r=ps_r, W_hbm=W_hbm, dWh=dWh,
        maskEO=maskEO, onehot3=onehot3, ones_c=ones_c, dbg=dbg, stage=stage,
        wstep=wstep,
    )
    for H in range(2 if stage >= 3 else 1):
        _attn_half(ctx, H)
        if stage == 2:
            break
    while len(wred) < 128:
        wstep1()

    if stage == 3 and "ao" in dbg:
        dD3 = P.dsem("dDbg3")
        @P.on("Y")
        def _(e, w=P.sem["V"].n):
            e.wait_ge(P.sem["V"].h, w)
            e.dma_start(out=dbg["ao"].rearrange("p (c m) -> p c m", c=8),
                        in_=aoT[:]).then_inc(dD3.h, 16)
        dD3.bump(1)

    P.barrier()
    phB.close()
    if stage <= 3:
        _dev_out(nc, P, out_hbm)
        return

    phC = contextlib.ExitStack()
    psC = phC.enter_context(nc.psum_tensor("c3_ps", [128, 4, 512], F32))
    WoT_sb = phC.enter_context(nc.sbuf_tensor("c3_wo", [128, NDC, D], BF16))
    xtl = [phC.enter_context(nc.sbuf_tensor(f"c3_xt{i}", [128, D], F32))
           for i in range(2)]
    xmidb = [phC.enter_context(nc.sbuf_tensor(f"c3_xm{i}", [128, D], F32))
             for i in range(2)]
    lx2b = [phC.enter_context(nc.sbuf_tensor(f"c3_lx{i}", [128, D], F32))
            for i in range(2)]
    sqscrC = phC.enter_context(nc.sbuf_tensor("c3_sq", [128, D], F32))
    qscrC = phC.enter_context(nc.sbuf_tensor("c3_qs", [128, D], F32))
    xqfull = phC.enter_context(nc.sbuf_tensor("c3_xqf", [128, NT, D], BF16))
    dWo = P.dsem("dWo")
    dXc = P.dsem("dXc")
    dXw = P.dsem("dXw")
    dTr2 = P.dsem("dTr2")

    @P.on(G_)
    def _(e):
        e.dma_start(out=WoT_sb[:], in_=W_hbm["Wo"].rearrange(
            "(c p) m -> p c m", p=128)).then_inc(dWo.h, 16)
    wo_ld = 16 * dWo.bump()
    P.wait(T_, dWo, wo_ld)

    v_xmid = []
    v_q = []
    s_lx = []
    for t in range(NT):
        if t >= 2:
            P.wait(Y_, sV, v_xmid[t - 2])
        P.wait(Y_, dXc, 16 * dXc.n)
        @P.on(Y_)
        def _(e, t=t):
            e.dma_start(out=xtl[t % 2][:],
                        in_=x_hbm[128 * t:128 * (t + 1), :]).then_inc(dXc.h, 16)
        xld = 16 * dXc.bump()
        if t >= 2:
            P.wait(T_, sV, v_xmid[t - 2])
        @P.on(T_)
        def _(e, t=t):
            for half in range(2):
                for c2 in range(NDC):
                    mm = e.matmul(psC[:, 2 * (t % 2) + half, :],
                                  aoT[:, c2, 128 * t:128 * (t + 1)],
                                  WoT_sb[:, c2, 512 * half:512 * (half + 1)],
                                  start=(c2 == 0), stop=(c2 == NDC - 1))
            mm.then_inc(sT.h, 1)
        t_mm = sT.bump()
        P.wait(V_, sT, t_mm)
        P.wait(V_, dXc, xld)
        if t >= 2:
            P.wait(V_, sS, s_lx[t - 2])
            P.wait(V_, dXw, 16 * (t - 1))
        @P.on(V_)
        def _(e, t=t):
            e.drain()
            e.tensor_tensor(
                out=xmidb[t % 2][:].rearrange("p (a m) -> p a m", a=2),
                in0=psC[:, 2 * (t % 2):2 * (t % 2) + 2, :],
                in1=xtl[t % 2][:].rearrange("p (a m) -> p a m", a=2),
                op=ALU.add).then_inc(sV.h, 1)
        v_xmid.append(sV.bump())
        P.wait(G_, sV, v_xmid[t])
        P.wait(G_, dXw, 16 * dXw.n)
        @P.on(G_)
        def _(e, t=t):
            e.dma_start(out=xnew_hbm[128 * t:128 * (t + 1), :],
                        in_=xmidb[t % 2][:]).then_inc(dXw.h, 16)
        dXw.bump()
        @P.on(V_)
        def _(e, t=t):
            e.drain()
            e.tensor_reduce(out=colscr[:, 0:1], in_=xmidb[t % 2][:], axis=AX.X,
                            op=ALU.add).then_inc(sV.h, 1)
        v_red = sV.bump()
        P.wait(S_, sV, v_red)
        @P.on(S_)
        def _(e, t=t):
            e.drain()
            e.activation(out=sqscrC[:], in_=xmidb[t % 2][:], func=AF.Square,
                         accum_out=colscr[:, 1:2]).then_inc(sS.h, 1)
        s_sq = sS.bump()
        P.wait(V_, sS, s_sq)
        @P.on(V_)
        def _(e):
            e.drain()
            e.tensor_scalar(out=colscr[:, 2:3], in0=colscr[:, 0:1],
                            scalar1=1.0 / D, scalar2=None, op0=ALU.mult)
            e.tensor_scalar(out=colscr[:, 3:4], in0=colscr[:, 1:2],
                            scalar1=1.0 / D, scalar2=EPS, op0=ALU.mult,
                            op1=ALU.add)
            e.drain()
            e.tensor_tensor(out=colscr[:, 4:5], in0=colscr[:, 2:3],
                            in1=colscr[:, 2:3], op=ALU.mult)
            e.drain()
            e.tensor_tensor(out=colscr[:, 5:6], in0=colscr[:, 3:4],
                            in1=colscr[:, 4:5], op=ALU.subtract
                            ).then_inc(sV.h, 1)
        v_var = sV.bump()
        P.wait(S_, sV, v_var)
        @P.on(S_)
        def _(e):
            e.drain()
            e.activation(out=colscr[:, 8:9], in_=colscr[:, 5:6], func=AF.Ln)
            e.drain()
            e.activation(out=colscr[:, 6:7], in_=colscr[:, 8:9], func=AF.Exp,
                         scale=-0.5).then_inc(sS.h, 1)
        s_rstd = sS.bump()
        P.wait(V_, sS, s_rstd)
        @P.on(V_)
        def _(e):
            e.drain()
            e.scalar_tensor_tensor(out=colscr[:, 7:8], in0=colscr[:, 2:3],
                                   scalar=-1.0, in1=colscr[:, 6:7],
                                   op0=ALU.mult, op1=ALU.mult).then_inc(sV.h, 1)
        v_nmr = sV.bump()
        P.wait(S_, sV, v_nmr)
        if t >= 2:
            P.wait(S_, sV, v_q[t - 2])
        @P.on(S_)
        def _(e, t=t):
            e.drain()
            e.activation(out=lx2b[t % 2][:], in_=xmidb[t % 2][:],
                         func=AF.Identity, scale=colscr[:, 6:7],
                         bias=colscr[:, 7:8]).then_inc(sS.h, 1)
        s_lx.append(sS.bump())
        P.wait(V_, sS, s_lx[t])
        @P.on(V_)
        def _(e, t=t):
            e.drain()
            e.tensor_reduce(out=colscr[:, 9:10], in_=lx2b[t % 2][:], axis=AX.X,
                            op=ALU.max, apply_absolute_value=True)
            e.drain()
            e.tensor_scalar(out=colscr[:, 10:11], in0=colscr[:, 9:10],
                            scalar1=1e-5, scalar2=None, op0=ALU.max)
            e.drain()
            e.reciprocal(out=colscr[:, 11:12], in_=colscr[:, 10:11])
            e.tensor_scalar(out=colscr[:, 13:14], in0=colscr[:, 10:11],
                            scalar1=1.0 / 127.0, scalar2=None, op0=ALU.mult)
            e.drain()
            e.tensor_scalar(out=colscr[:, 12:13], in0=colscr[:, 11:12],
                            scalar1=127.0, scalar2=None, op0=ALU.mult)
            e.tensor_copy(out=dq_all[:, t:t + 1], in_=colscr[:, 13:14])
            e.drain()
            e.tensor_scalar(out=qscrC[:], in0=lx2b[t % 2][:],
                            scalar1=colscr[:, 12:13], scalar2=None,
                            op0=ALU.mult)
            e.drain()
            e.tensor_scalar(out=xqfull[:, t, :], in0=qscrC[:],
                            scalar1=MAGIC, scalar2=MAGIC, op0=ALU.add,
                            op1=ALU.subtract).then_inc(sV.h, 1)
        v_q.append(sV.bump())

    P.wait(S_, sV, v_q[NT - 1])
    P.wait(S_, dXc, 16 * dXc.n)
    P.wait(S_, dXw, 16 * dXw.n)
    P.wait(S_, dWo, wo_ld)
    for t in range(NT):
        @P.on(S_)
        def _(e, t=t):
            e.wait_ge(dTr2.h, 16 * t)
            e.dma_start(out=lnxT[:, :, 128 * t:128 * (t + 1)],
                        in_=xqfull[:, t, :], transpose=True
                        ).then_inc(dTr2.h, 16)
        dTr2.bump()

    if stage == 4:
        dD4 = P.dsem("dDbg4")
        P.wait(Y_, dTr2, 16 * dTr2.n)
        @P.on(Y_)
        def _(e, w=sV.n):
            e.wait_ge(sV.h, w)
            e.dma_start(out=dbg["xq"].rearrange("p (c m) -> p c m", c=NDC),
                        in_=lnxT[:]).then_inc(dD4.h, 16)
            e.dma_start(out=dbg["row"][:, 0:NT], in_=dq_all[:]
                        ).then_inc(dD4.h, 16)
        dD4.bump(2)

    P.barrier()
    phC.close()
    phBC.close()
    if stage <= 4:
        _dev_out(nc, P, out_hbm)
        return

    phW = contextlib.ExitStack()
    psW = phW.enter_context(nc.psum_tensor("w_ps", [128, 1, 512], F32))
    wbuf = [phW.enter_context(nc.sbuf_tensor(f"w_wb{i}", [128, DH], F32))
            for i in range(2)]
    wqb = [phW.enter_context(nc.sbuf_tensor(f"w_wq{i}", [128, DH], BF16))
           for i in range(2)]
    qscrW = phW.enter_context(nc.sbuf_tensor("w_qs", [128, DH], F32))
    gscrW = phW.enter_context(nc.sbuf_tensor("w_gs", [128, DH], F32))
    wacc = phW.enter_context(nc.sbuf_tensor("w_acc", [128, 8], F32))
    wtb = phW.enter_context(nc.sbuf_tensor("w_tb", [128, 1], BF16))
    wtl = phW.enter_context(nc.sbuf_tensor("w_tl", [128, 1], BF16))
    dWL = P.dsem("dWL")
    dWS = P.dsem("dWS")

    w1_src = w1_hbm.rearrange("(c p) m -> p c m", p=128)
    w1_dst = w1q_hbm.rearrange("(c p) m -> p c m", p=128)
    w2_src = w2_hbm.rearrange("(c p) m -> p c m", p=128)
    w2_dst = w2q_hbm.rearrange("(c p) m -> p c m", p=128)

    for wi, (src, dst, swc, invc) in enumerate((
            (w1_src, w1_dst, 0, 1), (w2_src, w2_dst, 2, 3))):
        def chunk(ap, i, wi=wi):
            if wi == 0:
                return ap[:, i, :]
            return ap[:, 4 * i:4 * (i + 1), :]
        vshape = (lambda b: b[:]) if wi == 0 else (
            lambda b: b[:].rearrange("p (a m) -> p a m", a=4))
        P.wait(V_, sV, wred[64 * wi + 63])
        @P.on(V_)
        def _(e, wi=wi):
            e.drain()
            e.tensor_reduce(out=colscr[:, 0:1],
                            in_=waccS[:, 64 * wi:64 * wi + 64], axis=AX.X,
                            op=ALU.add)
            e.drain()
            e.tensor_copy(out=wtb[:], in_=colscr[:, 0:1])
            e.drain()
            e.tensor_tensor(out=wtl[:], in0=colscr[:, 0:1], in1=wtb[:],
                            op=ALU.subtract).then_inc(sV.h, 1)
        v_tot = sV.bump()
        P.wait(T_, sV, v_tot)
        @P.on(T_)
        def _(e):
            e.matmul(psW[:, 0, 0:1], ones_c[:], wtb[:], start=True,
                     stop=False)
            e.matmul(psW[:, 0, 0:1], ones_c[:], wtl[:], start=False,
                     stop=True).then_inc(sT.h, 1)
        t_tot = sT.bump()
        P.wait(V_, sT, t_tot)
        @P.on(V_)
        def _(e, swc=swc, invc=invc):
            e.drain()
            e.tensor_scalar(out=wcols[:, invc:invc + 1], in0=psW[:, 0, 0:1],
                            scalar1=1.0 / (D * DH), scalar2=1e-5,
                            op0=ALU.mult, op1=ALU.max)
            e.drain()
            e.reciprocal(out=wcols[:, swc:swc + 1],
                         in_=wcols[:, invc:invc + 1]).then_inc(sV.h, 1)
        v_sw = sV.bump()
        g_q = []
        for i in range(8):
            P.wait(G_, sV, v_sw if i < 2 else 0)
            if i >= 2:
                P.wait(G_, sV, v_r2_list[i - 2])
            P.wait(G_, dWL, 16 * dWL.n)
            @P.on(G_)
            def _(e, i=i, src=src, chunk=chunk, vshape=vshape):
                e.dma_start(out=vshape(wbuf[i % 2]), in_=chunk(src, i)
                            ).then_inc(dWL.h, 16)
            ld = 16 * dWL.bump()
            P.wait(V_, dWL, ld)
            if i >= 2:
                P.wait(V_, dWS, 16 * (dWS.n - 1))
            @P.on(V_)
            def _(e, i=i, swc=swc):
                e.drain()
                e.tensor_scalar(out=qscrW[:], in0=wbuf[i % 2][:],
                                scalar1=wcols[:, swc:swc + 1], scalar2=MAGIC,
                                op0=ALU.mult, op1=ALU.add).then_inc(sV.h, 1)
            vq2 = sV.bump()
            if i == 0:
                v_r2_list = []
            v_r2_list.append(vq2)
            @P.on(V_)
            def _(e, i=i):
                e.drain()
                e.tensor_scalar(out=gscrW[:], in0=qscrW[:], scalar1=MAGIC,
                                scalar2=1.0, op0=ALU.subtract, op1=ALU.min)
                e.drain()
                e.tensor_scalar(out=wqb[i % 2][:], in0=gscrW[:], scalar1=-1.0,
                                scalar2=None, op0=ALU.max
                                ).then_inc(sV.h, 1)
            g_q.append(sV.bump())
            P.wait(Y_, sV, g_q[i])
            P.wait(Y_, dWS, 16 * dWS.n)
            @P.on(Y_)
            def _(e, i=i, dst=dst, chunk=chunk, vshape=vshape):
                e.dma_start(out=chunk(dst, i), in_=vshape(wqb[i % 2])
                            ).then_inc(dWS.h, 16)
            dWS.bump()

    if stage == 5:
        dD5 = P.dsem("dDbg5")
        P.wait(Y_, dWS, 16 * dWS.n)
        @P.on(Y_)
        def _(e, w=sV.n):
            e.wait_ge(sV.h, w)
            e.dma_start(out=dbg["row"][:, 16:24], in_=wcols[:]
                        ).then_inc(dD5.h, 16)
        dD5.bump(1)

    P.barrier()
    phW.close()
    if stage <= 5:
        _dev_out(nc, P, out_hbm)
        return

    phD = contextlib.ExitStack()
    psD = phD.enter_context(nc.psum_tensor("d_ps", [128, 8, 512], F32))
    h3 = phD.enter_context(nc.sbuf_tensor("ds_h3", [128, NHIDC, T], FP16))
    phDq = contextlib.ExitStack()
    dqbc = phDq.enter_context(nc.sbuf_tensor("d_dqbc", [128, T], F32))
    phDp = contextlib.ExitStack()
    dqrow = phDp.enter_context(nc.sbuf_tensor("d_dqr", [1, T], F32))
    dqhi = phDp.enter_context(nc.sbuf_tensor("d_dqh", [1, T], BF16))
    dqlo = phDp.enter_context(nc.sbuf_tensor("d_dql", [1, T], BF16))
    dDq = P.dsem("dDq")
    dW1q = P.dsem("dW1q")

    @P.on(Y_)
    def _(e):
        for c2 in range(NT):
            e.dma_start(out=dqrt_hbm[c2:c2 + 1, :].rearrange("a b -> b a"),
                        in_=dq_all[:, c2:c2 + 1]).then_inc(dDq.h, 16)
        e.wait_ge(dDq.h, 16 * NT)
        e.dma_start(out=dqrow[:], in_=dqrt_hbm.rearrange("a b -> (a b)"
                                                         )[None, :]
                    ).then_inc(dDq.h, 16)
    dDq.bump(NT + 1)
    P.wait(V_, dDq, 16 * (NT + 1))
    @P.on(V_)
    def _(e):
        e.drain()
        e.tensor_copy(out=dqhi[:], in_=dqrow[:])
        e.drain()
        e.tensor_tensor(out=dqlo[:], in0=dqrow[:], in1=dqhi[:],
                        op=ALU.subtract).then_inc(sV.h, 1)
    v_split = sV.bump()
    P.wait(T_, sV, v_split)
    @P.on(T_)
    def _(e):
        for s4 in range(4):
            e.matmul(psD[:, s4, :], ones_c[0:1, :],
                     dqhi[0:1, 512 * s4:512 * (s4 + 1)], start=True,
                     stop=False)
            mm = e.matmul(psD[:, s4, :], ones_c[0:1, :],
                          dqlo[0:1, 512 * s4:512 * (s4 + 1)], start=False,
                          stop=True)
        mm.then_inc(sT.h, 1)
    t_bc = sT.bump()
    P.wait(V_, sT, t_bc)
    @P.on(V_)
    def _(e):
        e.drain()
        e.tensor_copy(out=dqbc[:].rearrange("p (a m) -> p a m", a=4),
                      in_=psD[:, 0:4, :]).then_inc(sV.h, 1)
    v_dqbc = sV.bump()
    P.barrier()
    phDp.close()

    phD2 = contextlib.ExitStack()
    hbuf = phD2.enter_context(nc.sbuf_tensor("d_hb", [128, T + 2], F32))
    g0 = phD2.enter_context(nc.sbuf_tensor("d_g0", [128, T], F32))
    g1 = phD2.enter_context(nc.sbuf_tensor("d_g1", [128, T], F32))
    w1qsb = [phD2.enter_context(nc.sbuf_tensor(f"d_w1q{i}", [128, NDC, 128],
                                               BF16)) for i in range(2)]
    @P.on(V_)
    def _(e):
        e.memset(hbuf[:, 0:1], 0.0)
        e.memset(hbuf[:, T + 1:T + 2], 0.0)
        e.drain()
        e.nop().then_inc(sV.h, 1)
    v_init = sV.bump()
    v_evac = []
    s_silu = []
    for hc in range(NHIDC):
        if hc >= 2:
            P.wait(G_, sT, t_mm1[hc - 2])
        P.wait(G_, dW1q, 16 * dW1q.n)
        @P.on(G_)
        def _(e, hc=hc):
            e.dma_start(out=w1qsb[hc % 2][:],
                        in_=w1q_hbm.rearrange("(c p) m -> p c m", p=128
                                              )[:, :, 128 * hc:128 * (hc + 1)]
                        ).then_inc(dW1q.h, 16)
        w1ld = 16 * dW1q.bump()
        P.wait(T_, dW1q, w1ld)
        P.wait(T_, sV, v_dqbc if hc == 0 else
               (v_evac[hc - 2] if hc >= 2 else 0))
        @P.on(T_)
        def _(e, hc=hc):
            for ts4 in range(4):
                for dc in range(NDC):
                    mm = e.matmul(psD[:, 4 * (hc % 2) + ts4, :],
                                  w1qsb[hc % 2][:, dc, :],
                                  lnxT[:, dc, 512 * ts4:512 * (ts4 + 1)],
                                  start=(dc == 0), stop=(dc == NDC - 1))
            mm.then_inc(sT.h, 1)
        if hc == 0:
            t_mm1 = []
        t_mm1.append(sT.bump())
        P.wait(V_, sT, t_mm1[hc])
        if hc >= 1:
            P.wait(V_, sS, s_silu[hc - 1])
        @P.on(V_)
        def _(e, hc=hc):
            e.drain()
            e.tensor_tensor(
                out=hbuf[:, 1:T + 1].rearrange("p (a m) -> p a m", a=4),
                in0=psD[:, 4 * (hc % 2):4 * (hc % 2) + 4, :],
                in1=dqbc[:].rearrange("p (a m) -> p a m", a=4), op=ALU.mult)
            e.drain()
            e.tensor_scalar(out=g1[:], in0=hbuf[:, 1:T + 1],
                            scalar1=convw_sb[:, 1, hc:hc + 1], scalar2=None,
                            op0=ALU.mult)
            e.drain()
            e.scalar_tensor_tensor(out=g0[:], in0=hbuf[:, 0:T],
                                   scalar=convw_sb[:, 0, hc:hc + 1],
                                   in1=g1[:], op0=ALU.mult, op1=ALU.add)
            e.drain()
            e.scalar_tensor_tensor(out=g1[:], in0=hbuf[:, 2:T + 2],
                                   scalar=convw_sb[:, 2, hc:hc + 1],
                                   in1=g0[:], op0=ALU.mult, op1=ALU.add
                                   ).then_inc(sV.h, 1)
        v_evac.append(sV.bump())
        P.wait(S_, sV, v_evac[hc])
        @P.on(S_)
        def _(e, hc=hc):
            e.drain()
            e.activation(out=h3[:, hc, :], in_=g1[:], func=AF.Silu,
                         scale=wcols[:, 1:2], bias=convb_sb[:, hc:hc + 1]
                         ).then_inc(sS.h, 1)
        s_silu.append(sS.bump())

    P.barrier()
    phD2.close()
    phDq.close()

    phD3a = contextlib.ExitStack()
    s2bc = phD3a.enter_context(nc.sbuf_tensor("d_s2bc", [128, T], F32))
    s2all = phD3a.enter_context(nc.sbuf_tensor("d_s2a", [128, NT], F32))
    phD3b = contextlib.ExitStack()
    mxp = [phD3b.enter_context(nc.sbuf_tensor(f"d_mx{i}", [128, T], FP16))
           for i in range(2)]
    mnp = [phD3b.enter_context(nc.sbuf_tensor(f"d_mn{i}", [128, T], FP16))
           for i in range(2)]
    mxT = phD3b.enter_context(nc.sbuf_tensor("d_mxT", [128, NT, 128], FP16,
                                             is_transpose_dest=True))
    s2row = phD3b.enter_context(nc.sbuf_tensor("d_s2r", [1, T], F32))
    s2hi = phD3b.enter_context(nc.sbuf_tensor("d_s2h", [1, T], BF16))
    s2lo = phD3b.enter_context(nc.sbuf_tensor("d_s2l", [1, T], BF16))
    dS2 = P.dsem("dS2")
    dYw = P.dsem("dYw")

    @P.on(V_)
    def _(e):
        e.memset(mxp[0][:], 0.0)
        e.memset(mnp[0][:], 0.0)
        e.drain()
        for hc in range(NHIDC):
            e.tensor_tensor(out=mxp[(hc + 1) % 2][:], in0=mxp[hc % 2][:],
                            in1=h3[:, hc, :], op=ALU.max)
            e.tensor_tensor(out=mnp[(hc + 1) % 2][:], in0=mnp[hc % 2][:],
                            in1=h3[:, hc, :], op=ALU.min)
            e.drain()
        e.nop().then_inc(sV.h, 1)
    v_mn = sV.bump()
    @P.on(V_)
    def _(e):
        e.tensor_scalar(out=mnp[1][:], in0=mnp[0][:], scalar1=-1.0,
                        scalar2=None, op0=ALU.mult)
        e.drain()
        e.tensor_tensor(out=mxp[1][:], in0=mxp[0][:], in1=mnp[1][:],
                        op=ALU.max).then_inc(sV.h, 1)
    v_comb = sV.bump()
    P.wait(S_, sV, v_comb)
    @P.on(S_)
    def _(e):
        e.dma_start(out=mxT[:], in_=mxp[1][:], transpose=True
                    ).then_inc(dS2.h, 16)
    P.wait(V_, dS2, 16 * dS2.bump())
    @P.on(V_)
    def _(e):
        e.drain()
        e.tensor_reduce(out=colscr[:].rearrange("p (c u) -> p c u", u=1),
                        in_=mxT[:], axis=AX.X, op=ALU.max)
        e.drain()
        e.tensor_scalar(out=dq2_all[:], in0=colscr[:], scalar1=1e-5,
                        scalar2=1.0 / 127.0, op0=ALU.max, op1=ALU.mult)
        e.tensor_scalar(out=s2all[:], in0=colscr[:], scalar1=1e-5,
                        scalar2=None, op0=ALU.max)
        e.drain()
        e.reciprocal(out=s2all[:], in_=s2all[:])
        e.drain()
        e.tensor_scalar(out=s2all[:], in0=s2all[:], scalar1=127.0,
                        scalar2=None, op0=ALU.mult).then_inc(sV.h, 1)
    v_s2 = sV.bump()
    P.wait(Y_, sV, v_s2)
    @P.on(Y_)
    def _(e):
        for c2 in range(NT):
            e.dma_start(out=s2rt_hbm[c2:c2 + 1, :].rearrange("a b -> b a"),
                        in_=s2all[:, c2:c2 + 1]).then_inc(dS2.h, 16)
        e.wait_ge(dS2.h, 16 * (NT + 1))
        e.dma_start(out=s2row[:], in_=s2rt_hbm.rearrange("a b -> (a b)"
                                                         )[None, :]
                    ).then_inc(dS2.h, 16)
    dS2.bump(NT + 1)
    P.wait(V_, dS2, 16 * (NT + 2))
    @P.on(V_)
    def _(e):
        e.drain()
        e.tensor_copy(out=s2hi[:], in_=s2row[:])
        e.drain()
        e.tensor_tensor(out=s2lo[:], in0=s2row[:], in1=s2hi[:],
                        op=ALU.subtract).then_inc(sV.h, 1)
    v_s2s = sV.bump()
    P.wait(T_, sV, v_s2s)
    @P.on(T_)
    def _(e):
        for s4 in range(4):
            e.matmul(psD[:, s4, :], ones_c[0:1, :],
                     s2hi[0:1, 512 * s4:512 * (s4 + 1)], start=True,
                     stop=False)
            mm = e.matmul(psD[:, s4, :], ones_c[0:1, :],
                          s2lo[0:1, 512 * s4:512 * (s4 + 1)], start=False,
                          stop=True)
        mm.then_inc(sT.h, 1)
    t_s2bc = sT.bump()
    P.wait(V_, sT, t_s2bc)
    @P.on(V_)
    def _(e):
        e.drain()
        e.tensor_copy(out=s2bc[:].rearrange("p (a m) -> p a m", a=4),
                      in_=psD[:, 0:4, :]).then_inc(sV.h, 1)
    v_s2bc = sV.bump()
    P.barrier()
    phD3b.close()

    phD3c = contextlib.ExitStack()
    qscrD = phD3c.enter_context(nc.sbuf_tensor("d_qsD", [128, T], F32))
    hqrow = [phD3c.enter_context(nc.sbuf_tensor(f"d_hq{i}", [128, T], BF16))
             for i in range(2)]
    g_hq = []
    for hc in range(NHIDC):
        if hc >= 2:
            P.wait(V_, dYw, 16 * (hc - 1))
        @P.on(V_)
        def _(e, hc=hc):
            e.drain()
            e.tensor_tensor(out=qscrD[:], in0=h3[:, hc, :], in1=s2bc[:],
                            op=ALU.mult)
            e.drain()
            e.tensor_scalar(out=hqrow[hc % 2][:], in0=qscrD[:], scalar1=MAGIC,
                            scalar2=MAGIC, op0=ALU.add, op1=ALU.subtract
                            ).then_inc(sV.h, 1)
        g_hq.append(sV.bump())
        P.wait(Y_, sV, g_hq[hc])
        P.wait(Y_, dYw, 16 * dYw.n)
        @P.on(Y_)
        def _(e, hc=hc):
            e.dma_start(out=y_hbm.rearrange("(c p) m -> p c m", p=128
                                            )[:, hc, :],
                        in_=hqrow[hc % 2][:]).then_inc(dYw.h, 16)
        dYw.bump()

    if stage == 6:
        dD6 = P.dsem("dDbg6")
        P.wait(Y_, dYw, 16 * dYw.n)
        @P.on(Y_)
        def _(e, w=sV.n):
            e.wait_ge(sV.h, w)
            e.dma_start(out=dbg["h3"].rearrange("p (c m) -> p c m", c=NHIDC),
                        in_=h3[:]).then_inc(dD6.h, 16)
            e.dma_start(out=dbg["row"][:, 32:48], in_=dq2_all[:]
                        ).then_inc(dD6.h, 16)
        dD6.bump(2)

    P.barrier()
    phD3c.close()
    phD3a.close()
    phD.close()
    if stage <= 6:
        _dev_out(nc, P, out_hbm)
        return

    phE = contextlib.ExitStack()
    psE = phE.enter_context(nc.psum_tensor("e_ps", [128, 4, 512], F32))
    w2qsb = phE.enter_context(nc.sbuf_tensor("e_w2q", [128, NHIDC, D], BF16))
    hqt = [phE.enter_context(nc.sbuf_tensor(f"e_hq{i}", [128, NHIDC, 128],
                                            BF16)) for i in range(2)]
    xmt = [phE.enter_context(nc.sbuf_tensor(f"e_xm{i}", [128, D], F32))
           for i in range(2)]
    outb = [phE.enter_context(nc.sbuf_tensor(f"e_ob{i}", [128, D], F32))
            for i in range(2)]
    dq2w = phE.enter_context(nc.sbuf_tensor("e_dq2w", [128, NT], F32))
    dW2q = P.dsem("dW2q")
    dHq = P.dsem("dHq")
    dXm = P.dsem("dXm")
    dOut = P.dsem("dOut")

    @P.on(G_)
    def _(e):
        e.dma_start(out=w2qsb[:], in_=w2q_hbm.rearrange("(c p) m -> p c m",
                                                        p=128)
                    ).then_inc(dW2q.h, 16)
    w2ld = 16 * dW2q.bump()
    @P.on(V_)
    def _(e):
        e.drain()
        e.tensor_scalar(out=dq2w[:], in0=dq2_all[:], scalar1=wcols[:, 3:4],
                        scalar2=None, op0=ALU.mult).then_inc(sV.h, 1)
    v_dq2w = sV.bump()
    P.wait(T_, dW2q, w2ld)

    y_src = y_hbm.rearrange("(c p) (n m) -> p c n m", p=128, m=128)
    v_out = []
    for t in range(NT):
        if t >= 2:
            P.wait(Y_, sT, t_mm2[t - 2])
            P.wait(Y_, sV, v_out[t - 2])
        P.wait(Y_, dHq, 16 * dHq.n)
        P.wait(Y_, dXm, 16 * dXm.n)
        @P.on(Y_)
        def _(e, t=t):
            e.dma_start(out=hqt[t % 2][:], in_=y_src[:, :, t, :]
                        ).then_inc(dHq.h, 16)
            e.dma_start(out=xmt[t % 2][:],
                        in_=xnew_hbm[128 * t:128 * (t + 1), :]
                        ).then_inc(dXm.h, 16)
        ld_hq = 16 * dHq.bump()
        ld_xm = 16 * dXm.bump()
        P.wait(T_, dHq, ld_hq)
        if t >= 2:
            P.wait(T_, sV, v_out[t - 2])
        @P.on(T_)
        def _(e, t=t):
            for half in range(2):
                for hc in range(NHIDC):
                    mm = e.matmul(psE[:, 2 * (t % 2) + half, :],
                                  hqt[t % 2][:, hc, :],
                                  w2qsb[:, hc, 512 * half:512 * (half + 1)],
                                  start=(hc == 0), stop=(hc == NHIDC - 1))
            mm.then_inc(sT.h, 1)
        if t == 0:
            t_mm2 = []
        t_mm2.append(sT.bump())
        P.wait(V_, sT, t_mm2[t])
        P.wait(V_, dXm, ld_xm)
        if t >= 2:
            P.wait(V_, dOut, 16 * (t - 1))
        @P.on(V_)
        def _(e, t=t):
            e.drain()
            e.scalar_tensor_tensor(
                out=outb[t % 2][:].rearrange("p (a m) -> p a m", a=2),
                in0=psE[:, 2 * (t % 2):2 * (t % 2) + 2, :],
                scalar=dq2w[:, t:t + 1],
                in1=xmt[t % 2][:].rearrange("p (a m) -> p a m", a=2),
                op0=ALU.mult, op1=ALU.add).then_inc(sV.h, 1)
        v_out.append(sV.bump())
        P.wait(G_, sV, v_out[t])
        P.wait(G_, dOut, 16 * dOut.n)
        @P.on(G_)
        def _(e, t=t):
            e.dma_start(out=out_hbm[128 * t:128 * (t + 1), :],
                        in_=outb[t % 2][:]).then_inc(dOut.h, 16)
        dOut.bump()
    P.wait(G_, dOut, 16 * NT)
    P.wait(Y_, dOut, 16 * NT)
    phE.close()


def _attn_half(c, H):
    P = c["P"]
    sT, sV, sS, sG = (P.sem[k] for k in ("T", "V", "S", "G"))
    T_, V_, S_, G_, Y_ = "T", "V", "S", "G", "Y"
    lnxT = c["lnxT"]; k1f = c["k1f"]; k2f = c["k2f"]; vtk = c["vtk"]
    Wk_sb = c["Wk_sb"]; Wq_sb = c["Wq_sb"]; qf = c["qf"]; e_sb = c["e_sb"]
    aoT = c["aoT"]; tbuf = c["tbuf"]; cbuf = c["cbuf"]; rcbuf = c["rcbuf"]
    Gb = c["Gb"]; Db = c["Db"]; srow = c["srow"]; grow = c["grow"]
    dinvrow = c["dinvrow"]; lamsc = c["lamsc"]
    ps_sc = c["ps_sc"]; ps_ev = c["ps_ev"]
    ps_r = c["ps_r"]; W_hbm = c["W_hbm"]; dWh = c["dWh"]
    maskEO = c["maskEO"]; onehot3 = c["onehot3"]; ones_c = c["ones_c"]
    dbg = c["dbg"]

    cols = slice(512 * H, 512 * H + 512)
    dWq = P.dsem(f"dWq{H}")
    entry_mm = P.sem["T"].n

    def warm(n, ev=False):
        @P.on(T_)
        def _(e, n=n, ev=ev):
            for _ in range(n):
                if ev:
                    e.matmul(ps_ev[0:64, 1, 0:448], ones_c[0:64, 0:64],
                             maskEO[0:64, 0:448], start=True, stop=True,
                             tile_position=(0, 0))
                else:
                    e.matmul(ps_sc[0:64, 0, 0:448], ones_c[0:64, 0:64],
                             maskEO[0:64, 0:448], start=True, stop=True,
                             tile_position=(0, 0))

    P.wait(G_, sT, entry_mm)
    @P.on(G_)
    def _(e):
        e.dma_start(out=Wq_sb[:],
                    in_=W_hbm["Wq"].rearrange("(c p) m -> p c m", p=128
                                              )[:, :, cols]).then_inc(dWq.h, 16)
    wq_ld = 16 * dWq.bump()

    def proj(name, which, prev_mm):
        if prev_mm:
            P.wait(G_, sT, prev_mm)
        @P.on(G_)
        def _(e, name=name):
            e.dma_start(out=Wk_sb[:],
                        in_=W_hbm[name].rearrange("(c p) m -> p c m", p=128
                                                  )[:, :, cols]
                        ).then_inc(dWh.h, 16)
        ld = 16 * dWh.bump()
        P.wait(T_, dWh, ld)
        evs = []
        if which in ("k1", "k2"):
            dst = k1f if which == "k1" else k2f
            for tg in range(4):
                for dc in range(4):
                    if len(evs) >= 4:
                        P.wait(T_, sV, evs[-4])
                    @P.on(T_)
                    def _(e, dc=dc, tg=tg):
                        for dinc in range(NDC):
                            mm = e.matmul(
                                ps_sc[:, dc, 0:512],
                                Wk_sb[:, dinc, 128 * dc:128 * (dc + 1)],
                                lnxT[:, dinc, 512 * tg:512 * (tg + 1)],
                                start=(dinc == 0), stop=(dinc == NDC - 1))
                        mm.then_inc(sT.h, 1)
                    mm = sT.bump()
                    @P.on(V_)
                    def _(e, dc=dc, tg=tg, mm=mm, dst=dst):
                        e.wait_ge(sT.h, mm)
                        e.drain()
                        e.tensor_copy(out=dst[:, dc, 512 * tg:512 * (tg + 1)],
                                      in_=ps_sc[:, dc, 0:512]).then_inc(sV.h, 1)
                    evs.append(sV.bump())
            P.wait(T_, sV, evs[-1])
        else:
            for tokt in range(NT):
                if len(evs) >= 4:
                    P.wait(T_, sV, evs[-4])
                @P.on(T_)
                def _(e, tokt=tokt):
                    for dinc in range(NDC):
                        mm = e.matmul(ps_sc[:, tokt % 4, 0:512],
                                      lnxT[:, dinc, 128 * tokt:128 * (tokt + 1)],
                                      Wk_sb[:, dinc, 0:512],
                                      start=(dinc == 0), stop=(dinc == NDC - 1))
                    mm.then_inc(sT.h, 1)
                mm = sT.bump()
                @P.on(V_)
                def _(e, tokt=tokt, mm=mm):
                    e.wait_ge(sT.h, mm)
                    e.drain()
                    e.tensor_copy(out=vtk[:, tokt, :],
                                  in_=ps_sc[:, tokt % 4, 0:512]).then_inc(sV.h, 1)
                evs.append(sV.bump())
            P.wait(T_, sV, evs[-1])
        return evs[-1], sT.n

    pk1, m1 = proj("Wk1", "k1", entry_mm)
    pk2, m2 = proj("Wk2", "k2", m1)
    pv, m3 = proj("Wv", "v", m2)
    P.wait(T_, dWq, wq_ld)

    if c["stage"] == 2 and H == 0 and "k1f" in dbg:
        dD = P.dsem(f"dDbgB{H}")
        @P.on(Y_)
        def _(e, w=pv):
            e.wait_ge(sV.h, w)
            e.dma_start(out=dbg["k1f"].rearrange("p (c m) -> p c m", c=4),
                        in_=k1f[:]).then_inc(dD.h, 16)
            e.dma_start(out=dbg["k2f"].rearrange("p (c m) -> p c m", c=4),
                        in_=k2f[:]).then_inc(dD.h, 16)
            e.dma_start(out=dbg["vtk"].rearrange("p (c m) -> p c m", c=NT),
                        in_=vtk[:]).then_inc(dD.h, 16)
        dD.bump(3)

    def qproj_T(qg2):
        qs2 = slice(QG * qg2, QG * (qg2 + 1))
        @P.on(T_)
        def _(e, qs2=qs2):
            for dc in range(4):
                for dinc in range(NDC):
                    mm = e.matmul(ps_sc[:, dc, 0:256],
                                  Wq_sb[:, dinc, 128 * dc:128 * (dc + 1)],
                                  lnxT[:, dinc, qs2],
                                  start=(dinc == 0), stop=(dinc == NDC - 1))
            mm.then_inc(sT.h, 1)
        return sT.bump()

    def qproj_S(qmm2, qg2):
        P.wait(S_, sT, qmm2)
        @P.on(S_)
        def _(e, qbuf=qf[qg2 % 2]):
            e.drain()
            for dc in range(4):
                act = e.activation(out=qbuf[:, dc, :], in_=ps_sc[:, dc, 0:256],
                                   func=AF.Identity, scale=SCALE)
            act.then_inc(sS.h, 1)
        return sS.bump()

    qev_pending = qproj_S(qproj_T(0), 0)
    for qg in range(_NQG):
        qs = slice(QG * qg, QG * (qg + 1))
        qbuf = qf[qg % 2]
        qev = qev_pending
        P.wait(T_, sS, qev)

        nkc = 2 * qg + 2
        for p in range(_NPAIR):
            if c.get("wstep") is not None:
                c["wstep"]()
            par = p % 2
            ebuf = e_sb[p % 2]
            gbuf = Gb[par]
            dbuf = Db[par]
            exp_s = []
            for kc2 in range(0, nkc, 2):
                for kc in (kc2, kc2 + 1):
                    b = 2 * (kc % 2)
                    ks = slice(128 * kc, 128 * (kc + 1))
                    @P.on(T_)
                    def _(e, b=b, ks=ks, p=p, qs=qs, qbuf=qbuf):
                        e.matmul(ps_sc[:, b, 0:256], k1f[0:64, p, ks],
                                 qbuf[0:64, p, :], start=True, stop=True,
                                 tile_position=(0, 0))
                        e.matmul(ps_sc[:, b + 1, 0:256], k1f[64:128, p, ks],
                                 qbuf[64:128, p, :], start=True, stop=True,
                                 tile_position=(64, 0))
                        e.matmul(ps_sc[:, b, 256:512], k2f[0:64, p, ks],
                                 qbuf[0:64, p, :], start=True, stop=True,
                                 tile_position=(0, 0))
                        e.matmul(ps_sc[:, b + 1, 256:512], k2f[64:128, p, ks],
                                 qbuf[64:128, p, :], start=True, stop=True,
                                 tile_position=(64, 0)).then_inc(sT.h, 1)
                    sT.bump()
                P.wait(S_, sT, sT.n)
                @P.on(S_)
                def _(e, kc2=kc2, ebuf=ebuf):
                    e.drain()
                    e.activation(out=ebuf[:, kc2:kc2 + 2, :, :],
                                 in_=ps_sc[:], func=AF.Exp).then_inc(sS.h, 1)
                ex = sS.bump()
                exp_s.append(ex)
                if kc2 + 2 >= nkc:
                    P.wait(V_, sS, ex)
                    @P.on(V_)
                    def _(e, kc2=kc2, ebuf=ebuf):
                        e.drain()
                        e.tensor_tensor(
                            out=ebuf[:, kc2:kc2 + 2, :, :].rearrange(
                                "p a b m -> p (a b m)"),
                            in0=ebuf[:, kc2:kc2 + 2, :, :].rearrange(
                                "p a b m -> p (a b m)"),
                            in1=maskEO[:], op=ALU.mult).then_inc(sV.h, 1)
                    sV.bump()
                    P.wait(T_, sV, sV.n)
                else:
                    P.wait(T_, sS, ex)
                @P.on(T_)
                def _(e, kc2=kc2, p=p, ebuf=ebuf, first=(kc2 == 0),
                      last=(kc2 + 2 >= nkc)):
                    for i, kc in enumerate((kc2, kc2 + 1)):
                        st = first and i == 0
                        sp = last and i == 1
                        e.matmul(ps_r[0:16, 0, 0:512],
                                 onehot3[:, 0:16],
                                 ebuf[:, kc, 0:2, :], start=st, stop=sp,
                                 tile_position=(0, 0))
                        mm = e.matmul(ps_r[0:16, 1, 0:512],
                                      onehot3[:, 0:16],
                                      ebuf[:, kc, 2:4, :], start=st, stop=sp,
                                      tile_position=(0, 0))
                    mm.then_inc(sT.h, 1)
                sT.bump()
            r_done = sT.n

            P.wait(S_, sT, r_done)
            @P.on(S_)
            def _(e, p=p):
                e.drain()
                e.activation(out=srow[0:1, 0:512], in_=ps_r[0:1, 0, 0:512],
                             func=AF.Ln)
                e.activation(out=srow[0:1, 512:1024], in_=ps_r[0:1, 1, 0:512],
                             func=AF.Ln).then_inc(sS.h, 1)
            ln_r = sS.bump()
            P.wait(V_, sS, ln_r)
            @P.on(V_)
            def _(e, p=p, H=H):
                e.drain()
                e.tensor_tensor(out=srow[0:1, 1024:1280], in0=srow[0:1, 0:256],
                                in1=srow[0:1, 256:512], op=ALU.subtract)
                e.tensor_tensor(out=srow[0:1, 1280:1536], in0=srow[0:1, 512:768],
                                in1=srow[0:1, 768:1024], op=ALU.subtract)
                hA = 8 * H + 2 * p
                e.drain()
                e.tensor_scalar(out=srow[0:1, 0:256],
                                in0=ps_r[0:1, 0, 0:256],
                                scalar1=lamsc[0:1, 2 * hA + 1:2 * hA + 2],
                                scalar2=None, op0=ALU.mult)
                e.tensor_scalar(out=srow[0:1, 256:512],
                                in0=ps_r[0:1, 1, 0:256],
                                scalar1=lamsc[0:1, 2 * hA + 3:2 * hA + 4],
                                scalar2=None, op0=ALU.mult)
                e.tensor_scalar(
                    out=srow[0:1, 512:1024].rearrange("p (a m) -> p a m", a=2),
                    in0=ps_r[0:1, 0:2, 0:256],
                    scalar1=1e-6, scalar2=None, op0=ALU.mult
                    ).then_inc(sV.h, 1)
            v_t2 = sV.bump()
            P.wait(S_, sV, v_t2)
            @P.on(S_)
            def _(e, p=p, H=H):
                hA = 8 * H + 2 * p
                e.drain()
                e.activation(out=grow[0:1, 0:256], in_=srow[0:1, 1024:1280],
                             func=AF.Exp, bias=lamsc[0:1, 2 * hA:2 * hA + 1])
                e.activation(out=grow[0:1, 256:512], in_=srow[0:1, 1280:1536],
                             func=AF.Exp,
                             bias=lamsc[0:1, 2 * hA + 2:2 * hA + 3]
                             ).then_inc(sS.h, 1)
            s_g = sS.bump()
            P.wait(T_, sS, s_g)
            @P.on(T_)
            def _(e):
                e.matmul(ps_r[:, 1, :], ones_c[0:1, :], grow[0:1, :],
                         start=True, stop=True).then_inc(sT.h, 1)
            t_gb = sT.bump()
            P.wait(V_, sT, t_gb)
            @P.on(V_)
            def _(e, gbuf=gbuf):
                e.drain()
                e.tensor_copy(out=gbuf[:, 0:512], in_=ps_r[:, 1, :])
                e.tensor_copy(out=gbuf[:, 512:1024], in_=ps_r[:, 1, :]
                              ).then_inc(sV.h, 1)
            v_gb = sV.bump()

            P.wait(V_, sS, exp_s[-1])
            ev_mms = []
            for g in range(nkc // 2):
                par2 = g % 2
                if g >= 2:
                    P.wait(V_, sT, ev_mms[g - 2])
                @P.on(V_)
                def _(e, g=g, par2=par2, ebuf=ebuf, gbuf=gbuf, w=v_gb):
                    e.wait_ge(sV.h, w)
                    e.drain()
                    e.tensor_tensor(
                        out=tbuf[:].rearrange("p (a b m) -> p a b m",
                                              a=2, b=2),
                        in0=ebuf[:, 2 * g:2 * g + 2, 1:4:2, :],
                        in1=gbuf[:].rearrange("p (a b m) -> p a b m",
                                              a=2, b=2),
                        op=ALU.mult)
                    e.drain()
                    e.tensor_tensor(
                        out=cbuf[:, par2, :].rearrange(
                            "p (a b m) -> p a b m", a=2, b=2),
                        in0=ebuf[:, 2 * g:2 * g + 2, 0:4:2, :],
                        in1=tbuf[:].rearrange("p (a b m) -> p a b m",
                                              a=2, b=2),
                        op=ALU.subtract)
                    e.drain()
                    e.tensor_scalar(out=rcbuf[:, par2, :],
                                    in0=cbuf[:, par2, :], scalar1=0.0,
                                    scalar2=None, op0=ALU.max).then_inc(sV.h, 1)
                v_c = sV.bump()
                P.wait(T_, sV, v_c)
                @P.on(T_)
                def _(e, g=g, par2=par2, p=p, w=v_t2, stg=(g == 0),
                      spg=(g == nkc // 2 - 1)):
                    e.wait_ge(sV.h, w)
                    for j in range(2):
                        kc = 2 * g + j
                        st = stg and j == 0
                        sp = spg and j == 1
                        e.matmul(ps_ev[0:64, 0, 0:256],
                                 vtk[:, kc, 128 * p:128 * p + 64],
                                 cbuf[:, par2, 512 * j:512 * j + 256],
                                 start=st, stop=sp, tile_position=(0, 0))
                        e.matmul(ps_ev[64:128, 1, 0:256],
                                 vtk[:, kc, 128 * p + 64:128 * (p + 1)],
                                 cbuf[:, par2, 512 * j + 256:512 * (j + 1)],
                                 start=st, stop=sp, tile_position=(0, 64))
                        mm = e.matmul(ps_r[0:16, 0, 0:512],
                                      onehot3[:, 0:16],
                                      rcbuf[:, par2, 512 * j:512 * (j + 1)],
                                      start=st, stop=sp, tile_position=(0, 0))
                    mm.then_inc(sT.h, 1)
                ev_mms.append(sT.bump())
            ev_done = sT.n
            if p == _NPAIR - 1 and qg + 1 < _NQG:
                P.wait(T_, sS, exp_s[-1])
                qmm_next = qproj_T(qg + 1)
            P.wait(V_, sT, ev_done)

            @P.on(V_)
            def _(e, p=p):
                e.drain()
                e.scalar_tensor_tensor(out=srow[0:1, 1024:1536],
                                       in0=ps_r[0:1, 0, :],
                                       scalar=2.0, in1=srow[0:1, 0:512],
                                       op0=ALU.mult, op1=ALU.subtract)
                e.drain()
                e.tensor_tensor(out=srow[0:1, 1536:2048], in0=srow[0:1, 512:1024],
                                in1=srow[0:1, 1024:1536], op=ALU.max
                                ).then_inc(sV.h, 1)
            v_d = sV.bump()
            P.wait(S_, sV, v_d)
            @P.on(S_)
            def _(e):
                e.drain()
                e.activation(out=srow[0:1, 1024:1536], in_=srow[0:1, 1536:2048], func=AF.Ln)
                e.drain()
                e.activation(out=dinvrow[0:1, :], in_=srow[0:1, 1024:1536],
                             func=AF.Exp, scale=-1.0).then_inc(sS.h, 1)
            s_d = sS.bump()
            P.wait(T_, sS, s_d)
            @P.on(T_)
            def _(e):
                e.matmul(ps_r[:, 1, 0:256], ones_c[0:1, :], dinvrow[0:1, 0:256],
                         start=True, stop=True)
                e.matmul(ps_r[:, 1, 256:512], ones_c[0:1, :],
                         dinvrow[0:1, 256:512],
                         start=True, stop=True).then_inc(sT.h, 1)
            t_db = sT.bump()
            P.wait(V_, sT, t_db)
            @P.on(V_)
            def _(e, dbuf=dbuf):
                e.drain()
                e.tensor_copy(out=dbuf[:], in_=ps_r[:, 1, :]
                              ).then_inc(sV.h, 1)
            v_db = sV.bump()
            @P.on(V_)
            def _(e, p=p, H=H, qs=qs, dbuf=dbuf):
                e.drain()
                e.tensor_tensor(out=aoT[0:64, 4 * H + p, qs],
                                in0=ps_ev[0:64, 0, 0:256],
                                in1=dbuf[0:64, 0:256], op=ALU.mult)
                e.tensor_tensor(out=aoT[64:128, 4 * H + p, qs],
                                in0=ps_ev[64:128, 1, 0:256],
                                in1=dbuf[64:128, 256:512],
                                op=ALU.mult).then_inc(sV.h, 1)
            v_ao = sV.bump()
            if p == _NPAIR - 1 and qg + 1 < _NQG:
                qev_pending = qproj_S(qmm_next, qg + 1)
            P.wait(T_, sV, v_ao)
            P.wait(S_, sV, v_ao)

    if c["stage"] == 2 and "ao" in dbg:
        dD2 = P.dsem(f"dDbgB2{H}")
        @P.on(Y_)
        def _(e, w=sV.n):
            e.wait_ge(sV.h, w)
            e.dma_start(out=dbg["ao"].rearrange("p (c m) -> p c m", c=8
                                                )[:, 0:4, :],
                        in_=aoT[:, 0:4, :]).then_inc(dD2.h, 16)
            e.dma_start(out=dbg["e"].rearrange("p (a b) -> p a b", a=16),
                        in_=e_sb[0][:].rearrange("p a b m -> p a (b m)")
                        ).then_inc(dD2.h, 16)
            e.dma_start(out=dbg["qf"].rearrange("p (c m) -> p c m", c=4),
                        in_=qf[1][:]).then_inc(dD2.h, 16)
        dD2.bump(3)


def _dev_out(nc, P, out_hbm):
    dOut = P.dsem("dOut")
    zb = P.st.enter_context(nc.sbuf_tensor("z_out", [128, D], F32))
    @P.on("V")
    def _(e):
        e.memset(zb[:], 0.0)
        e.drain()
        e.nop().then_inc(P.sem["V"].h, 1)
    w = P.sem["V"].bump()
    P.wait("Y", P.sem["V"], w)
    @P.on("Y")
    def _(e):
        for t in range(NT):
            e.dma_start(out=out_hbm[128 * t:128 * (t + 1), :],
                        in_=zb[:]).then_inc(dOut.h, 16)
        e.wait_ge(dOut.h, 16 * NT)
    dOut.bump(NT)


def kernel(L0_attn_x, L0_attn_Wq, L0_attn_Wk1, L0_attn_Wk2, L0_attn_Wv,
           L0_attn_Wo, L0_attn_lambda_logit, L0_ffn_w1, L0_ffn_conv_w,
           L0_ffn_conv_b, L0_ffn_w2, _stage=99, _res=None):
    consts = _consts()
    nc = build(_stage)
    cw = np.asarray(L0_ffn_conv_w, np.float32).reshape(DH, 3)
    base = {
        "WqT": np.ascontiguousarray(np.asarray(L0_attn_Wq, np.float32).T),
        "Wk1T": np.ascontiguousarray(np.asarray(L0_attn_Wk1, np.float32).T),
        "Wk2T": np.ascontiguousarray(np.asarray(L0_attn_Wk2, np.float32).T),
        "WvT": np.ascontiguousarray(np.asarray(L0_attn_Wv, np.float32).T),
        "WoT": np.ascontiguousarray(np.asarray(L0_attn_Wo, np.float32).T),
        "lam": np.ascontiguousarray(L0_attn_lambda_logit, np.float32
                                    ).reshape(NH, 1),
        "w1T": np.ascontiguousarray(np.asarray(L0_ffn_w1, np.float32).T),
        "convw": np.ascontiguousarray(cw.reshape(32, 128, 3
                                                 ).transpose(1, 2, 0)),
        "convb": np.ascontiguousarray(
            np.asarray(L0_ffn_conv_b, np.float32).reshape(32, 128).T),
        "w2T": np.ascontiguousarray(np.asarray(L0_ffn_w2, np.float32).T),
    }
    base.update({"h_" + k: v for k, v in consts.items()})
    in_maps = []
    xs = np.asarray(L0_attn_x)
    for c in range(B):
        m = dict(base)
        m["x"] = np.ascontiguousarray(xs[c], np.float32)
        in_maps.append(m)
    res = run_bass_kernel_spmd(nc, in_maps, core_ids=list(range(B)))
    if _res is not None:
        _res.append(res)
    return np.stack([res.results[c]["out"] for c in range(B)], axis=0)


# version 14 (final)
# speedup vs baseline: 1.0815x; 1.0154x over previous
import contextlib
import os
import numpy as np
import ml_dtypes
import concourse.bass as bass
import concourse.mybir as mybir
from concourse.bass_utils import run_bass_kernel_spmd

F32 = mybir.dt.float32
BF16 = mybir.dt.bfloat16
FP16 = mybir.dt.float16
AF = mybir.ActivationFunctionType
ALU = mybir.AluOpType
AX = mybir.AxisListType

B = 8
T = 2048
D = 1024
NH = 16
HD = 64
DH = 4096
NT = T // 128
NDC = D // 128
SCALE = HD ** -0.5
MAGIC = float(np.float32(1.5 * 2.0 ** 23))
EPS = 1e-5
QG = 256
NQG = T // QG
NHIDC = DH // 128
_NLN = int(os.environ.get("KLN_NT", str(NT)))
_NQG = int(os.environ.get("KLN_NQG", str(NQG)))
_NPAIR = int(os.environ.get("KLN_NPAIR", "4"))


def _consts():
    c = {}
    kk = np.arange(128)[:, None]
    qq = np.arange(128)[None, :]
    tri = (kk <= qq).astype(np.float32)
    ev = np.concatenate([tri, np.ones((128, 128), np.float32)], axis=1)
    od = np.concatenate([np.zeros((128, 128), np.float32), tri], axis=1)
    mask = np.concatenate([np.tile(ev, (1, 4)), np.tile(od, (1, 4))], axis=1)
    c["maskEO"] = mask.astype(ml_dtypes.bfloat16)
    o3 = np.zeros((128, 16, 16), dtype=np.float32)
    for p in range(16):
        o3[:, p, p] = 1
    c["onehot3"] = o3.reshape(128, 256).astype(ml_dtypes.bfloat16)
    c["ones_c"] = np.ones((128, 128), dtype=ml_dtypes.bfloat16)
    c["onesf"] = np.ones((128, 256), dtype=np.float32)
    s2 = np.zeros((2, 128), dtype=np.float32)
    s2[0, 0:64] = 1
    s2[1, 64:128] = 1
    c["sel2"] = s2.astype(ml_dtypes.bfloat16)
    return c


class Sem:
    def __init__(self, nc, st, name):
        self.h = st.enter_context(nc.semaphore(name))
        self.n = 0

    def bump(self, k=1):
        self.n += k
        return self.n


class Prog:
    ENGS = ("T", "V", "S", "G", "Y")

    def __init__(self, nc, st):
        self.nc = nc
        self.st = st
        self.steps = []
        self.sem = {e: Sem(nc, st, f"s{e}") for e in self.ENGS}
        self.dsems = []
        self.bar = Sem(nc, st, "bar")
        self.nbar = 0

    def dsem(self, name):
        s = Sem(self.nc, self.st, name)
        self.dsems.append(s)
        return s

    def on(self, eng):
        def reg(fn):
            self.steps.append((eng, fn))
            return fn
        return reg

    def wait(self, eng, sem, val):
        if val <= 0:
            return
        @self.on(eng)
        def _(e, sem=sem, val=val):
            e.wait_ge(sem.h, val)

    def barrier(self):
        self.nbar += 1
        n = self.nbar
        tgt = [(s.h, 16 * s.n) for s in self.dsems if s.n]
        for eng in self.ENGS:
            @self.on(eng)
            def _(e, tgt=tuple(tgt), n=n):
                for h, v in tgt:
                    e.wait_ge(h, v)
                e.drain()
                e.nop().then_inc(self.bar.h, 1)
                e.wait_ge(self.bar.h, 5 * n)

    def finalize(self, block):
        for eng, attr in (("T", "tensor"), ("V", "vector"), ("S", "scalar"),
                          ("G", "gpsimd"), ("Y", "sync")):
            def mk(eng):
                def body(e):
                    for name, fn in self.steps:
                        if name == eng:
                            fn(e)
                return body
            getattr(block, attr)(mk(eng))


def build(stage=99):
    nc = bass.Bass("TRN2", target_bir_lowering=False)
    x_hbm = nc.dram_tensor("x", [T, D], F32, kind="ExternalInput").ap()
    W_hbm = {w: nc.dram_tensor(w + "T", [D, D], F32, kind="ExternalInput").ap()
             for w in ("Wq", "Wk1", "Wk2", "Wv", "Wo")}
    lam_hbm = nc.dram_tensor("lam", [NH, 1], F32, kind="ExternalInput").ap()
    w1_hbm = nc.dram_tensor("w1T", [D, DH], F32, kind="ExternalInput").ap()
    convw_hbm = nc.dram_tensor("convw", [128, 3, 32], F32,
                               kind="ExternalInput").ap()
    convb_hbm = nc.dram_tensor("convb", [128, 32], F32,
                               kind="ExternalInput").ap()
    w2_hbm = nc.dram_tensor("w2T", [DH, D], F32, kind="ExternalInput").ap()
    cvals = _consts()
    C_hbm = {k: nc.dram_tensor("h_" + k, list(v.shape),
                               BF16 if v.dtype == ml_dtypes.bfloat16 else F32,
                               kind="ExternalInput").ap()
             for k, v in cvals.items()}
    out_hbm = nc.dram_tensor("out", [T, D], F32, kind="ExternalOutput").ap()
    ik = "Internal" if stage >= 99 else "ExternalOutput"
    xnew_hbm = nc.dram_tensor("s_xnew", [T, D], F32, kind=ik).ap()
    lamrt_hbm = nc.dram_tensor("s_lamrt", [16, 2], F32, kind="Internal").ap()
    dqrt_hbm = nc.dram_tensor("s_dqrt", [NT, 128], F32, kind="Internal").ap()
    s2rt_hbm = nc.dram_tensor("s_s2rt", [NT, 128], F32, kind="Internal").ap()
    w1q_hbm = nc.dram_tensor("s_w1q", [D, DH], BF16, kind=ik).ap()
    w2q_hbm = nc.dram_tensor("s_w2q", [DH, D], BF16, kind=ik).ap()
    y_hbm = nc.dram_tensor("s_y", [DH, T], BF16, kind=ik).ap()
    dbg_hbm = {}
    if stage < 99:
        for k, shape, dt in (
            ("lnxT", [128, NDC * T], BF16), ("k1f", [128, 4 * T], BF16),
            ("k2f", [128, 4 * T], BF16), ("vtk", [128, NT * 512], BF16),
            ("qf", [128, 4 * QG], BF16), ("ao", [128, 8 * T], BF16),
            ("e", [128, 16 * 1024], BF16), ("row", [128, 2048], F32),
            ("xq", [128, NDC * T], BF16), ("h3", [128, NHIDC * T], FP16),
        ):
            dbg_hbm[k] = nc.dram_tensor("d_" + k, shape, dt,
                                        kind="ExternalOutput").ap()
    st = contextlib.ExitStack()
    with st:
        P = Prog(nc, st)
        _body(nc, st, P, stage, locals())
        with nc.Block() as block:
            P.finalize(block)
    return nc


def _body(nc, st, P, stage, tn):
    x_hbm = tn["x_hbm"]; W_hbm = tn["W_hbm"]
    lam_hbm = tn["lam_hbm"]; C_hbm = tn["C_hbm"]; out_hbm = tn["out_hbm"]
    w1_hbm = tn["w1_hbm"]; w2_hbm = tn["w2_hbm"]
    convw_hbm = tn["convw_hbm"]; convb_hbm = tn["convb_hbm"]
    xnew_hbm = tn["xnew_hbm"]; lamrt_hbm = tn["lamrt_hbm"]
    w1q_hbm = tn["w1q_hbm"]; w2q_hbm = tn["w2q_hbm"]; y_hbm = tn["y_hbm"]
    dqrt_hbm = tn["dqrt_hbm"]; s2rt_hbm = tn["s2rt_hbm"]
    dbg = tn["dbg_hbm"]

    sb = lambda name, shape, dt, **kw: st.enter_context(
        nc.sbuf_tensor(name, shape, dt, **kw))
    T_, V_, S_, G_, Y_ = "T", "V", "S", "G", "Y"
    sT, sV, sS, sG, sY = (P.sem[k] for k in ("T", "V", "S", "G", "Y"))

    lnxT = sb("c_lnxT", [128, NDC, T], BF16, is_transpose_dest=True)
    dq_all = sb("c_dq", [128, NT], F32)
    dq2_all = sb("c_dq2", [128, NT], F32)
    ones_c = sb("c_ones", [128, 128], BF16)
    lamcol = sb("c_lamcol", [128, 1], F32)
    lamsc = sb("c_lamsc", [1, 32], F32)
    convw_sb = sb("c_convw", [128, 3, NHIDC], F32)
    convb_sb = sb("c_convb", [128, NHIDC], F32)
    colscr = sb("c_colscr", [128, 16], F32)
    wcols = sb("c_wcols", [128, 8], F32)
    waccS = sb("w_sacc", [128, 128], F32)
    colhist = None
    phBC = contextlib.ExitStack()
    sbc = lambda name, shape, dt, **kw: phBC.enter_context(
        nc.sbuf_tensor(name, shape, dt, **kw))
    maskEO = sbc("c_maskEO", [128, 2048], BF16)
    onehot3 = sbc("c_onehot3", [128, 256], BF16)
    onesf = sbc("c_onesf", [128, 256], F32)
    sel2 = sbc("c_sel2", [2, 128], BF16)
    aoT = sbc("c_aoT", [128, 2 * 4, T], BF16)
    wbufS = [sbc(f"w_s{i}", [128, 512], F32) for i in range(2)]

    phA = contextlib.ExitStack()
    xfull = phA.enter_context(nc.sbuf_tensor("a_xfull", [128, NT, D], F32))
    sq_scr = phA.enter_context(nc.sbuf_tensor("a_sqscr", [128, D], F32))
    ltile = [phA.enter_context(nc.sbuf_tensor(f"a_ltile{i}", [128, D], BF16))
             for i in range(2)]
    dCon = P.dsem("dCon")
    dXf = P.dsem("dXf")
    dLamW = P.dsem("dLamW")
    dLamR = P.dsem("dLamR")
    dTr = P.dsem("dTr")
    dDbg = P.dsem("dDbg")

    @P.on(V_)
    def _(e):
        e.memset(colscr[:], 0.0)
        e.memset(lamcol[:], 1.0)
        e.memset(wcols[:], 0.0)
        e.drain()
        e.nop().then_inc(sV.h, 1)
    ms_v = sV.bump()
    P.wait(Y_, sV, ms_v)

    @P.on(Y_)
    def _(e):
        e.dma_start(out=maskEO[:], in_=C_hbm["maskEO"]).then_inc(dCon.h, 16)
        e.dma_start(out=onehot3[:], in_=C_hbm["onehot3"]).then_inc(dCon.h, 16)
        e.dma_start(out=ones_c[:], in_=C_hbm["ones_c"]).then_inc(dCon.h, 16)
        e.dma_start(out=onesf[:], in_=C_hbm["onesf"]).then_inc(dCon.h, 16)
        e.dma_start(out=sel2[:], in_=C_hbm["sel2"]).then_inc(dCon.h, 16)
        e.dma_start(out=convw_sb[:], in_=convw_hbm).then_inc(dCon.h, 16)
        e.dma_start(out=convb_sb[:], in_=convb_hbm).then_inc(dCon.h, 16)
        e.dma_start(out=colscr[0:16, 0:1], in_=lam_hbm).then_inc(dCon.h, 16)
        e.dma_start(out=xfull[:],
                    in_=x_hbm.rearrange("(n p) m -> p n m", p=128)
                    ).then_inc(dXf.h, 16)
    consts_done = 16 * dCon.bump(8)
    xfull_done = 16 * dXf.bump()

    P.wait(S_, dCon, consts_done)
    @P.on(S_)
    def _(e):
        e.activation(out=colscr[:, 1:2], in_=colscr[:, 0:1], func=AF.Exp,
                     scale=-1.0).then_inc(sS.h, 1)
    v = sS.bump()
    P.wait(V_, sS, v)
    @P.on(V_)
    def _(e):
        e.tensor_scalar(out=colscr[:, 2:3], in0=colscr[:, 1:2],
                        scalar1=1.0, scalar2=1e-30, op0=ALU.add, op1=ALU.max)
        e.drain()
        e.reciprocal(out=lamcol[:], in_=colscr[:, 2:3]).then_inc(sV.h, 1)
    v = sV.bump()
    P.wait(S_, sV, v)
    @P.on(S_)
    def _(e):
        e.drain()
        e.activation(out=colscr[:, 3:4], in_=lamcol[:],
                     func=AF.Ln).then_inc(sS.h, 1)
    v = sS.bump()
    P.wait(V_, sS, v)
    @P.on(V_)
    def _(e):
        e.drain()
        e.tensor_scalar(out=colscr[:, 4:5], in0=lamcol[:], scalar1=-1.0,
                        scalar2=1.0, op0=ALU.mult, op1=ALU.add
                        ).then_inc(sV.h, 1)
    v = sV.bump()
    P.wait(Y_, sV, v)
    @P.on(Y_)
    def _(e):
        e.dma_start(out=lamrt_hbm, in_=colscr[0:16, 3:5]).then_inc(dLamW.h, 16)
    P.wait(Y_, dLamW, 16 * dLamW.bump())
    @P.on(Y_)
    def _(e):
        e.dma_start(out=lamsc[:], in_=lamrt_hbm.rearrange(
            "a b -> (a b)")[None, :]).then_inc(dLamR.h, 16)
    lam_rd2 = 16 * dLamR.bump()
    lam_done_v = sV.n

    ln_s = []
    ln_v = []
    P.wait(V_, dXf, xfull_done)
    P.wait(V_, dLamW, 16)
    P.wait(S_, dXf, xfull_done)
    for t in range(_NLN):
        xt = xfull[:, t, :]
        @P.on(V_)
        def _(e, xt=xt):
            e.drain()
            e.tensor_reduce(out=colscr[:, 0:1], in_=xt, axis=AX.X,
                            op=ALU.add).then_inc(sV.h, 1)
        v_red = sV.bump()
        if t > 0:
            P.wait(S_, sV, ln_v[t - 1])
        @P.on(S_)
        def _(e, xt=xt):
            e.drain()
            e.activation(out=sq_scr[:], in_=xt, func=AF.Square,
                         accum_out=colscr[:, 1:2]).then_inc(sS.h, 1)
        s_sq = sS.bump()
        P.wait(V_, sS, s_sq)
        @P.on(V_)
        def _(e):
            e.drain()
            e.tensor_scalar(out=colscr[:, 2:3], in0=colscr[:, 0:1],
                            scalar1=1.0 / D, scalar2=None, op0=ALU.mult)
            e.tensor_scalar(out=colscr[:, 3:4], in0=colscr[:, 1:2],
                            scalar1=1.0 / D, scalar2=EPS, op0=ALU.mult,
                            op1=ALU.add)
            e.drain()
            e.tensor_tensor(out=colscr[:, 4:5], in0=colscr[:, 2:3],
                            in1=colscr[:, 2:3], op=ALU.mult)
            e.drain()
            e.tensor_tensor(out=colscr[:, 5:6], in0=colscr[:, 3:4],
                            in1=colscr[:, 4:5], op=ALU.subtract
                            ).then_inc(sV.h, 1)
        v_var = sV.bump()
        P.wait(S_, sV, v_var)
        @P.on(S_)
        def _(e):
            e.drain()
            e.activation(out=colscr[:, 8:9], in_=colscr[:, 5:6], func=AF.Ln)
            e.drain()
            e.activation(out=colscr[:, 6:7], in_=colscr[:, 8:9], func=AF.Exp,
                         scale=-0.5).then_inc(sS.h, 1)
        s_rstd = sS.bump()
        P.wait(V_, sS, s_rstd)
        @P.on(V_)
        def _(e):
            e.drain()
            e.scalar_tensor_tensor(out=colscr[:, 7:8], in0=colscr[:, 2:3],
                                   scalar=-1.0, in1=colscr[:, 6:7],
                                   op0=ALU.mult, op1=ALU.mult).then_inc(sV.h, 1)
        v_nmr = sV.bump()
        ln_v.append(sV.n)
        P.wait(S_, sV, v_nmr)
        if t >= 2:
            P.wait(S_, dTr, 16 * (t - 1))
        @P.on(S_)
        def _(e, t=t, xt=xt):
            e.drain()
            e.activation(out=ltile[t % 2][:], in_=xt,
                         func=AF.Identity, scale=colscr[:, 6:7],
                         bias=colscr[:, 7:8]).then_inc(sS.h, 1)
        s_lnx = sS.bump()
        ln_s.append(s_lnx)
        if t == 0:
            P.wait(S_, dCon, consts_done)
            P.wait(S_, dLamR, lam_rd2)
        @P.on(S_)
        def _(e, t=t, w=s_lnx):
            e.wait_ge(sS.h, w)
            e.wait_ge(dTr.h, 16 * t)
            e.dma_start(out=lnxT[:, :, 128 * t:128 * (t + 1)],
                        in_=ltile[t % 2][:], transpose=True).then_inc(dTr.h, 16)
        dTr.bump()
    lnxT_done = 16 * dTr.n

    if "lnxT" in dbg:
        P.wait(Y_, dTr, lnxT_done)
        @P.on(Y_)
        def _(e, w=sV.n):
            e.wait_ge(sV.h, w)
            e.dma_start(out=dbg["lnxT"].rearrange("p (c m) -> p c m", c=NDC
                                                  )[:, :, 0:128 * _NLN],
                        in_=lnxT[:, :, 0:128 * _NLN]).then_inc(dDbg.h, 16)
        dDbg.bump(1)

    P.barrier()
    phA.close()
    if stage <= 1:
        _dev_out(nc, P, out_hbm)
        return

    dWP = P.dsem("dWP")
    w1_srcS = w1_hbm.rearrange("(c p) m -> p c m", p=128)
    w2_srcS = w2_hbm.rearrange("(c p) m -> p c m", p=128)

    def _wchunk(i):
        if i < 64:
            return w1_srcS[:, i // 8, 512 * (i % 8):512 * (i % 8) + 512]
        j = i - 64
        return w2_srcS[:, j // 2, 512 * (j % 2):512 * (j % 2) + 512]

    wred = []

    def wstep1():
        k = dWP.n
        if k < 128:
            if k >= 2:
                P.wait(G_, sV, wred[k - 2])
            @P.on(G_)
            def _(e, k=k):
                e.dma_start(out=wbufS[k % 2][:], in_=_wchunk(k)
                            ).then_inc(dWP.h, 16)
            dWP.bump()
        j = k - 1
        if 0 <= j < 128:
            P.wait(V_, dWP, 16 * (j + 1))
            @P.on(V_)
            def _(e, j=j):
                e.drain()
                e.tensor_reduce(out=waccS[:, j:j + 1], in_=wbufS[j % 2][:],
                                axis=AX.X, op=ALU.add,
                                apply_absolute_value=True).then_inc(sV.h, 1)
            wred.append(sV.bump())

    def wstep():
        wstep1()
        wstep1()

    phB = contextlib.ExitStack()
    ps_sc = phB.enter_context(nc.psum_tensor("b_sc", [128, 4, 512], F32))
    ps_ev = phB.enter_context(nc.psum_tensor("b_ev", [128, 2, 512], F32))
    ps_r = phB.enter_context(nc.psum_tensor("b_r", [128, 2, 512], F32))
    k1f = phB.enter_context(nc.sbuf_tensor("b_k1f", [128, 4, T], BF16))
    k2f = phB.enter_context(nc.sbuf_tensor("b_k2f", [128, 4, T], BF16))
    vtk = phB.enter_context(nc.sbuf_tensor("b_vtk", [128, NT, 512], BF16))
    Wk_sb = phB.enter_context(nc.sbuf_tensor("b_wk", [128, NDC, 512], BF16))
    Wq_sb = phB.enter_context(nc.sbuf_tensor("b_wq", [128, NDC, 512], BF16))
    qf = [phB.enter_context(nc.sbuf_tensor(f"b_qf{i}", [128, 4, QG], BF16))
          for i in range(2)]
    e_sb = [phB.enter_context(nc.sbuf_tensor("b_e0", [128, 16, 4, 256],
                                             BF16))] * 2
    tbuf = phB.enter_context(nc.sbuf_tensor("b_tbuf", [128, 1024], BF16))
    cbuf = phB.enter_context(nc.sbuf_tensor("b_cbuf", [128, 2, 1024], BF16))
    rcbuf = phB.enter_context(nc.sbuf_tensor("b_rcbuf", [128, 2, 1024], BF16))
    Gb = [phB.enter_context(nc.sbuf_tensor("b_Gb0", [128, 1024], BF16))] * 2
    Db = [phB.enter_context(nc.sbuf_tensor("b_Db0", [128, 512], BF16))] * 2
    srow = phB.enter_context(nc.sbuf_tensor("b_srow", [1, 4 * 512], F32))
    grow = phB.enter_context(nc.sbuf_tensor("b_grow", [1, 512], BF16))
    dinvrow = phB.enter_context(nc.sbuf_tensor("b_dinvr", [1, 512], BF16))
    dWh = P.dsem("dWh")

    if stage == 2 or _NQG < NQG or _NPAIR < 4:
        @P.on("V")
        def _(e):
            e.memset(aoT[:], 0.0)
            e.memset(e_sb[0][:], 0.0)
            e.drain()
            e.nop().then_inc(P.sem["V"].h, 1)
        P.sem["V"].bump()

    ctx = dict(
        P=P, nc=nc, lnxT=lnxT, k1f=k1f, k2f=k2f, vtk=vtk, Wk_sb=Wk_sb,
        Wq_sb=Wq_sb, qf=qf, e_sb=e_sb, aoT=aoT, tbuf=tbuf, cbuf=cbuf,
        rcbuf=rcbuf, Gb=Gb, Db=Db, srow=srow, grow=grow, dinvrow=dinvrow, lamsc=lamsc,
        ps_sc=ps_sc, ps_ev=ps_ev, ps_r=ps_r, W_hbm=W_hbm, dWh=dWh,
        maskEO=maskEO, onehot3=onehot3, ones_c=ones_c, dbg=dbg, stage=stage,
        wstep=wstep,
    )
    for H in range(2 if stage >= 3 else 1):
        _attn_half(ctx, H)
        if stage == 2:
            break
    while len(wred) < 128:
        wstep1()

    if stage == 3 and "ao" in dbg:
        dD3 = P.dsem("dDbg3")
        @P.on("Y")
        def _(e, w=P.sem["V"].n):
            e.wait_ge(P.sem["V"].h, w)
            e.dma_start(out=dbg["ao"].rearrange("p (c m) -> p c m", c=8),
                        in_=aoT[:]).then_inc(dD3.h, 16)
        dD3.bump(1)

    P.barrier()
    phB.close()
    if stage <= 3:
        _dev_out(nc, P, out_hbm)
        return

    phC = contextlib.ExitStack()
    psC = phC.enter_context(nc.psum_tensor("c3_ps", [128, 4, 512], F32))
    WoT_sb = phC.enter_context(nc.sbuf_tensor("c3_wo", [128, NDC, D], BF16))
    xtl = [phC.enter_context(nc.sbuf_tensor(f"c3_xt{i}", [128, D], F32))
           for i in range(2)]
    xmidb = [phC.enter_context(nc.sbuf_tensor(f"c3_xm{i}", [128, D], F32))
             for i in range(2)]
    lx2b = [phC.enter_context(nc.sbuf_tensor(f"c3_lx{i}", [128, D], F32))
            for i in range(2)]
    sqscrC = phC.enter_context(nc.sbuf_tensor("c3_sq", [128, D], F32))
    qscrC = phC.enter_context(nc.sbuf_tensor("c3_qs", [128, D], F32))
    xqfull = phC.enter_context(nc.sbuf_tensor("c3_xqf", [128, NT, D], BF16))
    dWo = P.dsem("dWo")
    dXc = P.dsem("dXc")
    dXw = P.dsem("dXw")
    dTr2 = P.dsem("dTr2")

    @P.on(G_)
    def _(e):
        e.dma_start(out=WoT_sb[:], in_=W_hbm["Wo"].rearrange(
            "(c p) m -> p c m", p=128)).then_inc(dWo.h, 16)
    wo_ld = 16 * dWo.bump()
    P.wait(T_, dWo, wo_ld)

    v_xmid = []
    v_q = []
    s_lx = []
    for t in range(NT):
        if t >= 2:
            P.wait(Y_, sV, v_xmid[t - 2])
        P.wait(Y_, dXc, 16 * dXc.n)
        @P.on(Y_)
        def _(e, t=t):
            e.dma_start(out=xtl[t % 2][:],
                        in_=x_hbm[128 * t:128 * (t + 1), :]).then_inc(dXc.h, 16)
        xld = 16 * dXc.bump()
        if t >= 2:
            P.wait(T_, sV, v_xmid[t - 2])
        @P.on(T_)
        def _(e, t=t):
            for half in range(2):
                for c2 in range(NDC):
                    mm = e.matmul(psC[:, 2 * (t % 2) + half, :],
                                  aoT[:, c2, 128 * t:128 * (t + 1)],
                                  WoT_sb[:, c2, 512 * half:512 * (half + 1)],
                                  start=(c2 == 0), stop=(c2 == NDC - 1))
            mm.then_inc(sT.h, 1)
        t_mm = sT.bump()
        P.wait(V_, sT, t_mm)
        P.wait(V_, dXc, xld)
        if t >= 2:
            P.wait(V_, sS, s_lx[t - 2])
            P.wait(V_, dXw, 16 * (t - 1))
        @P.on(V_)
        def _(e, t=t):
            e.drain()
            e.tensor_tensor(
                out=xmidb[t % 2][:].rearrange("p (a m) -> p a m", a=2),
                in0=psC[:, 2 * (t % 2):2 * (t % 2) + 2, :],
                in1=xtl[t % 2][:].rearrange("p (a m) -> p a m", a=2),
                op=ALU.add).then_inc(sV.h, 1)
        v_xmid.append(sV.bump())
        P.wait(G_, sV, v_xmid[t])
        P.wait(G_, dXw, 16 * dXw.n)
        @P.on(G_)
        def _(e, t=t):
            e.dma_start(out=xnew_hbm[128 * t:128 * (t + 1), :],
                        in_=xmidb[t % 2][:]).then_inc(dXw.h, 16)
        dXw.bump()
        @P.on(V_)
        def _(e, t=t):
            e.drain()
            e.tensor_reduce(out=colscr[:, 0:1], in_=xmidb[t % 2][:], axis=AX.X,
                            op=ALU.add).then_inc(sV.h, 1)
        v_red = sV.bump()
        P.wait(S_, sV, v_red)
        @P.on(S_)
        def _(e, t=t):
            e.drain()
            e.activation(out=sqscrC[:], in_=xmidb[t % 2][:], func=AF.Square,
                         accum_out=colscr[:, 1:2]).then_inc(sS.h, 1)
        s_sq = sS.bump()
        P.wait(V_, sS, s_sq)
        @P.on(V_)
        def _(e):
            e.drain()
            e.tensor_scalar(out=colscr[:, 2:3], in0=colscr[:, 0:1],
                            scalar1=1.0 / D, scalar2=None, op0=ALU.mult)
            e.tensor_scalar(out=colscr[:, 3:4], in0=colscr[:, 1:2],
                            scalar1=1.0 / D, scalar2=EPS, op0=ALU.mult,
                            op1=ALU.add)
            e.drain()
            e.tensor_tensor(out=colscr[:, 4:5], in0=colscr[:, 2:3],
                            in1=colscr[:, 2:3], op=ALU.mult)
            e.drain()
            e.tensor_tensor(out=colscr[:, 5:6], in0=colscr[:, 3:4],
                            in1=colscr[:, 4:5], op=ALU.subtract
                            ).then_inc(sV.h, 1)
        v_var = sV.bump()
        P.wait(S_, sV, v_var)
        @P.on(S_)
        def _(e):
            e.drain()
            e.activation(out=colscr[:, 8:9], in_=colscr[:, 5:6], func=AF.Ln)
            e.drain()
            e.activation(out=colscr[:, 6:7], in_=colscr[:, 8:9], func=AF.Exp,
                         scale=-0.5).then_inc(sS.h, 1)
        s_rstd = sS.bump()
        P.wait(V_, sS, s_rstd)
        @P.on(V_)
        def _(e):
            e.drain()
            e.scalar_tensor_tensor(out=colscr[:, 7:8], in0=colscr[:, 2:3],
                                   scalar=-1.0, in1=colscr[:, 6:7],
                                   op0=ALU.mult, op1=ALU.mult).then_inc(sV.h, 1)
        v_nmr = sV.bump()
        P.wait(S_, sV, v_nmr)
        if t >= 2:
            P.wait(S_, sV, v_q[t - 2])
        @P.on(S_)
        def _(e, t=t):
            e.drain()
            e.activation(out=lx2b[t % 2][:], in_=xmidb[t % 2][:],
                         func=AF.Identity, scale=colscr[:, 6:7],
                         bias=colscr[:, 7:8]).then_inc(sS.h, 1)
        s_lx.append(sS.bump())
        P.wait(V_, sS, s_lx[t])
        @P.on(V_)
        def _(e, t=t):
            e.drain()
            e.tensor_reduce(out=colscr[:, 9:10], in_=lx2b[t % 2][:], axis=AX.X,
                            op=ALU.max, apply_absolute_value=True)
            e.drain()
            e.tensor_scalar(out=colscr[:, 10:11], in0=colscr[:, 9:10],
                            scalar1=1e-5, scalar2=None, op0=ALU.max)
            e.drain()
            e.reciprocal(out=colscr[:, 11:12], in_=colscr[:, 10:11])
            e.tensor_scalar(out=colscr[:, 13:14], in0=colscr[:, 10:11],
                            scalar1=1.0 / 127.0, scalar2=None, op0=ALU.mult)
            e.drain()
            e.tensor_scalar(out=colscr[:, 12:13], in0=colscr[:, 11:12],
                            scalar1=127.0, scalar2=None, op0=ALU.mult)
            e.tensor_copy(out=dq_all[:, t:t + 1], in_=colscr[:, 13:14])
            e.drain()
            e.tensor_scalar(out=qscrC[:], in0=lx2b[t % 2][:],
                            scalar1=colscr[:, 12:13], scalar2=None,
                            op0=ALU.mult)
            e.drain()
            e.tensor_scalar(out=xqfull[:, t, :], in0=qscrC[:],
                            scalar1=MAGIC, scalar2=MAGIC, op0=ALU.add,
                            op1=ALU.subtract).then_inc(sV.h, 1)
        v_q.append(sV.bump())

    P.wait(S_, sV, v_q[NT - 1])
    P.wait(S_, dXc, 16 * dXc.n)
    P.wait(S_, dXw, 16 * dXw.n)
    P.wait(S_, dWo, wo_ld)
    for t in range(NT):
        @P.on(S_)
        def _(e, t=t):
            e.wait_ge(dTr2.h, 16 * t)
            e.dma_start(out=lnxT[:, :, 128 * t:128 * (t + 1)],
                        in_=xqfull[:, t, :], transpose=True
                        ).then_inc(dTr2.h, 16)
        dTr2.bump()

    if stage == 4:
        dD4 = P.dsem("dDbg4")
        P.wait(Y_, dTr2, 16 * dTr2.n)
        @P.on(Y_)
        def _(e, w=sV.n):
            e.wait_ge(sV.h, w)
            e.dma_start(out=dbg["xq"].rearrange("p (c m) -> p c m", c=NDC),
                        in_=lnxT[:]).then_inc(dD4.h, 16)
            e.dma_start(out=dbg["row"][:, 0:NT], in_=dq_all[:]
                        ).then_inc(dD4.h, 16)
        dD4.bump(2)

    P.barrier()
    phC.close()
    phBC.close()
    if stage <= 4:
        _dev_out(nc, P, out_hbm)
        return

    phW = contextlib.ExitStack()
    psW = phW.enter_context(nc.psum_tensor("w_ps", [128, 1, 512], F32))
    wbuf = [phW.enter_context(nc.sbuf_tensor(f"w_wb{i}", [128, DH], F32))
            for i in range(2)]
    wqb = [phW.enter_context(nc.sbuf_tensor(f"w_wq{i}", [128, DH], BF16))
           for i in range(2)]
    qscrW = phW.enter_context(nc.sbuf_tensor("w_qs", [128, DH], F32))
    gscrW = phW.enter_context(nc.sbuf_tensor("w_gs", [128, DH], F32))
    wacc = phW.enter_context(nc.sbuf_tensor("w_acc", [128, 8], F32))
    wtb = phW.enter_context(nc.sbuf_tensor("w_tb", [128, 1], BF16))
    wtl = phW.enter_context(nc.sbuf_tensor("w_tl", [128, 1], BF16))
    dWL = P.dsem("dWL")
    dWS = P.dsem("dWS")

    w1_src = w1_hbm.rearrange("(c p) m -> p c m", p=128)
    w1_dst = w1q_hbm.rearrange("(c p) m -> p c m", p=128)
    w2_src = w2_hbm.rearrange("(c p) m -> p c m", p=128)
    w2_dst = w2q_hbm.rearrange("(c p) m -> p c m", p=128)

    for wi, (src, dst, swc, invc) in enumerate((
            (w1_src, w1_dst, 0, 1), (w2_src, w2_dst, 2, 3))):
        def chunk(ap, i, wi=wi):
            if wi == 0:
                return ap[:, i, :]
            return ap[:, 4 * i:4 * (i + 1), :]
        vshape = (lambda b: b[:]) if wi == 0 else (
            lambda b: b[:].rearrange("p (a m) -> p a m", a=4))
        P.wait(V_, sV, wred[64 * wi + 63])
        @P.on(V_)
        def _(e, wi=wi):
            e.drain()
            e.tensor_reduce(out=colscr[:, 0:1],
                            in_=waccS[:, 64 * wi:64 * wi + 64], axis=AX.X,
                            op=ALU.add)
            e.drain()
            e.tensor_copy(out=wtb[:], in_=colscr[:, 0:1])
            e.drain()
            e.tensor_tensor(out=wtl[:], in0=colscr[:, 0:1], in1=wtb[:],
                            op=ALU.subtract).then_inc(sV.h, 1)
        v_tot = sV.bump()
        P.wait(T_, sV, v_tot)
        @P.on(T_)
        def _(e):
            e.matmul(psW[:, 0, 0:1], ones_c[:], wtb[:], start=True,
                     stop=False)
            e.matmul(psW[:, 0, 0:1], ones_c[:], wtl[:], start=False,
                     stop=True).then_inc(sT.h, 1)
        t_tot = sT.bump()
        P.wait(V_, sT, t_tot)
        @P.on(V_)
        def _(e, swc=swc, invc=invc):
            e.drain()
            e.tensor_scalar(out=wcols[:, invc:invc + 1], in0=psW[:, 0, 0:1],
                            scalar1=1.0 / (D * DH), scalar2=1e-5,
                            op0=ALU.mult, op1=ALU.max)
            e.drain()
            e.reciprocal(out=wcols[:, swc:swc + 1],
                         in_=wcols[:, invc:invc + 1]).then_inc(sV.h, 1)
        v_sw = sV.bump()
        g_q = []
        for i in range(8):
            P.wait(G_, sV, v_sw if i < 2 else 0)
            if i >= 2:
                P.wait(G_, sV, v_r2_list[i - 2])
            P.wait(G_, dWL, 16 * dWL.n)
            @P.on(G_)
            def _(e, i=i, src=src, chunk=chunk, vshape=vshape):
                e.dma_start(out=vshape(wbuf[i % 2]), in_=chunk(src, i)
                            ).then_inc(dWL.h, 16)
            ld = 16 * dWL.bump()
            P.wait(V_, dWL, ld)
            if i >= 2:
                P.wait(V_, dWS, 16 * (dWS.n - 1))
            @P.on(V_)
            def _(e, i=i, swc=swc):
                e.drain()
                e.tensor_scalar(out=qscrW[:], in0=wbuf[i % 2][:],
                                scalar1=wcols[:, swc:swc + 1], scalar2=MAGIC,
                                op0=ALU.mult, op1=ALU.add).then_inc(sV.h, 1)
            vq2 = sV.bump()
            if i == 0:
                v_r2_list = []
            v_r2_list.append(vq2)
            @P.on(V_)
            def _(e, i=i):
                e.drain()
                e.tensor_scalar(out=gscrW[:], in0=qscrW[:], scalar1=MAGIC,
                                scalar2=1.0, op0=ALU.subtract, op1=ALU.min)
                e.drain()
                e.tensor_scalar(out=wqb[i % 2][:], in0=gscrW[:], scalar1=-1.0,
                                scalar2=None, op0=ALU.max
                                ).then_inc(sV.h, 1)
            g_q.append(sV.bump())
            P.wait(Y_, sV, g_q[i])
            P.wait(Y_, dWS, 16 * dWS.n)
            @P.on(Y_)
            def _(e, i=i, dst=dst, chunk=chunk, vshape=vshape):
                e.dma_start(out=chunk(dst, i), in_=vshape(wqb[i % 2])
                            ).then_inc(dWS.h, 16)
            dWS.bump()

    if stage == 5:
        dD5 = P.dsem("dDbg5")
        P.wait(Y_, dWS, 16 * dWS.n)
        @P.on(Y_)
        def _(e, w=sV.n):
            e.wait_ge(sV.h, w)
            e.dma_start(out=dbg["row"][:, 16:24], in_=wcols[:]
                        ).then_inc(dD5.h, 16)
        dD5.bump(1)

    P.barrier()
    phW.close()
    if stage <= 5:
        _dev_out(nc, P, out_hbm)
        return

    phD = contextlib.ExitStack()
    psD = phD.enter_context(nc.psum_tensor("d_ps", [128, 8, 512], F32))
    h3 = phD.enter_context(nc.sbuf_tensor("ds_h3", [128, NHIDC, T], FP16))
    mxp = [phD.enter_context(nc.sbuf_tensor("d_mx0", [128, T], FP16))]
    mnp = [phD.enter_context(nc.sbuf_tensor("d_mn0", [128, T], FP16))]
    phDq = contextlib.ExitStack()
    dqbc = phDq.enter_context(nc.sbuf_tensor("d_dqbc", [128, T], F32))
    phDp = contextlib.ExitStack()
    dqrow = phDp.enter_context(nc.sbuf_tensor("d_dqr", [1, T], F32))
    dqhi = phDp.enter_context(nc.sbuf_tensor("d_dqh", [1, T], BF16))
    dqlo = phDp.enter_context(nc.sbuf_tensor("d_dql", [1, T], BF16))
    dDq = P.dsem("dDq")
    dW1q = P.dsem("dW1q")

    @P.on(Y_)
    def _(e):
        for c2 in range(NT):
            e.dma_start(out=dqrt_hbm[c2:c2 + 1, :].rearrange("a b -> b a"),
                        in_=dq_all[:, c2:c2 + 1]).then_inc(dDq.h, 16)
        e.wait_ge(dDq.h, 16 * NT)
        e.dma_start(out=dqrow[:], in_=dqrt_hbm.rearrange("a b -> (a b)"
                                                         )[None, :]
                    ).then_inc(dDq.h, 16)
    dDq.bump(NT + 1)
    P.wait(V_, dDq, 16 * (NT + 1))
    @P.on(V_)
    def _(e):
        e.drain()
        e.tensor_copy(out=dqhi[:], in_=dqrow[:])
        e.drain()
        e.tensor_tensor(out=dqlo[:], in0=dqrow[:], in1=dqhi[:],
                        op=ALU.subtract).then_inc(sV.h, 1)
    v_split = sV.bump()
    P.wait(T_, sV, v_split)
    @P.on(T_)
    def _(e):
        for s4 in range(4):
            e.matmul(psD[:, s4, :], ones_c[0:1, :],
                     dqhi[0:1, 512 * s4:512 * (s4 + 1)], start=True,
                     stop=False)
            mm = e.matmul(psD[:, s4, :], ones_c[0:1, :],
                          dqlo[0:1, 512 * s4:512 * (s4 + 1)], start=False,
                          stop=True)
        mm.then_inc(sT.h, 1)
    t_bc = sT.bump()
    P.wait(V_, sT, t_bc)
    @P.on(V_)
    def _(e):
        e.drain()
        e.tensor_copy(out=dqbc[:].rearrange("p (a m) -> p a m", a=4),
                      in_=psD[:, 0:4, :]).then_inc(sV.h, 1)
    v_dqbc = sV.bump()
    P.barrier()
    phDp.close()

    phD2 = contextlib.ExitStack()
    hbuf = phD2.enter_context(nc.sbuf_tensor("d_hb", [128, T + 2], F32))
    g0 = phD2.enter_context(nc.sbuf_tensor("d_g0", [128, T], F32))
    g1 = phD2.enter_context(nc.sbuf_tensor("d_g1", [128, T], F32))
    w1qsb = [phD2.enter_context(nc.sbuf_tensor(f"d_w1q{i}", [128, NDC, 128],
                                               BF16)) for i in range(2)]
    @P.on(V_)
    def _(e):
        e.memset(hbuf[:, 0:1], 0.0)
        e.memset(hbuf[:, T + 1:T + 2], 0.0)
        e.memset(mxp[0][:], 0.0)
        e.memset(mnp[0][:], 0.0)
        e.drain()
        e.nop().then_inc(sV.h, 1)
    v_init = sV.bump()
    v_evac = []
    s_silu = []
    for hc in range(NHIDC):
        if hc >= 2:
            P.wait(G_, sT, t_mm1[hc - 2])
        P.wait(G_, dW1q, 16 * dW1q.n)
        @P.on(G_)
        def _(e, hc=hc):
            e.dma_start(out=w1qsb[hc % 2][:],
                        in_=w1q_hbm.rearrange("(c p) m -> p c m", p=128
                                              )[:, :, 128 * hc:128 * (hc + 1)]
                        ).then_inc(dW1q.h, 16)
        w1ld = 16 * dW1q.bump()
        P.wait(T_, dW1q, w1ld)
        P.wait(T_, sV, v_dqbc if hc == 0 else
               (v_evac[hc - 2] if hc >= 2 else 0))
        @P.on(T_)
        def _(e, hc=hc):
            for ts4 in range(4):
                for dc in range(NDC):
                    mm = e.matmul(psD[:, 4 * (hc % 2) + ts4, :],
                                  w1qsb[hc % 2][:, dc, :],
                                  lnxT[:, dc, 512 * ts4:512 * (ts4 + 1)],
                                  start=(dc == 0), stop=(dc == NDC - 1))
            mm.then_inc(sT.h, 1)
        if hc == 0:
            t_mm1 = []
        t_mm1.append(sT.bump())
        P.wait(V_, sT, t_mm1[hc])
        if hc >= 1:
            P.wait(V_, sS, s_silu[hc - 1])
        @P.on(V_)
        def _(e, hc=hc):
            e.drain()
            e.tensor_tensor(
                out=hbuf[:, 1:T + 1].rearrange("p (a m) -> p a m", a=4),
                in0=psD[:, 4 * (hc % 2):4 * (hc % 2) + 4, :],
                in1=dqbc[:].rearrange("p (a m) -> p a m", a=4), op=ALU.mult)
            e.drain()
            e.tensor_scalar(out=g1[:], in0=hbuf[:, 1:T + 1],
                            scalar1=convw_sb[:, 1, hc:hc + 1], scalar2=None,
                            op0=ALU.mult)
            e.drain()
            e.scalar_tensor_tensor(out=g0[:], in0=hbuf[:, 0:T],
                                   scalar=convw_sb[:, 0, hc:hc + 1],
                                   in1=g1[:], op0=ALU.mult, op1=ALU.add)
            e.drain()
            last = e.scalar_tensor_tensor(out=g1[:], in0=hbuf[:, 2:T + 2],
                                          scalar=convw_sb[:, 2, hc:hc + 1],
                                          in1=g0[:], op0=ALU.mult,
                                          op1=ALU.add)
            if hc >= 1:
                e.drain()
                e.tensor_tensor(out=mxp[0][:], in0=mxp[0][:],
                                in1=h3[:, hc - 1, :], op=ALU.max)
                last = e.tensor_tensor(out=mnp[0][:], in0=mnp[0][:],
                                       in1=h3[:, hc - 1, :], op=ALU.min)
            last.then_inc(sV.h, 1)
        v_evac.append(sV.bump())
        P.wait(S_, sV, v_evac[hc])
        @P.on(S_)
        def _(e, hc=hc):
            e.drain()
            e.activation(out=h3[:, hc, :], in_=g1[:], func=AF.Silu,
                         scale=wcols[:, 1:2], bias=convb_sb[:, hc:hc + 1]
                         ).then_inc(sS.h, 1)
        s_silu.append(sS.bump())

    P.barrier()
    phD2.close()
    phDq.close()

    phD3a = contextlib.ExitStack()
    s2bc = phD3a.enter_context(nc.sbuf_tensor("d_s2bc", [128, T], F32))
    s2all = phD3a.enter_context(nc.sbuf_tensor("d_s2a", [128, NT], F32))
    phD3b = contextlib.ExitStack()
    mxT = phD3b.enter_context(nc.sbuf_tensor("d_mxT", [128, NT, 128], FP16,
                                             is_transpose_dest=True))
    s2row = phD3b.enter_context(nc.sbuf_tensor("d_s2r", [1, T], F32))
    s2hi = phD3b.enter_context(nc.sbuf_tensor("d_s2h", [1, T], BF16))
    s2lo = phD3b.enter_context(nc.sbuf_tensor("d_s2l", [1, T], BF16))
    dS2 = P.dsem("dS2")
    dYw = P.dsem("dYw")

    @P.on(V_)
    def _(e):
        e.drain()
        e.tensor_tensor(out=mxp[0][:], in0=mxp[0][:],
                        in1=h3[:, NHIDC - 1, :], op=ALU.max)
        e.tensor_tensor(out=mnp[0][:], in0=mnp[0][:],
                        in1=h3[:, NHIDC - 1, :], op=ALU.min
                        ).then_inc(sV.h, 1)
    v_mn = sV.bump()
    @P.on(V_)
    def _(e):
        e.tensor_scalar(out=mnp[0][:], in0=mnp[0][:], scalar1=-1.0,
                        scalar2=None, op0=ALU.mult)
        e.drain()
        e.tensor_tensor(out=mxp[0][:], in0=mxp[0][:], in1=mnp[0][:],
                        op=ALU.max).then_inc(sV.h, 1)
    v_comb = sV.bump()
    P.wait(S_, sV, v_comb)
    @P.on(S_)
    def _(e):
        e.dma_start(out=mxT[:], in_=mxp[0][:], transpose=True
                    ).then_inc(dS2.h, 16)
    P.wait(V_, dS2, 16 * dS2.bump())
    @P.on(V_)
    def _(e):
        e.drain()
        e.tensor_reduce(out=colscr[:].rearrange("p (c u) -> p c u", u=1),
                        in_=mxT[:], axis=AX.X, op=ALU.max)
        e.drain()
        e.tensor_scalar(out=dq2_all[:], in0=colscr[:], scalar1=1e-5,
                        scalar2=1.0 / 127.0, op0=ALU.max, op1=ALU.mult)
        e.tensor_scalar(out=s2all[:], in0=colscr[:], scalar1=1e-5,
                        scalar2=None, op0=ALU.max)
        e.drain()
        e.reciprocal(out=s2all[:], in_=s2all[:])
        e.drain()
        e.tensor_scalar(out=s2all[:], in0=s2all[:], scalar1=127.0,
                        scalar2=None, op0=ALU.mult).then_inc(sV.h, 1)
    v_s2 = sV.bump()
    P.wait(Y_, sV, v_s2)
    @P.on(Y_)
    def _(e):
        for c2 in range(NT):
            e.dma_start(out=s2rt_hbm[c2:c2 + 1, :].rearrange("a b -> b a"),
                        in_=s2all[:, c2:c2 + 1]).then_inc(dS2.h, 16)
        e.wait_ge(dS2.h, 16 * (NT + 1))
        e.dma_start(out=s2row[:], in_=s2rt_hbm.rearrange("a b -> (a b)"
                                                         )[None, :]
                    ).then_inc(dS2.h, 16)
    dS2.bump(NT + 1)
    P.wait(V_, dS2, 16 * (NT + 2))
    @P.on(V_)
    def _(e):
        e.drain()
        e.tensor_copy(out=s2hi[:], in_=s2row[:])
        e.drain()
        e.tensor_tensor(out=s2lo[:], in0=s2row[:], in1=s2hi[:],
                        op=ALU.subtract).then_inc(sV.h, 1)
    v_s2s = sV.bump()
    P.wait(T_, sV, v_s2s)
    @P.on(T_)
    def _(e):
        for s4 in range(4):
            e.matmul(psD[:, s4, :], ones_c[0:1, :],
                     s2hi[0:1, 512 * s4:512 * (s4 + 1)], start=True,
                     stop=False)
            mm = e.matmul(psD[:, s4, :], ones_c[0:1, :],
                          s2lo[0:1, 512 * s4:512 * (s4 + 1)], start=False,
                          stop=True)
        mm.then_inc(sT.h, 1)
    t_s2bc = sT.bump()
    P.wait(V_, sT, t_s2bc)
    @P.on(V_)
    def _(e):
        e.drain()
        e.tensor_copy(out=s2bc[:].rearrange("p (a m) -> p a m", a=4),
                      in_=psD[:, 0:4, :]).then_inc(sV.h, 1)
    v_s2bc = sV.bump()
    P.barrier()
    phD3b.close()

    phD3c = contextlib.ExitStack()
    qscrD = phD3c.enter_context(nc.sbuf_tensor("d_qsD", [128, T], F32))
    hqrow = [phD3c.enter_context(nc.sbuf_tensor(f"d_hq{i}", [128, T], BF16))
             for i in range(2)]
    g_hq = []
    for hc in range(NHIDC):
        if hc >= 2:
            P.wait(V_, dYw, 16 * (hc - 1))
        @P.on(V_)
        def _(e, hc=hc):
            e.drain()
            e.tensor_tensor(out=qscrD[:], in0=h3[:, hc, :], in1=s2bc[:],
                            op=ALU.mult)
            e.drain()
            e.tensor_scalar(out=hqrow[hc % 2][:], in0=qscrD[:], scalar1=MAGIC,
                            scalar2=MAGIC, op0=ALU.add, op1=ALU.subtract
                            ).then_inc(sV.h, 1)
        g_hq.append(sV.bump())
        P.wait(Y_, sV, g_hq[hc])
        P.wait(Y_, dYw, 16 * dYw.n)
        @P.on(Y_)
        def _(e, hc=hc):
            e.dma_start(out=y_hbm.rearrange("(c p) m -> p c m", p=128
                                            )[:, hc, :],
                        in_=hqrow[hc % 2][:]).then_inc(dYw.h, 16)
        dYw.bump()

    if stage == 6:
        dD6 = P.dsem("dDbg6")
        P.wait(Y_, dYw, 16 * dYw.n)
        @P.on(Y_)
        def _(e, w=sV.n):
            e.wait_ge(sV.h, w)
            e.dma_start(out=dbg["h3"].rearrange("p (c m) -> p c m", c=NHIDC),
                        in_=h3[:]).then_inc(dD6.h, 16)
            e.dma_start(out=dbg["row"][:, 32:48], in_=dq2_all[:]
                        ).then_inc(dD6.h, 16)
        dD6.bump(2)

    P.barrier()
    phD3c.close()
    phD3a.close()
    phD.close()
    if stage <= 6:
        _dev_out(nc, P, out_hbm)
        return

    phE = contextlib.ExitStack()
    psE = phE.enter_context(nc.psum_tensor("e_ps", [128, 4, 512], F32))
    w2qsb = phE.enter_context(nc.sbuf_tensor("e_w2q", [128, NHIDC, D], BF16))
    hqt = [phE.enter_context(nc.sbuf_tensor(f"e_hq{i}", [128, NHIDC, 128],
                                            BF16)) for i in range(2)]
    xmt = [phE.enter_context(nc.sbuf_tensor(f"e_xm{i}", [128, D], F32))
           for i in range(2)]
    outb = [phE.enter_context(nc.sbuf_tensor(f"e_ob{i}", [128, D], F32))
            for i in range(2)]
    dq2w = phE.enter_context(nc.sbuf_tensor("e_dq2w", [128, NT], F32))
    dW2q = P.dsem("dW2q")
    dHq = P.dsem("dHq")
    dXm = P.dsem("dXm")
    dOut = P.dsem("dOut")

    @P.on(G_)
    def _(e):
        e.dma_start(out=w2qsb[:], in_=w2q_hbm.rearrange("(c p) m -> p c m",
                                                        p=128)
                    ).then_inc(dW2q.h, 16)
    w2ld = 16 * dW2q.bump()
    @P.on(V_)
    def _(e):
        e.drain()
        e.tensor_scalar(out=dq2w[:], in0=dq2_all[:], scalar1=wcols[:, 3:4],
                        scalar2=None, op0=ALU.mult).then_inc(sV.h, 1)
    v_dq2w = sV.bump()
    P.wait(T_, dW2q, w2ld)

    y_src = y_hbm.rearrange("(c p) (n m) -> p c n m", p=128, m=128)
    v_out = []
    for t in range(NT):
        if t >= 2:
            P.wait(Y_, sT, t_mm2[t - 2])
            P.wait(Y_, sV, v_out[t - 2])
        P.wait(Y_, dHq, 16 * dHq.n)
        P.wait(Y_, dXm, 16 * dXm.n)
        @P.on(Y_)
        def _(e, t=t):
            e.dma_start(out=hqt[t % 2][:], in_=y_src[:, :, t, :]
                        ).then_inc(dHq.h, 16)
            e.dma_start(out=xmt[t % 2][:],
                        in_=xnew_hbm[128 * t:128 * (t + 1), :]
                        ).then_inc(dXm.h, 16)
        ld_hq = 16 * dHq.bump()
        ld_xm = 16 * dXm.bump()
        P.wait(T_, dHq, ld_hq)
        if t >= 2:
            P.wait(T_, sV, v_out[t - 2])
        @P.on(T_)
        def _(e, t=t):
            for half in range(2):
                for hc in range(NHIDC):
                    mm = e.matmul(psE[:, 2 * (t % 2) + half, :],
                                  hqt[t % 2][:, hc, :],
                                  w2qsb[:, hc, 512 * half:512 * (half + 1)],
                                  start=(hc == 0), stop=(hc == NHIDC - 1))
            mm.then_inc(sT.h, 1)
        if t == 0:
            t_mm2 = []
        t_mm2.append(sT.bump())
        P.wait(V_, sT, t_mm2[t])
        P.wait(V_, dXm, ld_xm)
        if t >= 2:
            P.wait(V_, dOut, 16 * (t - 1))
        @P.on(V_)
        def _(e, t=t):
            e.drain()
            e.scalar_tensor_tensor(
                out=outb[t % 2][:].rearrange("p (a m) -> p a m", a=2),
                in0=psE[:, 2 * (t % 2):2 * (t % 2) + 2, :],
                scalar=dq2w[:, t:t + 1],
                in1=xmt[t % 2][:].rearrange("p (a m) -> p a m", a=2),
                op0=ALU.mult, op1=ALU.add).then_inc(sV.h, 1)
        v_out.append(sV.bump())
        P.wait(G_, sV, v_out[t])
        P.wait(G_, dOut, 16 * dOut.n)
        @P.on(G_)
        def _(e, t=t):
            e.dma_start(out=out_hbm[128 * t:128 * (t + 1), :],
                        in_=outb[t % 2][:]).then_inc(dOut.h, 16)
        dOut.bump()
    P.wait(G_, dOut, 16 * NT)
    P.wait(Y_, dOut, 16 * NT)
    phE.close()


def _attn_half(c, H):
    P = c["P"]
    sT, sV, sS, sG = (P.sem[k] for k in ("T", "V", "S", "G"))
    T_, V_, S_, G_, Y_ = "T", "V", "S", "G", "Y"
    lnxT = c["lnxT"]; k1f = c["k1f"]; k2f = c["k2f"]; vtk = c["vtk"]
    Wk_sb = c["Wk_sb"]; Wq_sb = c["Wq_sb"]; qf = c["qf"]; e_sb = c["e_sb"]
    aoT = c["aoT"]; tbuf = c["tbuf"]; cbuf = c["cbuf"]; rcbuf = c["rcbuf"]
    Gb = c["Gb"]; Db = c["Db"]; srow = c["srow"]; grow = c["grow"]
    dinvrow = c["dinvrow"]; lamsc = c["lamsc"]
    ps_sc = c["ps_sc"]; ps_ev = c["ps_ev"]
    ps_r = c["ps_r"]; W_hbm = c["W_hbm"]; dWh = c["dWh"]
    maskEO = c["maskEO"]; onehot3 = c["onehot3"]; ones_c = c["ones_c"]
    dbg = c["dbg"]

    cols = slice(512 * H, 512 * H + 512)
    dWq = P.dsem(f"dWq{H}")
    entry_mm = P.sem["T"].n

    def warm(n, ev=False):
        @P.on(T_)
        def _(e, n=n, ev=ev):
            for _ in range(n):
                if ev:
                    e.matmul(ps_ev[0:64, 1, 0:448], ones_c[0:64, 0:64],
                             maskEO[0:64, 0:448], start=True, stop=True,
                             tile_position=(0, 0))
                else:
                    e.matmul(ps_sc[0:64, 0, 0:448], ones_c[0:64, 0:64],
                             maskEO[0:64, 0:448], start=True, stop=True,
                             tile_position=(0, 0))

    P.wait(G_, sT, entry_mm)
    @P.on(G_)
    def _(e):
        e.dma_start(out=Wq_sb[:],
                    in_=W_hbm["Wq"].rearrange("(c p) m -> p c m", p=128
                                              )[:, :, cols]).then_inc(dWq.h, 16)
    wq_ld = 16 * dWq.bump()

    def proj(name, which, prev_mm):
        if prev_mm:
            P.wait(G_, sT, prev_mm)
        @P.on(G_)
        def _(e, name=name):
            e.dma_start(out=Wk_sb[:],
                        in_=W_hbm[name].rearrange("(c p) m -> p c m", p=128
                                                  )[:, :, cols]
                        ).then_inc(dWh.h, 16)
        ld = 16 * dWh.bump()
        P.wait(T_, dWh, ld)
        evs = []
        if which in ("k1", "k2"):
            dst = k1f if which == "k1" else k2f
            for tg in range(4):
                for dc in range(4):
                    if len(evs) >= 4:
                        P.wait(T_, sV, evs[-4])
                    @P.on(T_)
                    def _(e, dc=dc, tg=tg):
                        for dinc in range(NDC):
                            mm = e.matmul(
                                ps_sc[:, dc, 0:512],
                                Wk_sb[:, dinc, 128 * dc:128 * (dc + 1)],
                                lnxT[:, dinc, 512 * tg:512 * (tg + 1)],
                                start=(dinc == 0), stop=(dinc == NDC - 1))
                        mm.then_inc(sT.h, 1)
                    mm = sT.bump()
                    @P.on(V_)
                    def _(e, dc=dc, tg=tg, mm=mm, dst=dst):
                        e.wait_ge(sT.h, mm)
                        e.drain()
                        e.tensor_copy(out=dst[:, dc, 512 * tg:512 * (tg + 1)],
                                      in_=ps_sc[:, dc, 0:512]).then_inc(sV.h, 1)
                    evs.append(sV.bump())
            P.wait(T_, sV, evs[-1])
        else:
            for tokt in range(NT):
                if len(evs) >= 4:
                    P.wait(T_, sV, evs[-4])
                @P.on(T_)
                def _(e, tokt=tokt):
                    for dinc in range(NDC):
                        mm = e.matmul(ps_sc[:, tokt % 4, 0:512],
                                      lnxT[:, dinc, 128 * tokt:128 * (tokt + 1)],
                                      Wk_sb[:, dinc, 0:512],
                                      start=(dinc == 0), stop=(dinc == NDC - 1))
                    mm.then_inc(sT.h, 1)
                mm = sT.bump()
                @P.on(V_)
                def _(e, tokt=tokt, mm=mm):
                    e.wait_ge(sT.h, mm)
                    e.drain()
                    e.tensor_copy(out=vtk[:, tokt, :],
                                  in_=ps_sc[:, tokt % 4, 0:512]).then_inc(sV.h, 1)
                evs.append(sV.bump())
            P.wait(T_, sV, evs[-1])
        return evs[-1], sT.n

    pk1, m1 = proj("Wk1", "k1", entry_mm)
    pk2, m2 = proj("Wk2", "k2", m1)
    pv, m3 = proj("Wv", "v", m2)
    P.wait(T_, dWq, wq_ld)

    if c["stage"] == 2 and H == 0 and "k1f" in dbg:
        dD = P.dsem(f"dDbgB{H}")
        @P.on(Y_)
        def _(e, w=pv):
            e.wait_ge(sV.h, w)
            e.dma_start(out=dbg["k1f"].rearrange("p (c m) -> p c m", c=4),
                        in_=k1f[:]).then_inc(dD.h, 16)
            e.dma_start(out=dbg["k2f"].rearrange("p (c m) -> p c m", c=4),
                        in_=k2f[:]).then_inc(dD.h, 16)
            e.dma_start(out=dbg["vtk"].rearrange("p (c m) -> p c m", c=NT),
                        in_=vtk[:]).then_inc(dD.h, 16)
        dD.bump(3)

    def qproj_T(qg2):
        qs2 = slice(QG * qg2, QG * (qg2 + 1))
        @P.on(T_)
        def _(e, qs2=qs2):
            for dc in range(4):
                for dinc in range(NDC):
                    mm = e.matmul(ps_sc[:, dc, 0:256],
                                  Wq_sb[:, dinc, 128 * dc:128 * (dc + 1)],
                                  lnxT[:, dinc, qs2],
                                  start=(dinc == 0), stop=(dinc == NDC - 1))
            mm.then_inc(sT.h, 1)
        return sT.bump()

    def qproj_S(qmm2, qg2):
        P.wait(S_, sT, qmm2)
        @P.on(S_)
        def _(e, qbuf=qf[qg2 % 2]):
            e.drain()
            for dc in range(4):
                act = e.activation(out=qbuf[:, dc, :], in_=ps_sc[:, dc, 0:256],
                                   func=AF.Identity, scale=SCALE)
            act.then_inc(sS.h, 1)
        return sS.bump()

    qev_pending = qproj_S(qproj_T(0), 0)
    for qg in range(_NQG):
        qs = slice(QG * qg, QG * (qg + 1))
        qbuf = qf[qg % 2]
        qev = qev_pending
        P.wait(T_, sS, qev)

        nkc = 2 * qg + 2
        for p in range(_NPAIR):
            if c.get("wstep") is not None:
                c["wstep"]()
            par = p % 2
            ebuf = e_sb[p % 2]
            gbuf = Gb[par]
            dbuf = Db[par]
            exp_s = []
            for kc2 in range(0, nkc, 2):
                for kc in (kc2, kc2 + 1):
                    b = 2 * (kc % 2)
                    ks = slice(128 * kc, 128 * (kc + 1))
                    @P.on(T_)
                    def _(e, b=b, ks=ks, p=p, qs=qs, qbuf=qbuf):
                        e.matmul(ps_sc[:, b, 0:256], k1f[0:64, p, ks],
                                 qbuf[0:64, p, :], start=True, stop=True,
                                 tile_position=(0, 0))
                        e.matmul(ps_sc[:, b + 1, 0:256], k1f[64:128, p, ks],
                                 qbuf[64:128, p, :], start=True, stop=True,
                                 tile_position=(64, 0))
                        e.matmul(ps_sc[:, b, 256:512], k2f[0:64, p, ks],
                                 qbuf[0:64, p, :], start=True, stop=True,
                                 tile_position=(0, 0))
                        e.matmul(ps_sc[:, b + 1, 256:512], k2f[64:128, p, ks],
                                 qbuf[64:128, p, :], start=True, stop=True,
                                 tile_position=(64, 0)).then_inc(sT.h, 1)
                    sT.bump()
                P.wait(S_, sT, sT.n)
                @P.on(S_)
                def _(e, kc2=kc2, ebuf=ebuf):
                    e.drain()
                    e.activation(out=ebuf[:, kc2:kc2 + 2, :, :],
                                 in_=ps_sc[:], func=AF.Exp).then_inc(sS.h, 1)
                ex = sS.bump()
                exp_s.append(ex)
                if kc2 + 2 >= nkc:
                    P.wait(V_, sS, ex)
                    @P.on(V_)
                    def _(e, kc2=kc2, ebuf=ebuf):
                        e.drain()
                        e.tensor_tensor(
                            out=ebuf[:, kc2:kc2 + 2, :, :].rearrange(
                                "p a b m -> p (a b m)"),
                            in0=ebuf[:, kc2:kc2 + 2, :, :].rearrange(
                                "p a b m -> p (a b m)"),
                            in1=maskEO[:], op=ALU.mult).then_inc(sV.h, 1)
                    sV.bump()
                    P.wait(T_, sV, sV.n)
                else:
                    P.wait(T_, sS, ex)
                @P.on(T_)
                def _(e, kc2=kc2, p=p, ebuf=ebuf, first=(kc2 == 0),
                      last=(kc2 + 2 >= nkc)):
                    for i, kc in enumerate((kc2, kc2 + 1)):
                        st = first and i == 0
                        sp = last and i == 1
                        e.matmul(ps_r[0:16, 0, 0:512],
                                 onehot3[:, 0:16],
                                 ebuf[:, kc, 0:2, :], start=st, stop=sp,
                                 tile_position=(0, 0))
                        mm = e.matmul(ps_r[0:16, 1, 0:512],
                                      onehot3[:, 0:16],
                                      ebuf[:, kc, 2:4, :], start=st, stop=sp,
                                      tile_position=(0, 0))
                    mm.then_inc(sT.h, 1)
                sT.bump()
            r_done = sT.n

            P.wait(S_, sT, r_done)
            @P.on(S_)
            def _(e, p=p):
                e.drain()
                e.activation(out=srow[0:1, 0:512], in_=ps_r[0:1, 0, 0:512],
                             func=AF.Ln)
                e.activation(out=srow[0:1, 512:1024], in_=ps_r[0:1, 1, 0:512],
                             func=AF.Ln).then_inc(sS.h, 1)
            ln_r = sS.bump()
            P.wait(V_, sS, ln_r)
            @P.on(V_)
            def _(e, p=p, H=H):
                e.drain()
                e.tensor_tensor(out=srow[0:1, 1024:1280], in0=srow[0:1, 0:256],
                                in1=srow[0:1, 256:512], op=ALU.subtract)
                e.tensor_tensor(out=srow[0:1, 1280:1536], in0=srow[0:1, 512:768],
                                in1=srow[0:1, 768:1024], op=ALU.subtract)
                hA = 8 * H + 2 * p
                e.drain()
                e.tensor_scalar(out=srow[0:1, 0:256],
                                in0=ps_r[0:1, 0, 0:256],
                                scalar1=lamsc[0:1, 2 * hA + 1:2 * hA + 2],
                                scalar2=None, op0=ALU.mult)
                e.tensor_scalar(out=srow[0:1, 256:512],
                                in0=ps_r[0:1, 1, 0:256],
                                scalar1=lamsc[0:1, 2 * hA + 3:2 * hA + 4],
                                scalar2=None, op0=ALU.mult)
                e.tensor_scalar(
                    out=srow[0:1, 512:1024].rearrange("p (a m) -> p a m", a=2),
                    in0=ps_r[0:1, 0:2, 0:256],
                    scalar1=1e-6, scalar2=None, op0=ALU.mult
                    ).then_inc(sV.h, 1)
            v_t2 = sV.bump()
            P.wait(S_, sV, v_t2)
            @P.on(S_)
            def _(e, p=p, H=H):
                hA = 8 * H + 2 * p
                e.drain()
                e.activation(out=grow[0:1, 0:256], in_=srow[0:1, 1024:1280],
                             func=AF.Exp, bias=lamsc[0:1, 2 * hA:2 * hA + 1])
                e.activation(out=grow[0:1, 256:512], in_=srow[0:1, 1280:1536],
                             func=AF.Exp,
                             bias=lamsc[0:1, 2 * hA + 2:2 * hA + 3]
                             ).then_inc(sS.h, 1)
            s_g = sS.bump()
            P.wait(T_, sS, s_g)
            @P.on(T_)
            def _(e):
                e.matmul(ps_r[:, 1, :], ones_c[0:1, :], grow[0:1, :],
                         start=True, stop=True).then_inc(sT.h, 1)
            t_gb = sT.bump()
            P.wait(V_, sT, t_gb)
            @P.on(V_)
            def _(e, gbuf=gbuf):
                e.drain()
                e.tensor_copy(out=gbuf[:, 0:512], in_=ps_r[:, 1, :])
                e.tensor_copy(out=gbuf[:, 512:1024], in_=ps_r[:, 1, :]
                              ).then_inc(sV.h, 1)
            v_gb = sV.bump()

            P.wait(V_, sS, exp_s[-1])
            ev_mms = []
            for g in range(nkc // 2):
                par2 = g % 2
                if g >= 2:
                    P.wait(V_, sT, ev_mms[g - 2])
                @P.on(V_)
                def _(e, g=g, par2=par2, ebuf=ebuf, gbuf=gbuf, w=v_gb):
                    e.wait_ge(sV.h, w)
                    e.drain()
                    e.tensor_tensor(
                        out=tbuf[:].rearrange("p (a b m) -> p a b m",
                                              a=2, b=2),
                        in0=ebuf[:, 2 * g:2 * g + 2, 1:4:2, :],
                        in1=gbuf[:].rearrange("p (a b m) -> p a b m",
                                              a=2, b=2),
                        op=ALU.mult)
                    e.drain()
                    e.tensor_tensor(
                        out=cbuf[:, par2, :].rearrange(
                            "p (a b m) -> p a b m", a=2, b=2),
                        in0=ebuf[:, 2 * g:2 * g + 2, 0:4:2, :],
                        in1=tbuf[:].rearrange("p (a b m) -> p a b m",
                                              a=2, b=2),
                        op=ALU.subtract)
                    e.drain()
                    e.tensor_scalar(out=rcbuf[:, par2, :],
                                    in0=cbuf[:, par2, :], scalar1=0.0,
                                    scalar2=None, op0=ALU.max).then_inc(sV.h, 1)
                v_c = sV.bump()
                P.wait(T_, sV, v_c)
                @P.on(T_)
                def _(e, g=g, par2=par2, p=p, w=v_t2, stg=(g == 0),
                      spg=(g == nkc // 2 - 1)):
                    e.wait_ge(sV.h, w)
                    for j in range(2):
                        kc = 2 * g + j
                        st = stg and j == 0
                        sp = spg and j == 1
                        e.matmul(ps_ev[0:64, 0, 0:256],
                                 vtk[:, kc, 128 * p:128 * p + 64],
                                 cbuf[:, par2, 512 * j:512 * j + 256],
                                 start=st, stop=sp, tile_position=(0, 0))
                        e.matmul(ps_ev[64:128, 1, 0:256],
                                 vtk[:, kc, 128 * p + 64:128 * (p + 1)],
                                 cbuf[:, par2, 512 * j + 256:512 * (j + 1)],
                                 start=st, stop=sp, tile_position=(0, 64))
                        mm = e.matmul(ps_r[0:16, 0, 0:512],
                                      onehot3[:, 0:16],
                                      rcbuf[:, par2, 512 * j:512 * (j + 1)],
                                      start=st, stop=sp, tile_position=(0, 0))
                    mm.then_inc(sT.h, 1)
                ev_mms.append(sT.bump())
            ev_done = sT.n
            if p == _NPAIR - 1 and qg + 1 < _NQG:
                P.wait(T_, sS, exp_s[-1])
                qmm_next = qproj_T(qg + 1)
            P.wait(V_, sT, ev_done)

            @P.on(V_)
            def _(e, p=p):
                e.drain()
                e.scalar_tensor_tensor(out=srow[0:1, 1024:1536],
                                       in0=ps_r[0:1, 0, :],
                                       scalar=2.0, in1=srow[0:1, 0:512],
                                       op0=ALU.mult, op1=ALU.subtract)
                e.drain()
                e.tensor_tensor(out=srow[0:1, 1536:2048], in0=srow[0:1, 512:1024],
                                in1=srow[0:1, 1024:1536], op=ALU.max
                                ).then_inc(sV.h, 1)
            v_d = sV.bump()
            P.wait(S_, sV, v_d)
            @P.on(S_)
            def _(e):
                e.drain()
                e.activation(out=srow[0:1, 1024:1536], in_=srow[0:1, 1536:2048], func=AF.Ln)
                e.drain()
                e.activation(out=dinvrow[0:1, :], in_=srow[0:1, 1024:1536],
                             func=AF.Exp, scale=-1.0).then_inc(sS.h, 1)
            s_d = sS.bump()
            P.wait(T_, sS, s_d)
            @P.on(T_)
            def _(e):
                e.matmul(ps_r[:, 1, 0:256], ones_c[0:1, :], dinvrow[0:1, 0:256],
                         start=True, stop=True)
                e.matmul(ps_r[:, 1, 256:512], ones_c[0:1, :],
                         dinvrow[0:1, 256:512],
                         start=True, stop=True).then_inc(sT.h, 1)
            t_db = sT.bump()
            P.wait(V_, sT, t_db)
            @P.on(V_)
            def _(e, dbuf=dbuf):
                e.drain()
                e.tensor_copy(out=dbuf[:], in_=ps_r[:, 1, :]
                              ).then_inc(sV.h, 1)
            v_db = sV.bump()
            @P.on(V_)
            def _(e, p=p, H=H, qs=qs, dbuf=dbuf):
                e.drain()
                e.tensor_tensor(out=aoT[0:64, 4 * H + p, qs],
                                in0=ps_ev[0:64, 0, 0:256],
                                in1=dbuf[0:64, 0:256], op=ALU.mult)
                e.tensor_tensor(out=aoT[64:128, 4 * H + p, qs],
                                in0=ps_ev[64:128, 1, 0:256],
                                in1=dbuf[64:128, 256:512],
                                op=ALU.mult).then_inc(sV.h, 1)
            v_ao = sV.bump()
            if p == _NPAIR - 1 and qg + 1 < _NQG:
                qev_pending = qproj_S(qmm_next, qg + 1)
            P.wait(T_, sV, v_ao)
            P.wait(S_, sV, v_ao)

    if c["stage"] == 2 and "ao" in dbg:
        dD2 = P.dsem(f"dDbgB2{H}")
        @P.on(Y_)
        def _(e, w=sV.n):
            e.wait_ge(sV.h, w)
            e.dma_start(out=dbg["ao"].rearrange("p (c m) -> p c m", c=8
                                                )[:, 0:4, :],
                        in_=aoT[:, 0:4, :]).then_inc(dD2.h, 16)
            e.dma_start(out=dbg["e"].rearrange("p (a b) -> p a b", a=16),
                        in_=e_sb[0][:].rearrange("p a b m -> p a (b m)")
                        ).then_inc(dD2.h, 16)
            e.dma_start(out=dbg["qf"].rearrange("p (c m) -> p c m", c=4),
                        in_=qf[1][:]).then_inc(dD2.h, 16)
        dD2.bump(3)


def _dev_out(nc, P, out_hbm):
    dOut = P.dsem("dOut")
    zb = P.st.enter_context(nc.sbuf_tensor("z_out", [128, D], F32))
    @P.on("V")
    def _(e):
        e.memset(zb[:], 0.0)
        e.drain()
        e.nop().then_inc(P.sem["V"].h, 1)
    w = P.sem["V"].bump()
    P.wait("Y", P.sem["V"], w)
    @P.on("Y")
    def _(e):
        for t in range(NT):
            e.dma_start(out=out_hbm[128 * t:128 * (t + 1), :],
                        in_=zb[:]).then_inc(dOut.h, 16)
        e.wait_ge(dOut.h, 16 * NT)
    dOut.bump(NT)


def kernel(L0_attn_x, L0_attn_Wq, L0_attn_Wk1, L0_attn_Wk2, L0_attn_Wv,
           L0_attn_Wo, L0_attn_lambda_logit, L0_ffn_w1, L0_ffn_conv_w,
           L0_ffn_conv_b, L0_ffn_w2, _stage=99, _res=None):
    consts = _consts()
    nc = build(_stage)
    cw = np.asarray(L0_ffn_conv_w, np.float32).reshape(DH, 3)
    base = {
        "WqT": np.ascontiguousarray(np.asarray(L0_attn_Wq, np.float32).T),
        "Wk1T": np.ascontiguousarray(np.asarray(L0_attn_Wk1, np.float32).T),
        "Wk2T": np.ascontiguousarray(np.asarray(L0_attn_Wk2, np.float32).T),
        "WvT": np.ascontiguousarray(np.asarray(L0_attn_Wv, np.float32).T),
        "WoT": np.ascontiguousarray(np.asarray(L0_attn_Wo, np.float32).T),
        "lam": np.ascontiguousarray(L0_attn_lambda_logit, np.float32
                                    ).reshape(NH, 1),
        "w1T": np.ascontiguousarray(np.asarray(L0_ffn_w1, np.float32).T),
        "convw": np.ascontiguousarray(cw.reshape(32, 128, 3
                                                 ).transpose(1, 2, 0)),
        "convb": np.ascontiguousarray(
            np.asarray(L0_ffn_conv_b, np.float32).reshape(32, 128).T),
        "w2T": np.ascontiguousarray(np.asarray(L0_ffn_w2, np.float32).T),
    }
    base.update({"h_" + k: v for k, v in consts.items()})
    in_maps = []
    xs = np.asarray(L0_attn_x)
    for c in range(B):
        m = dict(base)
        m["x"] = np.ascontiguousarray(xs[c], np.float32)
        in_maps.append(m)
    res = run_bass_kernel_spmd(nc, in_maps, core_ids=list(range(B)))
    if _res is not None:
        _res.append(res)
    return np.stack([res.results[c]["out"] for c in range(B)], axis=0)
